# version 07
# baseline (speedup 1.0000x reference)
import numpy as np
import ml_dtypes
from contextlib import ExitStack

import concourse.bass as bass
import concourse.mybir as mybir
from concourse.bass_utils import run_bass_kernel_spmd
from concourse.aot_env import allow_ici_all_to_all

F32 = mybir.dt.float32
BF16 = mybir.dt.bfloat16
I8 = mybir.dt.int8
AF = mybir.ActivationFunctionType
OP = mybir.AluOpType
AX = mybir.AxisListType.X

B = 8
S = 4096
SP = 4112
D = 1024
NCORES = 8
QW = S // NCORES
LN_EPS = 1e-5


class Ctr:
    def __init__(self, nc, name):
        self.h = nc.alloc_semaphore(name=name)
        self.n = 0

    def inc(self, inst, k=1):
        inst.then_inc(self.h, k)
        self.n += k
        return self.n


def _mk_identity(g, ap, ctr):
    ctr.inc(g.memset(ap, 0.0))
    g.wait_ge(ctr.h, ctr.n)
    return g.affine_select(out=ap, in_=ap, compare_op=OP.not_equal, fill=1.0,
                           base=0, pattern=[[-1, ap.shape[1]]], channel_multiplier=1)


def build_ln0():
    nc = bass.Bass("TRN2", target_bir_lowering=False)
    x_ap = nc.dram_tensor("x", [S, D], F32, kind="ExternalInput").ap()
    xnf_ap = nc.dram_tensor("xnf", [D, SP], BF16, kind="ExternalOutput").ap()
    emit_ln(nc, x_ap, xnf_ap, "ln0", SP)
    return nc


def emit_ln(nc, x_ap, xnf_ap, tag, tcols):
    NT = S // 128
    with ExitStack() as es:
        sb = lambda n, sh, dt: es.enter_context(nc.sbuf_tensor(tag + n, sh, dt)).ap()
        ps = lambda n, sh, dt: es.enter_context(nc.psum_tensor(tag + n, sh, dt)).ap()
        xt = [sb(f"xt{i}", [128, D], F32) for i in range(3)]
        sq = sb("sq", [128, D], F32)
        stt = [sb(f"stt{i}", [128, 8], F32) for i in range(3)]
        epst = sb("epst", [128, 1], F32)
        xn = [sb(f"xn{i}", [128, D], BF16) for i in range(2)]
        ptr = [ps(f"ptr{i}", [128, D], BF16) for i in range(2)]
        stage = [sb(f"stage{i}", [128, D], BF16) for i in range(2)]
        zpad = sb("zpad", [128, 16], BF16)
        ident = sb("id", [128, 128], BF16)

        lds = [Ctr(nc, tag + f"ld{i}") for i in range(3)]
        ac = Ctr(nc, tag + "ac")
        dv = Ctr(nc, tag + "dv")
        pe = Ctr(nc, tag + "pe")
        ev = Ctr(nc, tag + "ev")
        sos = [Ctr(nc, tag + f"so{i}") for i in range(2)]
        sop = Ctr(nc, tag + "sop")
        gi_ = Ctr(nc, tag + "gi")

        xn_done = {}
        stage_done = {}
        out_done = {}

        with nc.Block(tag) as blk:
            @blk.gpsimd
            def _(g):
                g.memset(epst, LN_EPS)
                g.memset(zpad, 0.0)
                _mk_identity(g, ident, gi_).then_inc(gi_.h, 1)
            gi_.n += 1
            gi_all = gi_.n

            if tcols > S:
                @blk.sync
                def _(sy):
                    sy.wait_ge(gi_.h, gi_all)
                    for j in range(8):
                        sy.dma_start(out=xnf_ap[j * 128:(j + 1) * 128, S:tcols],
                                     in_=zpad).then_inc(sop.h, 16)
                sop.n += 16 * 8

            for i in range(NT):
                ldc = lds[i % 3]

                @blk.sync
                def _(sy, i=i, ldc=ldc):
                    if i >= 3:
                        sy.wait_ge(dv.h, xn_done[i - 3])
                    sy.dma_start(out=xt[i % 3],
                                 in_=x_ap[i * 128:(i + 1) * 128, :]).then_inc(ldc.h, 16)
                ldc.n += 16
                ld_i = ldc.n
                b = stt[i % 3]

                @blk.scalar
                def _(sc, i=i, b=b, ld_i=ld_i, ldc=ldc):
                    sc.wait_ge(ldc.h, ld_i)
                    if i >= 3:
                        sc.wait_ge(dv.h, xn_done[i - 3])
                    ac.inc(sc.activation(out=sq, in_=xt[i % 3], func=AF.Square,
                                         accum_out=b[:, 1:2]))
                s2_ready = ac.n

                @blk.vector
                def _(v, i=i, b=b, ld_i=ld_i, s2=s2_ready, ldc=ldc):
                    v.wait_ge(ldc.h, ld_i)
                    if i >= 3:
                        v.wait_ge(dv.h, xn_done[i - 3])
                    dv.inc(v.reduce_sum(b[:, 0:1], xt[i % 3], axis=AX))
                    v.wait_ge(dv.h, dv.n)
                    dv.inc(v.tensor_scalar(out=b[:, 2:3], in0=b[:, 0:1],
                                           scalar1=1.0 / D, scalar2=None,
                                           op0=OP.mult))
                    v.wait_ge(dv.h, dv.n)
                    dv.inc(v.scalar_tensor_tensor(out=b[:, 5:6], in0=b[:, 2:3],
                                                  scalar=1.0, in1=b[:, 2:3],
                                                  op0=OP.mult, op1=OP.mult))
                    v.wait_ge(ac.h, s2)
                    v.wait_ge(dv.h, dv.n)
                    dv.inc(v.scalar_tensor_tensor(out=b[:, 3:4], in0=b[:, 1:2],
                                                  scalar=1.0 / D, in1=b[:, 5:6],
                                                  op0=OP.mult, op1=OP.subtract))
                var_ready = dv.n

                @blk.scalar
                def _(sc, b=b, var_ready=var_ready):
                    sc.wait_ge(dv.h, var_ready)
                    ac.inc(sc.activation(out=b[:, 4:5], in_=b[:, 3:4], func=AF.Sqrt,
                                         bias=epst[:, 0:1]))
                std_ready = ac.n

                @blk.vector
                def _(v, i=i, b=b, std_ready=std_ready):
                    v.wait_ge(ac.h, std_ready)
                    dv.inc(v.reciprocal(b[:, 5:6], b[:, 4:5]))
                    v.wait_ge(dv.h, dv.n)
                    if i >= 2:
                        v.wait_ge(sos[i % 2].h, out_done[i - 2])
                    dv.inc(v.tensor_scalar(out=xn[i % 2], in0=xt[i % 3],
                                           scalar1=b[:, 2:3], scalar2=b[:, 5:6],
                                           op0=OP.subtract, op1=OP.mult))
                xn_done[i] = dv.n
                xnr = dv.n

                @blk.tensor
                def _(t, i=i, xnr=xnr):
                    t.wait_ge(gi_.h, gi_all)
                    t.wait_ge(dv.h, xnr)
                    if i >= 2:
                        t.wait_ge(ev.h, stage_done[i - 2])
                    for j in range(8):
                        ins = t.transpose(ptr[i % 2][:, j * 128:(j + 1) * 128],
                                          xn[i % 2][:, j * 128:(j + 1) * 128], ident)
                    pe.inc(ins)
                tr_done = pe.n

                @blk.vector
                def _(v, i=i, tr=tr_done):
                    v.wait_ge(pe.h, tr)
                    if i >= 2:
                        v.wait_ge(sos[i % 2].h, out_done[i - 2])
                    ev.inc(v.tensor_copy(stage[i % 2], ptr[i % 2]))
                stage_done[i] = ev.n
                stg = ev.n

                @blk.sync
                def _(sy, i=i, stg=stg):
                    sy.wait_ge(ev.h, stg)
                    for j in range(8):
                        sy.dma_start(
                            out=xnf_ap[j * 128:(j + 1) * 128,
                                       i * 128:(i + 1) * 128],
                            in_=stage[i % 2][:, j * 128:(j + 1) * 128])\
                          .then_inc(sos[i % 2].h, 16)
                sos[i % 2].n += 16 * 8
                out_done[i] = sos[i % 2].n

            @blk.sync
            def _(sy):
                sy.wait_ge(sos[0].h, sos[0].n)
                sy.wait_ge(sos[1].h, sos[1].n)
                if tcols > S:
                    sy.wait_ge(sop.h, sop.n)
        fences = [(sos[0].h, sos[0].n), (sos[1].h, sos[1].n)]
        if tcols > S:
            fences.append((sop.h, sop.n))
        return fences


def build_l0attn(debug_qkv=False):
    nc = bass.Bass("TRN2", target_bir_lowering=False)
    xnf = nc.dram_tensor("xnf", [D, SP], BF16, kind="ExternalInput").ap()
    wq = nc.dram_tensor("wq", [D, 256], F32, kind="ExternalInput").ap()
    wqc = nc.dram_tensor("wqc", [D, 256], F32, kind="ExternalInput").ap()
    wk = nc.dram_tensor("wk", [D, 256], F32, kind="ExternalInput").ap()
    wkc = nc.dram_tensor("wkc", [D, 256], F32, kind="ExternalInput").ap()
    wv = nc.dram_tensor("wv", [D, D], F32, kind="ExternalInput").ap()
    wvc = nc.dram_tensor("wvc", [D, D], F32, kind="ExternalInput").ap()
    bq = nc.dram_tensor("bq", [256, 1], F32, kind="ExternalInput").ap()
    bqc = nc.dram_tensor("bqc", [256, 1], F32, kind="ExternalInput").ap()
    bk = nc.dram_tensor("bk", [256, 1], F32, kind="ExternalInput").ap()
    bkc = nc.dram_tensor("bkc", [256, 1], F32, kind="ExternalInput").ap()
    bv = nc.dram_tensor("bv", [1, D], F32, kind="ExternalInput").ap()
    bvc = nc.dram_tensor("bvc", [1, D], F32, kind="ExternalInput").ap()
    s_f = nc.dram_tensor("s_f", [D, S], F32, kind="ExternalOutput").ap()
    dbg = None
    if debug_qkv:
        dbg = (nc.dram_tensor("dbg_q", [128, 2 * S], BF16, kind="ExternalOutput").ap(),
               nc.dram_tensor("dbg_k", [128, 2 * SP], BF16, kind="ExternalOutput").ap(),
               nc.dram_tensor("dbg_v", [128, 33 * 1040], BF16, kind="ExternalOutput").ap())
    emit_l0attn(nc, xnf, (wq, wqc, wk, wkc, wv, wvc),
                (bq, bqc, bk, bkc, bv, bvc), s_f, "a0", dbg=dbg)
    return nc


def emit_l0attn(nc, xnf, ws, bs, s_f, tag, dbg=None):
    wq, wqc, wk, wkc, wv, wvc = ws
    bq, bqc, bk, bkc, bv, bvc = bs
    NH = 16
    QL = 16
    VL = 64
    NKC = SP // 128
    KCW = [128] * 32 + [16]
    GRP3 = [(3 * g, min(3, 33 - 3 * g)) for g in range((33 + 2) // 3)]

    with ExitStack() as es:
        sb = lambda n, sh, dt: es.enter_context(nc.sbuf_tensor(tag + n, sh, dt)).ap()
        ps = lambda n, sh, dt: es.enter_context(nc.psum_tensor(tag + n, sh, dt)).ap()
        q_sb = sb("q", [128, 2 * S], BF16)
        k_sb = sb("k", [128, 2 * SP], BF16)
        v_sb = sb("v", [128, 33 * 1040], BF16)

        ac = Ctr(nc, tag + "ac")
        dv = Ctr(nc, tag + "dv")
        pe = Ctr(nc, tag + "pe")
        so = Ctr(nc, tag + "so")

        with ExitStack() as es2:
            sb2 = lambda n, sh, dt: es2.enter_context(nc.sbuf_tensor(tag + n, sh, dt)).ap()
            xnc = [sb2(f"xnc{i}", [128, 8 * 512], BF16) for i in range(2)]
            wq_sb = sb2("wqs", [128, 8 * 256], BF16)
            wqc_sb = sb2("wqcs", [128, 8 * 256], BF16)
            wk_sb = sb2("wks", [128, 8 * 256], BF16)
            wkc_sb = sb2("wkcs", [128, 8 * 256], BF16)
            wv_sb = sb2("wvs", [128, 8 * 1024], BF16)
            wvc_sb = sb2("wvcs", [128, 8 * 1024], BF16)
            bq_sb = sb2("bqs", [128, 2], F32)
            bqc_sb = sb2("bqcs", [128, 2], F32)
            bk_sb = sb2("bks", [128, 2], F32)
            bkc_sb = sb2("bkcs", [128, 2], F32)
            brow = sb2("brow", [1, 2 * D], F32)
            ones1 = sb2("ones1", [1, 128], F32)
            bvb = sb2("bvb", [128, D], F32)
            bvcb = sb2("bvcb", [128, D], F32)
            tmp1 = sb2("tmp1", [128, 512], F32)
            tmp2 = sb2("tmp2", [128, 512], F32)
            tmp3 = sb2("tmp3", [128, 512], F32)
            ps2 = lambda n, sh, dt: es2.enter_context(nc.psum_tensor(tag + n, sh, dt)).ap()
            pql = ps2("pql", [128, 512], F32)
            pqg = ps2("pqg", [128, 512], F32)
            pvl = ps2("pvl", [128, 512], F32)
            pvg = ps2("pvg", [128, 512], F32)
            pbc = ps2("pbc", [128, 1024], F32)

            wgld = Ctr(nc, tag + "wgld")
            gmk = Ctr(nc, tag + "gmk")

            @_one_block(nc, tag + "prep")
            def _(blk):

                @blk.gpsimd
                def _(g):
                    g.memset(ones1, 1.0).then_inc(gmk.h, 1)
                    g.memset(v_sb[:, :], 0.0).then_inc(gmk.h, 1)
                    g.wait_ge(gmk.h, 2)
                    for kc in range(33):
                        ap = v_sb[:, kc * 1040:(kc + 1) * 1040]\
                            .rearrange("p (h c) -> p h c", h=16, c=65)[:, :, 64:65]
                        ins = g.memset(ap, 1.0)
                    ins.then_inc(gmk.h, 1)
                    for hc in range(8):
                        g.dma_start(out=wq_sb[:, hc * 256:(hc + 1) * 256],
                                    in_=wq[hc * 128:(hc + 1) * 128, :]).then_inc(wgld.h, 16)
                        g.dma_start(out=wqc_sb[:, hc * 256:(hc + 1) * 256],
                                    in_=wqc[hc * 128:(hc + 1) * 128, :]).then_inc(wgld.h, 16)
                        g.dma_start(out=wk_sb[:, hc * 256:(hc + 1) * 256],
                                    in_=wk[hc * 128:(hc + 1) * 128, :]).then_inc(wgld.h, 16)
                        g.dma_start(out=wkc_sb[:, hc * 256:(hc + 1) * 256],
                                    in_=wkc[hc * 128:(hc + 1) * 128, :]).then_inc(wgld.h, 16)
                        g.dma_start(out=wv_sb[:, hc * 1024:(hc + 1) * 1024],
                                    in_=wv[hc * 128:(hc + 1) * 128, :]).then_inc(wgld.h, 16)
                        g.dma_start(out=wvc_sb[:, hc * 1024:(hc + 1) * 1024],
                                    in_=wvc[hc * 128:(hc + 1) * 128, :]).then_inc(wgld.h, 16)
                wgld.n += 16 * 48
                gmk.n += 3

                bld = Ctr(nc, tag + "bld")

                @blk.sync
                def _(sy):
                    for name, srcw, dst in [("bq", bq, bq_sb), ("bqc", bqc, bqc_sb),
                                            ("bk", bk, bk_sb), ("bkc", bkc, bkc_sb)]:
                        for oc in range(2):
                            sy.dma_start(out=dst[:, oc:oc + 1],
                                         in_=srcw[oc * 128:(oc + 1) * 128, :])\
                              .then_inc(bld.h, 16)
                    sy.dma_start(out=brow[0:1, 0:D], in_=bv).then_inc(bld.h, 16)
                    sy.dma_start(out=brow[0:1, D:2 * D], in_=bvc).then_inc(bld.h, 16)
                bld.n += 16 * 10
                wld = bld.n

                @blk.tensor
                def _(t, wld=wld):
                    t.wait_ge(bld.h, wld)
                    t.wait_ge(gmk.h, 1)
                    t.matmul(pbc[:, 0:512], ones1[0:1, :], brow[0:1, 0:512],
                             start=True, stop=True)
                    pe.inc(t.matmul(pbc[:, 512:1024], ones1[0:1, :],
                                    brow[0:1, 512:1024], start=True, stop=True))
                bv_pe = pe.n

                @blk.vector
                def _(v, bv_pe=bv_pe):
                    v.wait_ge(pe.h, bv_pe)
                    dv.inc(v.tensor_copy(bvb, pbc[:, 0:1024]))
                bvb_dv = dv.n

                @blk.tensor
                def _(t, bvb_dv=bvb_dv):
                    t.wait_ge(dv.h, bvb_dv)
                    t.matmul(pbc[:, 0:512], ones1[0:1, :], brow[0:1, D:D + 512],
                             start=True, stop=True)
                    pe.inc(t.matmul(pbc[:, 512:1024], ones1[0:1, :],
                                    brow[0:1, D + 512:2 * D], start=True, stop=True))
                bvc_pe = pe.n

                @blk.vector
                def _(v, bvc_pe=bvc_pe):
                    v.wait_ge(pe.h, bvc_pe)
                    dv.inc(v.tensor_copy(bvcb, pbc[:, 0:1024]))
                bvcb_dv = dv.n

                NTT = 9
                xlds = [Ctr(nc, tag + f"xld{i}") for i in range(2)]
                evac_done = {}
                for tt in range(NTT):
                    t0 = tt * 512
                    tw = min(512, SP - t0)
                    xldc = xlds[tt % 2]

                    @blk.sync
                    def _(sy, tt=tt, t0=t0, tw=tw, xldc=xldc):
                        if tt >= 2:
                            sy.wait_ge(dv.h, evac_done[tt - 2])
                        for hc in range(8):
                            sy.dma_start(out=xnc[tt % 2][:, hc * 512:hc * 512 + tw],
                                         in_=xnf[hc * 128:(hc + 1) * 128, t0:t0 + tw])\
                              .then_inc(xldc.h, 16)
                    xldc.n += 16 * 8
                    xl = xldc.n

                    for kind, wlin, wgate, blin, bgate, dst, dw in (
                            [("q", wq_sb, wqc_sb, bq_sb, bqc_sb, q_sb, S)] if tt < 8 else []
                    ) + [("k", wk_sb, wkc_sb, bk_sb, bkc_sb, k_sb, SP)]:
                        for oc in range(2):
                            @blk.tensor
                            def _(t, tt=tt, tw=tw, oc=oc, wlin=wlin, wgate=wgate, xl=xl, xldc=xldc):
                                t.wait_ge(xldc.h, xl)
                                t.wait_ge(wgld.h, wgld.n)
                                t.wait_ge(ac.h, ac.n)
                                t.wait_ge(dv.h, dv.n)
                                for hc in range(8):
                                    t.matmul(pql[:, 0:tw],
                                             wlin[:, hc * 256 + oc * 128: hc * 256 + (oc + 1) * 128],
                                             xnc[tt % 2][:, hc * 512:hc * 512 + tw],
                                             start=(hc == 0), stop=(hc == 7))
                                for hc in range(8):
                                    ins = t.matmul(pqg[:, 0:tw],
                                                   wgate[:, hc * 256 + oc * 128: hc * 256 + (oc + 1) * 128],
                                                   xnc[tt % 2][:, hc * 512:hc * 512 + tw],
                                                   start=(hc == 0), stop=(hc == 7))
                                pe.inc(ins)
                            mm = pe.n

                            @blk.scalar
                            def _(sc, tw=tw, oc=oc, bgate=bgate, mm=mm):
                                sc.wait_ge(pe.h, mm)
                                sc.wait_ge(dv.h, dv.n)
                                ac.inc(sc.activation(out=tmp2[:, 0:tw], in_=pqg[:, 0:tw],
                                                     func=AF.Sigmoid,
                                                     bias=bgate[:, oc:oc + 1]))
                            gq = ac.n

                            @blk.vector
                            def _(v, t0=t0, tw=tw, oc=oc, blin=blin, dst=dst, dw=dw, gq=gq):
                                v.wait_ge(ac.h, gq)
                                dv.inc(v.scalar_tensor_tensor(
                                    out=dst[:, oc * dw + t0: oc * dw + t0 + tw],
                                    in0=pql[:, 0:tw], scalar=blin[:, oc:oc + 1],
                                    in1=tmp2[:, 0:tw], op0=OP.add, op1=OP.mult))

                    nsub = (tw + 127) // 128
                    for su in range(nsub):
                        stw = min(128, tw - su * 128)
                        kc = (t0 + su * 128) // 128
                        for half in range(2):
                            @blk.tensor
                            def _(t, tt=tt, su=su, stw=stw, half=half, xl=xl, xldc=xldc):
                                t.wait_ge(xldc.h, xl)
                                t.wait_ge(wgld.h, wgld.n)
                                t.wait_ge(dv.h, dv.n)
                                for hc in range(8):
                                    t.matmul(pvl[0:stw, :],
                                             xnc[tt % 2][:, hc * 512 + su * 128: hc * 512 + su * 128 + stw],
                                             wv_sb[:, hc * 1024 + half * 512: hc * 1024 + (half + 1) * 512],
                                             start=(hc == 0), stop=(hc == 7))
                                    ins = t.matmul(pvg[0:stw, :],
                                                   xnc[tt % 2][:, hc * 512 + su * 128: hc * 512 + su * 128 + stw],
                                                   wvc_sb[:, hc * 1024 + half * 512: hc * 1024 + (half + 1) * 512],
                                                   start=(hc == 0), stop=(hc == 7))
                                pe.inc(ins)
                            vm = pe.n

                            @blk.vector
                            def _(v, stw=stw, half=half, vm=vm):
                                v.wait_ge(pe.h, vm)
                                dv.inc(v.tensor_add(tmp1[0:stw, :], pvg[0:stw, :],
                                                    bvcb[0:stw, half * 512:(half + 1) * 512]))
                            g1 = dv.n

                            @blk.scalar
                            def _(sc, stw=stw, g1=g1):
                                sc.wait_ge(dv.h, dv.n)
                                ac.inc(sc.activation(out=tmp2[0:stw, :], in_=tmp1[0:stw, :],
                                                     func=AF.Sigmoid))
                            g2 = ac.n

                            @blk.vector
                            def _(v, stw=stw, half=half, kc=kc, g2=g2):
                                v.wait_ge(gmk.h, 3)
                                dv.inc(v.tensor_add(tmp3[0:stw, :], pvl[0:stw, :],
                                                    bvb[0:stw, half * 512:(half + 1) * 512]))
                                v.wait_ge(ac.h, g2)
                                v.wait_ge(dv.h, dv.n)
                                dst = v_sb[0:stw, kc * 1040:(kc + 1) * 1040]\
                                    .rearrange("p (h c) -> p h c", h=16, c=65)\
                                    [:, half * 8:(half + 1) * 8, 0:64]
                                src = tmp3[0:stw, :].rearrange("p (h c) -> p h c", h=8, c=64)
                                dv.inc(v.tensor_mul(dst, src,
                                    tmp2[0:stw, :].rearrange("p (h c) -> p h c", h=8, c=64)))
                    evac_done[tt] = dv.n

        if dbg is not None:
            dso = Ctr(nc, tag + "dso")
            with nc.Block(tag + "dbg") as blk:
                @blk.sync
                def _(sy):
                    sy.dma_start(out=dbg[0], in_=q_sb[:, :]).then_inc(dso.h, 16)
                    sy.dma_start(out=dbg[1], in_=k_sb[:, :]).then_inc(dso.h, 16)
                    sy.dma_start(out=dbg[2], in_=v_sb[:, :]).then_inc(dso.h, 16)
                    sy.wait_ge(dso.h, 48)
            return

        prep_dv = dv.n
        prep_ac = ac.n

        with ExitStack() as es3:
            sb3 = lambda n, sh, dt: es3.enter_context(nc.sbuf_tensor(tag + n, sh, dt)).ap()
            krep = sb3("krep", [128, SP], BF16)
            qrep = sb3("qrep", [128, S], BF16)
            e_sb = [sb3(f"e{i}", [128, 1536], BF16) for i in range(2)]
            den = sb3("den", [1, 512], F32)
            rden = sb3("rden", [1, 512], F32)
            bc_sb = sb3("bc", [64, 512], F32)
            sstg = sb3("sstg", [64, 512], F32)
            ones64 = sb3("o64", [1, 64], F32)
            ps3 = lambda n, sh, dt: es3.enter_context(nc.psum_tensor(tag + n, sh, dt)).ap()
            ppa = ps3("ppa", [128, 1536], F32)
            ppb = ps3("ppb", [128, 1536], F32)
            ppv = ps3("ppv", [65, 512], F32)
            pbc2 = ps3("pbc2", [64, 512], F32)
            pp = [ppa, ppb]

            rep = Ctr(nc, tag + "rep")

            @_one_block(nc, tag + "main")
            def _(blk):
                mk = Ctr(nc, tag + "mk")

                @blk.gpsimd
                def _(g):
                    g.memset(ones64, 1.0).then_inc(mk.h, 1)
                mk.n += 1

                for h in range(NH):
                    oc, row = h // 8, (h % 8) * 16

                    @blk.gpsimd
                    def _(g, oc=oc, row=row):
                        g.wait_ge(dv.h, prep_dv)
                        g.wait_ge(so.h, so.n)
                        for i in range(3):
                            g.dma_start(out=krep[32 * i:32 * i + 16, :],
                                        in_=k_sb[row:row + 16, oc * SP:(oc + 1) * SP])\
                                .then_inc(rep.h, 16)
                            g.dma_start(out=qrep[32 * i:32 * i + 16, :],
                                        in_=q_sb[row:row + 16, oc * S:(oc + 1) * S])\
                                .then_inc(rep.h, 16)
                    rep.n += 16 * 6
                    repd = rep.n

                    for tq in range(8):
                        ngrp = len(GRP3)
                        exp_done = {}
                        pv_done_grp = {}
                        qk_pe = {}
                        for gi in range(ngrp + 1):
                            if gi < ngrp:
                                c0, gw = GRP3[gi]

                                @blk.tensor
                                def _(t, h=h, tq=tq, gi=gi, c0=c0, gw=gw, repd=repd):
                                    t.wait_ge(rep.h, repd)
                                    if gi >= 2:
                                        t.wait_ge(ac.h, exp_done[gi - 2])
                                    for i in range(gw):
                                        kc = c0 + i
                                        kw = KCW[kc]
                                        ins = t.matmul(
                                            pp[gi % 2][0:kw, i * 512:(i + 1) * 512],
                                            krep[32 * i:32 * i + 16, kc * 128:kc * 128 + kw],
                                            qrep[32 * i:32 * i + 16, tq * 512:(tq + 1) * 512],
                                            start=True, stop=True,
                                            tile_position=(32 * i, 0))
                                    pe.inc(ins)
                                qk_pe[gi] = pe.n

                            if gi >= 1:
                                gj = gi - 1
                                c0j, gwj = GRP3[gj]

                                @blk.tensor
                                def _(t, h=h, gj=gj, c0j=c0j, gwj=gwj):
                                    t.wait_ge(ac.h, exp_done[gj])
                                    if gj == 0:
                                        t.wait_ge(gmk.h, 3)
                                        t.wait_ge(dv.h, dv.n)
                                    for i in range(gwj):
                                        kc = c0j + i
                                        kw = KCW[kc]
                                        ins = t.matmul(
                                            ppv[:, :],
                                            v_sb[0:kw, kc * 1040 + 65 * h: kc * 1040 + 65 * h + 65],
                                            e_sb[gj % 2][0:kw, i * 512:(i + 1) * 512],
                                            start=(gj == 0 and i == 0),
                                            stop=(gj == ngrp - 1 and i == gwj - 1))
                                    pe.inc(ins)
                                pv_done_grp[gj] = pe.n

                            if gi < ngrp:
                                c0, gw = GRP3[gi]
                                qk = qk_pe[gi]

                                @blk.scalar
                                def _(sc, gi=gi, c0=c0, gw=gw, qk=qk):
                                    sc.wait_ge(pe.h, qk)
                                    if gi >= 2:
                                        sc.wait_ge(pe.h, pv_done_grp.get(gi - 2, 0))
                                    ac.inc(sc.activation(out=e_sb[gi % 2][:, 0:gw * 512],
                                                         in_=pp[gi % 2][:, 0:gw * 512],
                                                         func=AF.Exp, scale=0.125))
                                exp_done[gi] = ac.n
                        pvd = pe.n

                        sworn = so.n
                        @blk.vector
                        def _(v, pvd=pvd, sworn=sworn):
                            v.wait_ge(pe.h, pvd)
                            v.wait_ge(so.h, sworn)
                            dv.inc(v.tensor_copy(den, ppv[64:65, :]))
                            v.wait_ge(dv.h, dv.n)
                            dv.inc(v.reciprocal(rden, den))
                        rd = dv.n

                        @blk.tensor
                        def _(t, rd=rd):
                            t.wait_ge(mk.h, 1)
                            t.wait_ge(dv.h, rd)
                            pe.inc(t.matmul(pbc2, ones64, rden, start=True, stop=True))
                        bcd = pe.n

                        @blk.vector
                        def _(v, bcd=bcd):
                            v.wait_ge(pe.h, bcd)
                            dv.inc(v.tensor_copy(bc_sb, pbc2))
                            v.wait_ge(dv.h, dv.n)
                            dv.inc(v.tensor_mul(sstg, ppv[0:64, :], bc_sb))
                        sd = dv.n

                        @blk.sync
                        def _(sy, h=h, tq=tq, sd=sd):
                            sy.wait_ge(dv.h, sd)
                            sy.dma_start(out=s_f[h * 64:(h + 1) * 64,
                                                 tq * 512:(tq + 1) * 512],
                                         in_=sstg).then_inc(so.h, 16)
                        so.n += 16

                @blk.sync
                def _(sy):
                    sy.wait_ge(so.h, so.n)
    return [(so.h, so.n)]


def build_l0ffn():
    nc = bass.Bass("TRN2", target_bir_lowering=False)
    xnf = nc.dram_tensor("xnf", [D, SP], BF16, kind="ExternalInput").ap()
    s_f = nc.dram_tensor("s_f", [D, S], F32, kind="ExternalInput").ap()
    x_ap = nc.dram_tensor("x", [S, D], F32, kind="ExternalInput").ap()
    wgs = nc.dram_tensor("wgs", [D, 2048], F32, kind="ExternalInput").ap()
    wus = nc.dram_tensor("wus", [D, 2048], F32, kind="ExternalInput").ap()
    wds = nc.dram_tensor("wds", [2048, D], F32, kind="ExternalInput").ap()
    wge = nc.dram_tensor("wge", [8, D, 2048], F32, kind="ExternalInput").ap()
    wue = nc.dram_tensor("wue", [8, D, 2048], F32, kind="ExternalInput").ap()
    wde = nc.dram_tensor("wde", [8, 2048, D], F32, kind="ExternalInput").ap()
    wr = nc.dram_tensor("wr", [D, 8], F32, kind="ExternalInput").ap()
    rb = nc.dram_tensor("rb", [1, 8], F32, kind="ExternalInput").ap()
    wkt = nc.dram_tensor("wkt", [D, D], F32, kind="ExternalInput").ap()
    wvt = nc.dram_tensor("wvt", [D, D], F32, kind="ExternalInput").ap()
    x1_ap = nc.dram_tensor("x1", [S, D], F32, kind="ExternalOutput").ap()
    xn1f_ap = nc.dram_tensor("xn1f", [D, S], BF16, kind="ExternalOutput").ap()
    k1_ap = nc.dram_tensor("k1", [S, D], BF16, kind="ExternalOutput").ap()
    v1_ap = nc.dram_tensor("v1", [S, D], BF16, kind="ExternalOutput").ap()
    emit_l0ffn(nc, xnf, s_f, x_ap, (wgs, wus, wds, wge, wue, wde, wr, rb),
               x1_ap, xn1f_ap, "f0")
    emit_kvproj(nc, xn1f_ap, wkt, wvt, k1_ap, v1_ap, "kv")
    return nc


def emit_l0ffn(nc, xnf, s_f, x_ap, ws, x1_ap, xn1f_ap, tag, xn1send_ap=None):
    wgs, wus, wds, wge, wue, wde, wr, rb = ws
    NTILE = 4
    TT = 1024
    NE = 9
    NIC = 16
    with ExitStack() as es:
        sb = lambda n, sh, dt: es.enter_context(nc.sbuf_tensor(tag + n, sh, dt)).ap()
        ps = lambda n, sh, dt: es.enter_context(nc.psum_tensor(tag + n, sh, dt)).ap()
        xnt = sb("xnt", [128, 8 * TT], BF16)
        wg_sb = [sb(f"wg{i}", [128, 1024], BF16) for i in range(2)]
        wu_sb = [sb(f"wu{i}", [128, 1024], BF16) for i in range(2)]
        wd_sb = [sb(f"wd{i}", [128, 16 * 1024], BF16) for i in range(2)]
        act_sb = sb("act", [128, 16 * TT], BF16)
        acc_sb = sb("acc", [128, 8 * 1024], F32)
        sg_sb = [sb(f"sg{i}", [128, 1024], F32) for i in range(2)]
        wr_sb = sb("wrs", [128, 8 * 8], BF16)
        rbb = sb("rbb", [128, 8], F32)
        rrow = sb("rrow", [1, 8], F32)
        ones1 = sb("ones1", [1, 128], F32)
        cw_sb = sb("cw", [128, 8 * 8], F32)
        rt = sb("rt", [128, 5 * 8], F32)
        rm = sb("rm", [128, 4], F32)
        sin_sb = [sb(f"sin{i}", [128, 1024], F32) for i in range(2)]
        stm_sb = sb("stm", [128, 1024], F32)
        xin_sb = [sb(f"xin{i}", [128, 1024], F32) for i in range(2)]
        x1t_sb = [sb(f"x1t{i}", [128, 1024], F32) for i in range(2)]
        xn1_sb = sb("xn1", [128, 1024], F32)
        nst_sb = [sb(f"nst{i}", [128, 1024], BF16) for i in range(2)]
        lstt = sb("lstt", [128, 8], F32)
        sqs = sb("sqs", [128, 1024], F32)
        epst = sb("epst", [128, 1], F32)
        ident = sb("idb", [128, 128], BF16)
        identf = sb("idf", [128, 128], F32)

        pg = [ps(f"pg{i}", [128, 1024], F32) for i in range(2)]
        pu = [ps(f"pu{i}", [128, 1024], F32) for i in range(2)]
        pd = [pg[0][:, 0:512], pg[1][:, 0:512]]
        prt = pu[0][:, 0:64]
        ptr_s = pu[1][:, 0:512]
        ptr_n = pg[1][:, 512:1024]

        ac = Ctr(nc, tag + "ac")
        dv = Ctr(nc, tag + "dv")
        pe = Ctr(nc, tag + "pe")
        gmk = Ctr(nc, tag + "gmk")
        bld = Ctr(nc, tag + "bld")
        rld = Ctr(nc, tag + "rld")
        xld = Ctr(nc, tag + "xld")
        wlds = [Ctr(nc, tag + f"wld{i}") for i in range(2)]
        wdld = [Ctr(nc, tag + f"wdld{i}") for i in range(2)]
        slds = [Ctr(nc, tag + f"sld{i}") for i in range(2)]
        xilds = [Ctr(nc, tag + f"xild{i}") for i in range(2)]
        sox = [Ctr(nc, tag + f"sox{i}") for i in range(2)]
        son = [Ctr(nc, tag + f"son{i}") for i in range(2)]

        wd_free = {}
        epi_done = {}

        with nc.Block(tag) as blk:
            @blk.gpsimd
            def _(g):
                g.memset(epst, LN_EPS)
                g.memset(ones1, 1.0)
                _mk_identity(g, ident, gmk).then_inc(gmk.h, 1)
                gmk.n += 1
                g.wait_ge(gmk.h, gmk.n)
                _mk_identity(g, identf, gmk).then_inc(gmk.h, 1)
                gmk.n += 1
                for hc in range(8):
                    g.dma_start(out=wr_sb[:, hc * 8:(hc + 1) * 8],
                                in_=wr[hc * 128:(hc + 1) * 128, :]).then_inc(rld.h, 16)
            gmk_all = gmk.n
            rld.n += 16 * 8

            @blk.sync
            def _(sy):
                sy.dma_start(out=rrow, in_=rb).then_inc(bld.h, 16)
            bld.n += 16
            rbld = bld.n

            @blk.tensor
            def _(t, rbld=rbld):
                t.wait_ge(bld.h, rbld)
                t.wait_ge(gmk.h, gmk_all)
                pe.inc(t.matmul(prt[:, 0:8], ones1, rrow, start=True, stop=True))
            rb_pe = pe.n

            @blk.vector
            def _(v, rb_pe=rb_pe):
                v.wait_ge(pe.h, rb_pe)
                dv.inc(v.tensor_copy(rbb, prt[:, 0:8]))
            rbb_dv = dv.n

            wgfen = [0, 0]
            pgfen = [0, 0]
            pufen = [0, 0]
            prev_dn_pe = 0
            prev_dn_dv = 0
            for ti in range(NTILE):
                t0 = ti * TT

                prev_tile_fence = pe.n

                @blk.sync
                def _(sy, ti=ti, t0=t0, fence=prev_tile_fence):
                    sy.wait_ge(pe.h, fence)
                    for hc in range(8):
                        sy.dma_start(out=xnt[:, hc * TT:(hc + 1) * TT],
                                     in_=xnf[hc * 128:(hc + 1) * 128, t0:t0 + TT])\
                          .then_inc(xld.h, 16)
                xld.n += 16 * 8
                xfence = xld.n

                @blk.tensor
                def _(t, xfence=xfence, rbld=rbld):
                    t.wait_ge(xld.h, xfence)
                    t.wait_ge(bld.h, rbld)
                    t.wait_ge(rld.h, rld.n)
                    t.wait_ge(dv.h, dv.n)
                    for ts in range(8):
                        for hc in range(8):
                            ins = t.matmul(prt[:, ts * 8:(ts + 1) * 8],
                                           xnt[:, hc * TT + ts * 128: hc * TT + (ts + 1) * 128],
                                           wr_sb[:, hc * 8:(hc + 1) * 8],
                                           start=(hc == 0), stop=(hc == 7))
                    pe.inc(ins)
                rt_pe = pe.n

                for ts in range(8):
                    sl = slice(ts * 8, (ts + 1) * 8)

                    @blk.vector
                    def _(v, ts=ts, sl=sl, rt_pe=rt_pe, rbb_dv=rbb_dv, dfen=None):
                        v.wait_ge(pe.h, rt_pe)
                        v.wait_ge(dv.h, dv.n)
                        dv.inc(v.tensor_add(rt[:, 0:8], prt[:, sl], rbb))
                    lg_dv = dv.n

                    @blk.scalar
                    def _(sc, lg_dv=lg_dv):
                        sc.wait_ge(dv.h, lg_dv)
                        ac.inc(sc.activation(out=rt[:, 8:16], in_=rt[:, 0:8],
                                             func=AF.Sigmoid))
                    pr_ac = ac.n

                    @blk.vector
                    def _(v, ts=ts, sl=sl, pr_ac=pr_ac):
                        v.wait_ge(ac.h, pr_ac)
                        dv.inc(v.reduce_max(rm[:, 0:1], rt[:, 8:16], axis=AX))
                        v.wait_ge(dv.h, dv.n)
                        dv.inc(v.tensor_scalar(out=rt[:, 16:24], in0=rt[:, 8:16],
                                               scalar1=rm[:, 0:1], scalar2=None,
                                               op0=OP.is_ge))
                        v.wait_ge(dv.h, dv.n)
                        dv.inc(v.tensor_sub(rt[:, 24:32], rt[:, 8:16], rt[:, 16:24]))
                        v.wait_ge(dv.h, dv.n)
                        dv.inc(v.reduce_max(rm[:, 1:2], rt[:, 24:32], axis=AX))
                        v.wait_ge(dv.h, dv.n)
                        dv.inc(v.tensor_scalar(out=rt[:, 32:40], in0=rt[:, 8:16],
                                               scalar1=rm[:, 1:2], scalar2=None,
                                               op0=OP.is_ge))
                        v.wait_ge(dv.h, dv.n)
                        dv.inc(v.tensor_mul(cw_sb[:, sl], rt[:, 8:16], rt[:, 32:40]))
                cw_fence = dv.n

                for e in range(NE):
                    wgt = wgs if e == 0 else wge[e - 1]
                    wut = wus if e == 0 else wue[e - 1]
                    wdt = wds if e == 0 else wde[e - 1]

                    key = ti * NE + e
                    kk = key % 2

                    @blk.gpsimd
                    def _(g, e=e, ti=ti, wdt=wdt, key=key, kk=kk):
                        if key >= 2:
                            g.wait_ge(dv.h, wd_free[key - 2])
                        for icc in range(16):
                            g.dma_start(out=wd_sb[kk][:, icc * 1024:(icc + 1) * 1024],
                                        in_=wdt[icc * 128:(icc + 1) * 128, :])\
                              .then_inc(wdld[kk].h, 16)
                    wdld[kk].n += 16 * 16
                    wd_fence = wdld[kk].n

                    for ic in range(NIC):
                        wguard = wgfen[ic % 2]

                        @blk.gpsimd
                        def _(g, e=e, ic=ic, wgt=wgt, wut=wut, wguard=wguard):
                            if wguard:
                                g.wait_ge(dv.h, wguard)
                            src_g = wgt[:, ic * 128:(ic + 1) * 128]\
                                .rearrange("(hc p) i -> p hc i", p=128)
                            src_u = wut[:, ic * 128:(ic + 1) * 128]\
                                .rearrange("(hc p) i -> p hc i", p=128)
                            dst_g = wg_sb[ic % 2].rearrange("p (hc i) -> p hc i", hc=8)
                            dst_u = wu_sb[ic % 2].rearrange("p (hc i) -> p hc i", hc=8)
                            g.dma_start(out=dst_g, in_=src_g)\
                             .then_inc(wlds[ic % 2].h, 16)
                            g.dma_start(out=dst_u, in_=src_u)\
                             .then_inc(wlds[ic % 2].h, 16)
                        wlds[ic % 2].n += 32
                        wfence = wlds[ic % 2].n
                        pguard, puguard = pgfen[ic % 2], pufen[ic % 2]

                        @blk.tensor
                        def _(t, e=e, ic=ic, wfence=wfence, xfence=xfence,
                              pguard=pguard, puguard=puguard, cwf=cw_fence):
                            t.wait_ge(xld.h, xfence)
                            t.wait_ge(wlds[ic % 2].h, wfence)
                            if e == 0 and ic < 2:
                                t.wait_ge(dv.h, cwf)
                            if prev_dn_dv and ic < 2:
                                t.wait_ge(dv.h, prev_dn_dv)
                            if pguard:
                                t.wait_ge(ac.h, pguard)
                            if puguard:
                                t.wait_ge(dv.h, puguard)
                            for hc in range(8):
                                for half in range(2):
                                    t.matmul(pg[ic % 2][:, half * 512:(half + 1) * 512],
                                             wg_sb[ic % 2][:, hc * 128:(hc + 1) * 128],
                                             xnt[:, hc * TT + half * 512: hc * TT + (half + 1) * 512],
                                             start=(hc == 0), stop=(hc == 7))
                            for hc in range(8):
                                for half in range(2):
                                    ins = t.matmul(pu[ic % 2][:, half * 512:(half + 1) * 512],
                                                   wu_sb[ic % 2][:, hc * 128:(hc + 1) * 128],
                                                   xnt[:, hc * TT + half * 512: hc * TT + (half + 1) * 512],
                                                   start=(hc == 0), stop=(hc == 7))
                            pe.inc(ins)
                        up_pe = pe.n

                        @blk.scalar
                        def _(sc, ic=ic, up_pe=up_pe):
                            sc.wait_ge(pe.h, up_pe)
                            sc.wait_ge(dv.h, dv.n)
                            ac.inc(sc.activation(out=sg_sb[ic % 2], in_=pg[ic % 2],
                                                 func=AF.Silu))
                        pgfen[ic % 2] = ac.n
                        sil_ac = ac.n
                        dnguard = prev_dn_pe

                        @blk.vector
                        def _(v, ic=ic, sil_ac=sil_ac, dnguard=dnguard):
                            v.wait_ge(ac.h, sil_ac)
                            if dnguard:
                                v.wait_ge(pe.h, dnguard)
                            dv.inc(v.tensor_mul(act_sb[:, ic * TT:(ic + 1) * TT],
                                                pu[ic % 2], sg_sb[ic % 2]))
                        pufen[ic % 2] = dv.n
                        wgfen[ic % 2] = dv.n
                    act_fence = dv.n

                    dn_dv = {}
                    for ts in range(8):
                        for half in range(2):
                            j = ts * 2 + half

                            acfence_dn = ac.n

                            @blk.tensor
                            def _(t, e=e, ts=ts, half=half, j=j, kk=kk,
                                  act_fence=act_fence, wd_fence=wd_fence,
                                  acfence_dn=acfence_dn):
                                t.wait_ge(dv.h, act_fence)
                                t.wait_ge(ac.h, acfence_dn)
                                t.wait_ge(wdld[kk].h, wd_fence)
                                if j >= 2:
                                    t.wait_ge(dv.h, dn_dv[j - 2])
                                for icc in range(16):
                                    ins = t.matmul(
                                        pd[j % 2],
                                        act_sb[:, icc * TT + ts * 128: icc * TT + (ts + 1) * 128],
                                        wd_sb[kk][:, icc * 1024 + half * 512: icc * 1024 + (half + 1) * 512],
                                        start=(icc == 0), stop=(icc == 15))
                                pe.inc(ins)
                            dn_pe = pe.n

                            acguard = ac.n if e == 0 else 0

                            @blk.vector
                            def _(v, e=e, ts=ts, half=half, j=j, dn_pe=dn_pe,
                                  cw_fence=cw_fence, acguard=acguard):
                                v.wait_ge(pe.h, dn_pe)
                                v.wait_ge(dv.h, cw_fence)
                                if acguard:
                                    v.wait_ge(ac.h, acguard)
                                asl = acc_sb[:, ts * 1024 + half * 512: ts * 1024 + (half + 1) * 512]
                                if e == 0:
                                    dv.inc(v.tensor_copy(asl, pd[j % 2]))
                                else:
                                    dv.inc(v.scalar_tensor_tensor(
                                        out=asl, in0=pd[j % 2],
                                        scalar=cw_sb[:, ts * 8 + (e - 1): ts * 8 + e],
                                        in1=asl, op0=OP.mult, op1=OP.add))
                            dn_dv[j] = dv.n
                    wd_free[key] = dv.n
                    prev_dn_pe = pe.n
                    prev_dn_dv = dv.n
                ffn_fence = dv.n

                for ts in range(8):
                    tglob = t0 + ts * 128
                    bsel = ts % 2

                    @blk.sync
                    def _(sy, ti=ti, ts=ts, tglob=tglob, bsel=bsel):
                        if ti * 8 + ts >= 2:
                            sy.wait_ge(dv.h, epi_done[ti * 8 + ts - 2])
                        for j in range(8):
                            sy.dma_start(out=sin_sb[bsel][:, j * 128:(j + 1) * 128],
                                         in_=s_f[j * 128:(j + 1) * 128, tglob:tglob + 128])\
                              .then_inc(slds[bsel].h, 16)
                        sy.dma_start(out=xin_sb[bsel], in_=x_ap[tglob:tglob + 128, :])\
                          .then_inc(xilds[bsel].h, 16)
                    slds[bsel].n += 16 * 8
                    xilds[bsel].n += 16
                    sfence = slds[bsel].n
                    xifence = xilds[bsel].n

                    stm_dv = None
                    for rnd in range(2):
                        @blk.tensor
                        def _(t, ts=ts, bsel=bsel, sfence=sfence, rnd=rnd):
                            t.wait_ge(slds[bsel].h, sfence)
                            t.wait_ge(gmk.h, gmk_all)
                            t.wait_ge(dv.h, dv.n)
                            for j in range(4):
                                jj = rnd * 4 + j
                                ins = t.transpose(ptr_s[:, j * 128:(j + 1) * 128],
                                                  sin_sb[bsel][:, jj * 128:(jj + 1) * 128],
                                                  identf)
                            pe.inc(ins)
                        str_pe = pe.n

                        @blk.vector
                        def _(v, str_pe=str_pe, rnd=rnd):
                            v.wait_ge(pe.h, str_pe)
                            v.wait_ge(dv.h, dv.n)
                            dv.inc(v.tensor_copy(stm_sb[:, rnd * 512:(rnd + 1) * 512],
                                                 ptr_s))
                    stm_dv = dv.n

                    @blk.scalar
                    def _(sc, ts=ts, ffn_fence=ffn_fence, acfen=None):
                        sc.wait_ge(dv.h, ffn_fence)
                        sc.wait_ge(dv.h, dv.n)
                        sc.wait_ge(ac.h, ac.n)
                        ac.inc(sc.activation(out=sqs, in_=acc_sb[:, ts * 1024:(ts + 1) * 1024],
                                             func=AF.Silu))
                    sil_ac2 = ac.n

                    @blk.vector
                    def _(v, ts=ts, bsel=bsel, stm_dv=stm_dv, sil_ac2=sil_ac2,
                          xifence=xifence):
                        v.wait_ge(dv.h, stm_dv)
                        v.wait_ge(ac.h, sil_ac2)
                        dv.inc(v.tensor_mul(stm_sb, stm_sb, sqs))
                        v.wait_ge(xilds[bsel].h, xifence)
                        v.wait_ge(sox[bsel].h, sox[bsel].n)
                        v.wait_ge(dv.h, dv.n)
                        dv.inc(v.tensor_add(x1t_sb[bsel], xin_sb[bsel], stm_sb))
                    x1_dv = dv.n

                    @blk.scalar
                    def _(sc, bsel=bsel, x1_dv=x1_dv):
                        sc.wait_ge(dv.h, x1_dv)
                        ac.inc(sc.activation(out=sqs, in_=x1t_sb[bsel], func=AF.Square,
                                             accum_out=lstt[:, 1:2]))
                    s2_ac = ac.n

                    @blk.vector
                    def _(v, bsel=bsel, x1_dv=x1_dv, s2_ac=s2_ac):
                        v.wait_ge(dv.h, x1_dv)
                        dv.inc(v.reduce_sum(lstt[:, 0:1], x1t_sb[bsel], axis=AX))
                        v.wait_ge(dv.h, dv.n)
                        dv.inc(v.tensor_scalar(out=lstt[:, 2:3], in0=lstt[:, 0:1],
                                               scalar1=1.0 / D, scalar2=None, op0=OP.mult))
                        v.wait_ge(dv.h, dv.n)
                        dv.inc(v.scalar_tensor_tensor(out=lstt[:, 5:6], in0=lstt[:, 2:3],
                                                      scalar=1.0, in1=lstt[:, 2:3],
                                                      op0=OP.mult, op1=OP.mult))
                        v.wait_ge(ac.h, s2_ac)
                        v.wait_ge(dv.h, dv.n)
                        dv.inc(v.scalar_tensor_tensor(out=lstt[:, 3:4], in0=lstt[:, 1:2],
                                                      scalar=1.0 / D, in1=lstt[:, 5:6],
                                                      op0=OP.mult, op1=OP.subtract))
                    var_dv = dv.n

                    @blk.scalar
                    def _(sc, var_dv=var_dv):
                        sc.wait_ge(dv.h, var_dv)
                        ac.inc(sc.activation(out=lstt[:, 4:5], in_=lstt[:, 3:4],
                                             func=AF.Sqrt, bias=epst[:, 0:1]))
                    std_ac = ac.n

                    @blk.vector
                    def _(v, bsel=bsel, std_ac=std_ac):
                        v.wait_ge(ac.h, std_ac)
                        dv.inc(v.reciprocal(lstt[:, 5:6], lstt[:, 4:5]))
                        v.wait_ge(dv.h, dv.n)
                        dv.inc(v.tensor_scalar(out=xn1_sb, in0=x1t_sb[bsel],
                                               scalar1=lstt[:, 2:3], scalar2=lstt[:, 5:6],
                                               op0=OP.subtract, op1=OP.mult))
                    xn1_dv = dv.n

                    for rnd in range(2):
                        @blk.tensor
                        def _(t, xn1_dv=xn1_dv, rnd=rnd):
                            t.wait_ge(dv.h, xn1_dv)
                            t.wait_ge(dv.h, dv.n)
                            for j in range(4):
                                jj = rnd * 4 + j
                                ins = t.transpose(ptr_n[:, j * 128:(j + 1) * 128],
                                                  xn1_sb[:, jj * 128:(jj + 1) * 128],
                                                  identf)
                            pe.inc(ins)
                        ntr_pe = pe.n

                        @blk.vector
                        def _(v, ti=ti, ts=ts, bsel=bsel, ntr_pe=ntr_pe, rnd=rnd):
                            v.wait_ge(pe.h, ntr_pe)
                            if ti * 8 + ts >= 2:
                                v.wait_ge(son[bsel].h, son[bsel].n)
                            dv.inc(v.tensor_copy(nst_sb[bsel][:, rnd * 512:(rnd + 1) * 512],
                                                 ptr_n))
                    nst_dv = dv.n

                    @blk.sync
                    def _(sy, ti=ti, ts=ts, tglob=tglob, bsel=bsel, x1_dv=x1_dv,
                          nst_dv=nst_dv):
                        sy.wait_ge(dv.h, x1_dv)
                        sy.dma_start(out=x1_ap[tglob:tglob + 128, :],
                                     in_=x1t_sb[bsel]).then_inc(sox[bsel].h, 16)
                        sy.wait_ge(dv.h, nst_dv)
                        for j in range(8):
                            sy.dma_start(out=xn1f_ap[j * 128:(j + 1) * 128,
                                                     tglob:tglob + 128],
                                         in_=nst_sb[bsel][:, j * 128:(j + 1) * 128])\
                              .then_inc(son[bsel].h, 16)
                        if xn1send_ap is not None:
                            w = tglob // 512
                            c0 = tglob - w * 512
                            for j in range(8):
                                sy.dma_start(
                                    out=xn1send_ap[w * 1024 + j * 128:
                                                   w * 1024 + (j + 1) * 128,
                                                   c0:c0 + 128],
                                    in_=nst_sb[bsel][:, j * 128:(j + 1) * 128])\
                                  .then_inc(son[bsel].h, 16)
                    sox[bsel].n += 16
                    son[bsel].n += 16 * 8 * (2 if xn1send_ap is not None else 1)
                    epi_done[ti * 8 + ts] = dv.n
                pgfen = [0, 0]
                pufen = [dv.n, dv.n]
                prev_dn_pe = pe.n

            @blk.sync
            def _(sy):
                for cs in sox + son:
                    sy.wait_ge(cs.h, cs.n)
        return [(cs.h, cs.n) for cs in sox + son]


def emit_kvproj(nc, xn1f, wkt, wvt, k1_ap, v1_ap, tag):
    with ExitStack() as es:
        sb = lambda n, sh, dt: es.enter_context(nc.sbuf_tensor(tag + n, sh, dt)).ap()
        ps = lambda n, sh, dt: es.enter_context(nc.psum_tensor(tag + n, sh, dt)).ap()
        xnt = sb("xnt", [128, 8 * 1024], BF16)
        wk_sb = sb("wk", [128, 8 * 1024], BF16)
        wv_sb = sb("wv", [128, 8 * 1024], BF16)
        kst = [sb(f"kst{i}", [128, 1024], BF16) for i in range(2)]
        vst = [sb(f"vst{i}", [128, 1024], BF16) for i in range(2)]
        pk = ps("pk", [128, 1024], F32)
        pv = ps("pv", [128, 1024], F32)

        wld = Ctr(nc, tag + "wld")
        xld = Ctr(nc, tag + "xld")
        pe = Ctr(nc, tag + "pe")
        dv = Ctr(nc, tag + "dv")
        sos = [Ctr(nc, tag + f"so{i}") for i in range(2)]

        with nc.Block(tag) as blk:
            @blk.gpsimd
            def _(g):
                for hc in range(8):
                    g.dma_start(out=wk_sb[:, hc * 1024:(hc + 1) * 1024],
                                in_=wkt[hc * 128:(hc + 1) * 128, :]).then_inc(wld.h, 16)
                    g.dma_start(out=wv_sb[:, hc * 1024:(hc + 1) * 1024],
                                in_=wvt[hc * 128:(hc + 1) * 128, :]).then_inc(wld.h, 16)
            wld.n += 16 * 16
            wfence = wld.n

            out_done = {}
            for ti in range(4):
                t0 = ti * 1024

                @blk.sync
                def _(sy, ti=ti, t0=t0, pfen=None):
                    sy.wait_ge(pe.h, pe.n)
                    for hc in range(8):
                        sy.dma_start(out=xnt[:, hc * 1024:(hc + 1) * 1024],
                                     in_=xn1f[hc * 128:(hc + 1) * 128, t0:t0 + 1024])\
                          .then_inc(xld.h, 16)
                xld.n += 16 * 8
                xfence = xld.n

                for ts in range(8):
                    bsel = ts % 2
                    idx = ti * 8 + ts

                    @blk.tensor
                    def _(t, ts=ts, xfence=xfence, wfence=wfence):
                        t.wait_ge(xld.h, xfence)
                        t.wait_ge(wld.h, wfence)
                        t.wait_ge(dv.h, dv.n)
                        for hc in range(8):
                            for half in range(2):
                                t.matmul(pk[:, half * 512:(half + 1) * 512],
                                         xnt[:, hc * 1024 + ts * 128: hc * 1024 + (ts + 1) * 128],
                                         wk_sb[:, hc * 1024 + half * 512: hc * 1024 + (half + 1) * 512],
                                         start=(hc == 0), stop=(hc == 7))
                        for hc in range(8):
                            for half in range(2):
                                ins = t.matmul(pv[:, half * 512:(half + 1) * 512],
                                               xnt[:, hc * 1024 + ts * 128: hc * 1024 + (ts + 1) * 128],
                                               wv_sb[:, hc * 1024 + half * 512: hc * 1024 + (half + 1) * 512],
                                               start=(hc == 0), stop=(hc == 7))
                        pe.inc(ins)
                    kv_pe = pe.n

                    @blk.vector
                    def _(v, bsel=bsel, idx=idx, kv_pe=kv_pe):
                        v.wait_ge(pe.h, kv_pe)
                        if idx >= 2:
                            v.wait_ge(sos[bsel].h, out_done[idx - 2])
                        dv.inc(v.tensor_copy(kst[bsel], pk))
                        dv.inc(v.tensor_copy(vst[bsel], pv))
                    kv_dv = dv.n

                    @blk.sync
                    def _(sy, ti=ti, ts=ts, t0=t0, bsel=bsel, kv_dv=kv_dv):
                        sy.wait_ge(dv.h, kv_dv)
                        tg = t0 + ts * 128
                        sy.dma_start(out=k1_ap[tg:tg + 128, :], in_=kst[bsel])\
                          .then_inc(sos[bsel].h, 16)
                        sy.dma_start(out=v1_ap[tg:tg + 128, :], in_=vst[bsel])\
                          .then_inc(sos[bsel].h, 16)
                    sos[bsel].n += 32
                    out_done[idx] = sos[bsel].n

            @blk.sync
            def _(sy):
                sy.wait_ge(sos[0].h, sos[0].n)
                sy.wait_ge(sos[1].h, sos[1].n)
        return [(sos[0].h, sos[0].n), (sos[1].h, sos[1].n)]


def _one_block(nc, name):
    def deco(f):
        with nc.Block(name) as blk:
            f(blk)
    return deco



def emit_l1scores(nc, xn1q_sh, wqt, k1all, mask, a_send, tag, a_recv=None, cc=None):
    NB = 8
    with ExitStack() as es:
        dv = Ctr(nc, tag + "dv")
        ac = Ctr(nc, tag + "ac")
        pe = Ctr(nc, tag + "pe")

        with ExitStack() as es2:
            sb2 = lambda n, sh, dt: es2.enter_context(nc.sbuf_tensor(tag + n, sh, dt)).ap()
            ps2 = lambda n, sh, dt: es2.enter_context(nc.psum_tensor(tag + n, sh, dt)).ap()
            q1_sb = sb2("q1", [128, 8 * S], BF16)
            wq_sb = sb2("wq", [128, 8 * 1024], BF16)
            xq_sb = sb2("xq", [128, 8 * 512], BF16)
            kt_sb = [sb2(f"kt{i}", [128, 8 * 512], BF16) for i in range(3)]
            e_sb = sb2("e", [128, 32 * 512], F32)
            mk_sb = sb2("mk", [128, 4 * 512], I8)
            tiny = sb2("tiny", [128, 512], F32)
            d_sb = sb2("d", [128, 512], F32)
            r_sb = sb2("r", [128, 512], F32)
            a_st = [sb2(f"ast{i}", [128, 512], BF16) for i in range(2)]
            pq = [ps2(f"pq{i}", [128, 512], F32) for i in range(2)]

            wld = Ctr(nc, tag + "wld")
            xqld = Ctr(nc, tag + "xqld")
            ktld = [Ctr(nc, tag + f"ktld{i}") for i in range(3)]
            mkld = Ctr(nc, tag + "mkld")
            aso = [Ctr(nc, tag + f"aso{i}") for i in range(2)]
            gmk = Ctr(nc, tag + "gmk")

            with nc.Block(tag + "qs") as blk:
                @blk.gpsimd
                def _(g):
                    g.memset(tiny, 1e-9).then_inc(gmk.h, 1)
                    for hc in range(8):
                        g.dma_start(out=wq_sb[:, hc * 1024:(hc + 1) * 1024],
                                    in_=wqt[hc * 128:(hc + 1) * 128, :])\
                          .then_inc(wld.h, 16)
                gmk.n += 1
                wld.n += 16 * 8
                wfen = wld.n

                for b in range(NB):
                    @blk.sync
                    def _(sy, b=b):
                        sy.wait_ge(pe.h, pe.n)
                        for hc in range(8):
                            sy.dma_start(out=xq_sb[:, hc * 512:(hc + 1) * 512],
                                         in_=xn1q_sh[b * 1024 + hc * 128:
                                                     b * 1024 + (hc + 1) * 128, 0:512])\
                              .then_inc(xqld.h, 16)
                    xqld.n += 16 * 8
                    xfen = xqld.n

                    for dc in range(8):
                        @blk.tensor
                        def _(t, b=b, dc=dc, xfen=xfen, wfen=wfen):
                            t.wait_ge(xqld.h, xfen)
                            t.wait_ge(wld.h, wfen)
                            t.wait_ge(dv.h, dv.n)
                            for hc in range(8):
                                ins = t.matmul(pq[dc % 2],
                                               wq_sb[:, hc * 1024 + dc * 128: hc * 1024 + (dc + 1) * 128],
                                               xq_sb[:, hc * 512:(hc + 1) * 512],
                                               start=(hc == 0), stop=(hc == 7))
                            pe.inc(ins)
                        qpe = pe.n

                        @blk.vector
                        def _(v, b=b, dc=dc, qpe=qpe):
                            v.wait_ge(pe.h, qpe)
                            dv.inc(v.tensor_copy(
                                q1_sb[:, dc * S + b * 512: dc * S + (b + 1) * 512],
                                pq[dc % 2]))
                q_fence = dv.n

                prev_stripe_dv = dv.n
                ktuse = {}
                for st in range(8):
                    j0 = st * 512

                    @blk.sync
                    def _(sy, st=st, j0=j0):
                        sy.wait_ge(dv.h, dv.n)
                        msrc = mask[j0:j0 + 512, :]\
                            .rearrange("(jc p) i -> p jc i", p=128)
                        mdst = mk_sb.rearrange("p (jc i) -> p jc i", jc=4)
                        sy.dma_start(out=mdst, in_=msrc).then_inc(mkld.h, 16)
                    mkld.n += 16
                    mfen = mkld.n

                    for b in range(NB):
                        kti = st * NB + b
                        ktc = ktld[kti % 3]

                        @blk.gpsimd
                        def _(g, st=st, b=b, j0=j0, kti=kti, ktc=ktc):
                            if kti >= 3:
                                g.wait_ge(pe.h, ktuse[kti - 3])
                            ktv = k1all[b * S:(b + 1) * S, :]\
                                .rearrange("s d -> (s d)")\
                                .rearrange("(dc p j) -> p dc j", dc=8, p=128)
                            dst = kt_sb[kti % 3].rearrange("p (dc j) -> p dc j", dc=8)
                            g.dma_start(out=dst, in_=ktv[:, :, j0:j0 + 512])\
                              .then_inc(ktc.h, 16)
                        ktc.n += 16
                        kfen = ktc.n

                        for jc in range(4):
                            @blk.tensor
                            def _(t, st=st, b=b, jc=jc, kti=kti, kfen=kfen, ktc=ktc,
                                  q_fence=q_fence):
                                t.wait_ge(ktc.h, kfen)
                                t.wait_ge(dv.h, q_fence)
                                t.wait_ge(ac.h, ac.n)
                                for dc in range(8):
                                    ins = t.matmul(
                                        pq[jc % 2],
                                        kt_sb[kti % 3][:, dc * 512 + jc * 128: dc * 512 + (jc + 1) * 128],
                                        q1_sb[:, dc * S + b * 512: dc * S + (b + 1) * 512],
                                        start=(dc == 0), stop=(dc == 7))
                                pe.inc(ins)
                            spe = pe.n

                            @blk.scalar
                            def _(sc, b=b, jc=jc, spe=spe, psd=prev_stripe_dv):
                                sc.wait_ge(pe.h, spe)
                                sc.wait_ge(dv.h, psd)
                                ac.inc(sc.activation(
                                    out=e_sb[:, (jc * 8 + b) * 512:(jc * 8 + b + 1) * 512],
                                    in_=pq[jc % 2], func=AF.Exp, scale=0.03125))
                        ktuse[kti] = pe.n
                    stripe_ac = ac.n

                    for jc in range(4):
                        @blk.vector
                        def _(v, jc=jc, stripe_ac=stripe_ac):
                            v.wait_ge(ac.h, stripe_ac)
                            esl = lambda b: e_sb[:, (jc * 8 + b) * 512:(jc * 8 + b + 1) * 512]
                            dv.inc(v.tensor_add(d_sb, esl(0), esl(1)))
                            for b in (2, 3, 4, 5, 6, 7):
                                v.wait_ge(dv.h, dv.n)
                                dv.inc(v.tensor_add(d_sb, d_sb, esl(b)))
                            v.wait_ge(dv.h, dv.n)
                            dv.inc(v.reciprocal(r_sb, d_sb))
                        r_fence = dv.n

                        for b in range(NB):
                            asb = a_st[b % 2]
                            asoc = aso[b % 2]

                            @blk.vector
                            def _(v, st=st, b=b, jc=jc, asb=asb, asoc=asoc,
                                  r_fence=r_fence, mfen=mfen):
                                v.wait_ge(dv.h, r_fence)
                                v.wait_ge(mkld.h, mfen)
                                v.wait_ge(gmk.h, 1)
                                v.wait_ge(asoc.h, asoc.n)
                                esl = e_sb[:, (jc * 8 + b) * 512:(jc * 8 + b + 1) * 512]
                                dv.inc(v.tensor_mul(esl, esl, r_sb))
                                v.wait_ge(dv.h, dv.n)
                                dv.inc(v.tensor_copy(asb, tiny))
                                v.wait_ge(dv.h, dv.n)
                                dv.inc(v.copy_predicated(asb,
                                                         mk_sb[:, jc * 512:(jc + 1) * 512],
                                                         esl))
                            a_dv = dv.n

                            @blk.sync
                            def _(sy, st=st, b=b, jc=jc, j0=j0, asb=asb,
                                  asoc=asoc, a_dv=a_dv):
                                sy.wait_ge(dv.h, a_dv)
                                sy.dma_start(
                                    out=a_send[st * S + b * 512 + jc * 128:
                                               st * S + b * 512 + (jc + 1) * 128, :],
                                    in_=asb).then_inc(asoc.h, 16)
                            asoc.n += 16
                    prev_stripe_dv = dv.n

                    if cc is not None:
                        @blk.gpsimd
                        def _(g, st=st, n0=aso[0].n, n1=aso[1].n):
                            g.wait_ge(aso[0].h, n0)
                            g.wait_ge(aso[1].h, n1)
                            with allow_ici_all_to_all():
                                g.collective_compute(
                                    "AllToAll", mybir.AluOpType.bypass,
                                    replica_groups=RG,
                                    ins=[a_send[st * S:(st + 1) * S, :].opt()],
                                    outs=[a_recv[st * S:(st + 1) * S, :].opt()])\
                                 .then_inc(cc.h, 1)
                        cc.n += 1

                @blk.sync
                def _(sy):
                    sy.wait_ge(aso[0].h, aso[0].n)
                    sy.wait_ge(aso[1].h, aso[1].n)
        return [(aso[0].h, aso[0].n), (aso[1].h, aso[1].n)]


def emit_l1ffn_g(nc, xn1f, ws, g1_ap, tag):
    sut, sdt, rut, rdt, rwt = ws
    NE = 18
    TT = 2048
    NTILE = 2
    NIC = 4
    with ExitStack() as es:
        sb = lambda n, sh, dt: es.enter_context(nc.sbuf_tensor(tag + n, sh, dt)).ap()
        ps = lambda n, sh, dt: es.enter_context(nc.psum_tensor(tag + n, sh, dt)).ap()
        xnt = sb("xnt", [128, 8 * TT], BF16)
        wu_sb = [sb(f"wu{i}", [128, 1024], BF16) for i in range(2)]
        wd_sb = [sb(f"wd{i}", [128, 4 * 1024], BF16) for i in range(2)]
        act_sb = sb("act", [128, 4 * TT], BF16)
        acc_sb = sb("acc", [128, 16 * 1024], F32)
        wr_sb = sb("wrs", [128, 8 * 16], BF16)
        gate_sb = sb("gate", [128, 16 * 16], F32)
        rt = sb("rt", [128, 6 * 16], F32)
        rm = sb("rm", [128, 4], F32)
        sqs = sb("sqs", [128, 1024], F32)
        gst = [sb(f"gst{i}", [128, 1024], BF16) for i in range(2)]

        pgx = [ps(f"pgx{i}", [128, 2048], F32) for i in range(2)]
        pd = [pgx[0][:, 0:512], pgx[1][:, 0:512]]
        prt = pgx[0][:, 512:1024]

        ac = Ctr(nc, tag + "ac")
        dv = Ctr(nc, tag + "dv")
        pe = Ctr(nc, tag + "pe")
        rld = Ctr(nc, tag + "rld")
        xld = Ctr(nc, tag + "xld")
        wlds = [Ctr(nc, tag + f"wld{i}") for i in range(2)]
        wdld = [Ctr(nc, tag + f"wdld{i}") for i in range(2)]
        dso = [Ctr(nc, tag + f"dso{i}") for i in range(2)]

        wd_free = {}
        epi_done = {}

        with nc.Block(tag) as blk:
            @blk.gpsimd
            def _(g):
                for hc in range(8):
                    g.dma_start(out=wr_sb[:, hc * 16:(hc + 1) * 16],
                                in_=rwt[hc * 128:(hc + 1) * 128, :]).then_inc(rld.h, 16)
            rld.n += 16 * 8

            prev_dn_pe = 0
            prev_dn_dv = 0
            pgfen = [0, 0]
            for ti in range(NTILE):
                t0 = ti * TT

                @blk.sync
                def _(sy, ti=ti, t0=t0):
                    sy.wait_ge(pe.h, pe.n)
                    for hc in range(8):
                        sy.dma_start(out=xnt[:, hc * TT:(hc + 1) * TT],
                                     in_=xn1f[hc * 128:(hc + 1) * 128, t0:t0 + TT])\
                          .then_inc(xld.h, 16)
                xld.n += 16 * 8
                xfence = xld.n

                for ts in range(16):
                    @blk.tensor
                    def _(t, ts=ts, xfence=xfence):
                        t.wait_ge(xld.h, xfence)
                        t.wait_ge(rld.h, rld.n)
                        t.wait_ge(dv.h, dv.n)
                        for hc in range(8):
                            ins = t.matmul(prt[:, 0:16],
                                           xnt[:, hc * TT + ts * 128: hc * TT + (ts + 1) * 128],
                                           wr_sb[:, hc * 16:(hc + 1) * 16],
                                           start=(hc == 0), stop=(hc == 7))
                        pe.inc(ins)
                    rpe = pe.n

                    @blk.vector
                    def _(v, ts=ts, rpe=rpe):
                        v.wait_ge(pe.h, rpe)
                        dv.inc(v.reduce_max(rm[:, 0:1], prt[:, 0:16], axis=AX))
                        v.wait_ge(dv.h, dv.n)
                        dv.inc(v.tensor_scalar(out=rt[:, 0:16], in0=prt[:, 0:16],
                                               scalar1=rm[:, 0:1], scalar2=None,
                                               op0=OP.subtract))
                    lg_dv = dv.n

                    @blk.scalar
                    def _(sc, lg_dv=lg_dv):
                        sc.wait_ge(dv.h, lg_dv)
                        sc.wait_ge(ac.h, ac.n)
                        ac.inc(sc.activation(out=rt[:, 16:32], in_=rt[:, 0:16],
                                             func=AF.Exp))
                    ex_ac = ac.n

                    @blk.vector
                    def _(v, ts=ts, ex_ac=ex_ac):
                        v.wait_ge(ac.h, ex_ac)
                        dv.inc(v.reduce_sum(rm[:, 1:2], rt[:, 16:32], axis=AX))
                        v.wait_ge(dv.h, dv.n)
                        dv.inc(v.reciprocal(rm[:, 2:3], rm[:, 1:2]))
                        v.wait_ge(dv.h, dv.n)
                        dv.inc(v.tensor_scalar(out=rt[:, 16:32], in0=rt[:, 16:32],
                                               scalar1=rm[:, 2:3], scalar2=None,
                                               op0=OP.mult))
                        v.wait_ge(dv.h, dv.n)
                        dv.inc(v.tensor_copy(rt[:, 32:48], rt[:, 16:32]))
                        for r in range(4):
                            v.wait_ge(dv.h, dv.n)
                            dv.inc(v.reduce_max(rm[:, 3:4], rt[:, 32:48], axis=AX))
                            if r < 3:
                                v.wait_ge(dv.h, dv.n)
                                dv.inc(v.tensor_scalar(out=rt[:, 48:64], in0=rt[:, 32:48],
                                                       scalar1=rm[:, 3:4], scalar2=None,
                                                       op0=OP.is_ge))
                                v.wait_ge(dv.h, dv.n)
                                dv.inc(v.tensor_sub(rt[:, 32:48], rt[:, 32:48],
                                                    rt[:, 48:64]))
                        v.wait_ge(dv.h, dv.n)
                        dv.inc(v.tensor_scalar(out=rt[:, 48:64], in0=rt[:, 16:32],
                                               scalar1=rm[:, 3:4], scalar2=None,
                                               op0=OP.is_ge))
                        v.wait_ge(dv.h, dv.n)
                        dv.inc(v.tensor_mul(gate_sb[:, ts * 16:(ts + 1) * 16],
                                            rt[:, 16:32], rt[:, 48:64]))
                gate_fence = dv.n

                for e in range(NE):
                    wut = sut[e] if e < 2 else rut[e - 2]
                    wdt = sdt[e] if e < 2 else rdt[e - 2]
                    key = ti * NE + e
                    kk = key % 2

                    @blk.gpsimd
                    def _(g, wdt=wdt, key=key, kk=kk):
                        if key >= 2:
                            g.wait_ge(dv.h, wd_free[key - 2])
                        for icc in range(4):
                            g.dma_start(out=wd_sb[kk][:, icc * 1024:(icc + 1) * 1024],
                                        in_=wdt[icc * 128:(icc + 1) * 128, :])\
                              .then_inc(wdld[kk].h, 16)
                    wdld[kk].n += 16 * 4
                    wd_fence = wdld[kk].n

                    for ic in range(NIC):
                        wguard = pgfen[ic % 2]

                        @blk.gpsimd
                        def _(g, ic=ic, wut=wut, wguard=wguard):
                            if wguard:
                                g.wait_ge(ac.h, wguard)
                            srcw = wut[:, ic * 128:(ic + 1) * 128]\
                                .rearrange("(hc p) i -> p hc i", p=128)
                            dstw = wu_sb[ic % 2].rearrange("p (hc i) -> p hc i", hc=8)
                            g.dma_start(out=dstw, in_=srcw).then_inc(wlds[ic % 2].h, 16)
                        wlds[ic % 2].n += 16
                        wfen = wlds[ic % 2].n

                        @blk.tensor
                        def _(t, e=e, ic=ic, wfen=wfen, xfence=xfence,
                              pguard=pgfen[ic % 2], pdd=prev_dn_dv, gf=gate_fence):
                            t.wait_ge(xld.h, xfence)
                            t.wait_ge(wlds[ic % 2].h, wfen)
                            if e == 0 and ic == 0:
                                t.wait_ge(dv.h, gf)
                            if pdd and ic < 2:
                                t.wait_ge(dv.h, pdd)
                            if pguard:
                                t.wait_ge(ac.h, pguard)
                            for hc in range(8):
                                for q in range(4):
                                    ins = t.matmul(
                                        pgx[ic % 2][:, q * 512:(q + 1) * 512],
                                        wu_sb[ic % 2][:, hc * 128:(hc + 1) * 128],
                                        xnt[:, hc * TT + q * 512: hc * TT + (q + 1) * 512],
                                        start=(hc == 0), stop=(hc == 7))
                            pe.inc(ins)
                        up_pe = pe.n

                        @blk.scalar
                        def _(sc, e=e, ic=ic, up_pe=up_pe, dnp=prev_dn_pe):
                            sc.wait_ge(pe.h, up_pe)
                            if dnp:
                                sc.wait_ge(pe.h, dnp)
                            ac.inc(sc.activation(out=act_sb[:, ic * TT:(ic + 1) * TT],
                                                 in_=pgx[ic % 2], func=AF.Gelu))
                        pgfen[ic % 2] = ac.n
                    act_fence = ac.n

                    dn_dv = {}
                    for ts in range(16):
                        for half in range(2):
                            j = ts * 2 + half

                            @blk.tensor
                            def _(t, ts=ts, half=half, j=j, kk=kk,
                                  act_fence=act_fence, wd_fence=wd_fence):
                                t.wait_ge(ac.h, act_fence)
                                t.wait_ge(wdld[kk].h, wd_fence)
                                if j >= 2:
                                    t.wait_ge(dv.h, dn_dv[j - 2])
                                for icc in range(4):
                                    ins = t.matmul(
                                        pd[j % 2],
                                        act_sb[:, icc * TT + ts * 128: icc * TT + (ts + 1) * 128],
                                        wd_sb[kk][:, icc * 1024 + half * 512: icc * 1024 + (half + 1) * 512],
                                        start=(icc == 0), stop=(icc == 3))
                                pe.inc(ins)
                            dn_pe = pe.n
                            acguard = ac.n if e == 0 else 0

                            @blk.vector
                            def _(v, e=e, ts=ts, half=half, j=j, dn_pe=dn_pe,
                                  gate_fence=gate_fence, acguard=acguard):
                                v.wait_ge(pe.h, dn_pe)
                                v.wait_ge(dv.h, gate_fence)
                                if acguard:
                                    v.wait_ge(ac.h, acguard)
                                asl = acc_sb[:, ts * 1024 + half * 512: ts * 1024 + (half + 1) * 512]
                                if e == 0:
                                    dv.inc(v.tensor_copy(asl, pd[j % 2]))
                                elif e == 1:
                                    dv.inc(v.scalar_tensor_tensor(
                                        out=asl, in0=pd[j % 2], scalar=1.0,
                                        in1=asl, op0=OP.mult, op1=OP.add))
                                else:
                                    dv.inc(v.scalar_tensor_tensor(
                                        out=asl, in0=pd[j % 2],
                                        scalar=gate_sb[:, ts * 16 + (e - 2): ts * 16 + e - 1],
                                        in1=asl, op0=OP.mult, op1=OP.add))
                            dn_dv[j] = dv.n
                    wd_free[key] = dv.n
                    prev_dn_pe = pe.n
                    prev_dn_dv = dv.n
                ffn_fence = dv.n

                for ts in range(16):
                    tglob = t0 + ts * 128
                    bsel = ts % 2

                    @blk.scalar
                    def _(sc, ts=ts, ffn_fence=ffn_fence):
                        sc.wait_ge(dv.h, ffn_fence)
                        sc.wait_ge(dv.h, dv.n)
                        sc.wait_ge(ac.h, ac.n)
                        ac.inc(sc.activation(out=sqs,
                                             in_=acc_sb[:, ts * 1024:(ts + 1) * 1024],
                                             func=AF.Silu))
                    sil_ac = ac.n

                    @blk.vector
                    def _(v, ts=ts, bsel=bsel, sil_ac=sil_ac):
                        v.wait_ge(ac.h, sil_ac)
                        v.wait_ge(dso[bsel].h, dso[bsel].n)
                        dv.inc(v.tensor_copy(gst[bsel], sqs))
                    g_dv = dv.n

                    @blk.sync
                    def _(sy, tglob=tglob, bsel=bsel, g_dv=g_dv):
                        sy.wait_ge(dv.h, g_dv)
                        sy.dma_start(out=g1_ap[tglob:tglob + 128, :],
                                     in_=gst[bsel]).then_inc(dso[bsel].h, 16)
                    dso[bsel].n += 16
                    epi_done[ti * 16 + ts] = dv.n

            @blk.sync
            def _(sy):
                sy.wait_ge(dso[0].h, dso[0].n)
                sy.wait_ge(dso[1].h, dso[1].n)
        return [(dso[0].h, dso[0].n), (dso[1].h, dso[1].n)]


def emit_pv_final(nc, a_recv, v1loc, g1, x1_ap, y_ap, tag, cc=None, ccbase=2):
    NJ = 32
    with ExitStack() as es:
        sb = lambda n, sh, dt: es.enter_context(nc.sbuf_tensor(tag + n, sh, dt)).ap()
        ps = lambda n, sh, dt: es.enter_context(nc.psum_tensor(tag + n, sh, dt)).ap()
        v_all = sb("vall", [128, NJ * 1024], BF16)
        a_sb = [sb(f"a{i}", [128, 512], BF16) for i in range(4)]
        gt = [sb(f"g{i}", [128, 1024], BF16) for i in range(2)]
        xt = [sb(f"x{i}", [128, 1024], F32) for i in range(2)]
        yt = [sb(f"yy{i}", [128, 1024], F32) for i in range(2)]
        ppv = [ps(f"p{i}", [128, 512], F32) for i in range(8)]

        pe = Ctr(nc, tag + "pe")
        dv = Ctr(nc, tag + "dv")
        vld = Ctr(nc, tag + "vld")
        ald = [Ctr(nc, tag + f"ald{i}") for i in range(4)]
        gld = [Ctr(nc, tag + f"gld{i}") for i in range(2)]
        yso = [Ctr(nc, tag + f"yso{i}") for i in range(2)]

        with nc.Block(tag) as blk:
            @blk.gpsimd
            def _(g):
                for j in range(NJ):
                    g.dma_start(out=v_all[:, j * 1024:(j + 1) * 1024],
                                in_=v1loc[j * 128:(j + 1) * 128, :]).then_inc(vld.h, 16)
            vld.n += 16 * NJ

            g_free = {}
            use_pe = {}
            for cw in range(8):
                ev_prev = dv.n
                for j in range(NJ):
                    idx = cw * NJ + j
                    alc = ald[idx % 4]

                    @blk.sync
                    def _(sy, cw=cw, j=j, idx=idx, alc=alc):
                        if idx >= 4:
                            sy.wait_ge(pe.h, use_pe[idx - 4])
                        st, jl = j // 4, j % 4
                        if cc is not None:
                            sy.wait_ge(cc.h, ccbase + st + 1)
                        sy.dma_start(out=a_sb[idx % 4],
                                     in_=a_recv[st * S + cw * 512 + jl * 128:
                                                st * S + cw * 512 + (jl + 1) * 128, :])\
                          .then_inc(alc.h, 16)
                    alc.n += 16
                    afen = alc.n

                    @blk.tensor
                    def _(t, cw=cw, j=j, idx=idx, afen=afen, alc=alc, ev_prev=ev_prev):
                        t.wait_ge(alc.h, afen)
                        if cw == 0 and j == 0:
                            t.wait_ge(vld.h, vld.n)
                        if j == 0:
                            t.wait_ge(dv.h, ev_prev)
                        for isub in range(4):
                            for half in range(2):
                                ins = t.matmul(
                                    ppv[isub * 2 + half],
                                    a_sb[idx % 4][:, isub * 128:(isub + 1) * 128],
                                    v_all[:, j * 1024 + half * 512:
                                          j * 1024 + (half + 1) * 512],
                                    start=(j == 0), stop=(j == NJ - 1))
                        pe.inc(ins)
                    use_pe[idx] = pe.n
                pv_pe = pe.n

                for isub in range(4):
                    bsel = isub % 2
                    idx = cw * 4 + isub
                    row0 = cw * 512 + isub * 128

                    @blk.sync
                    def _(sy, idx=idx, bsel=bsel, row0=row0):
                        if idx >= 2:
                            sy.wait_ge(dv.h, g_free[idx - 2])
                        sy.dma_start(out=gt[bsel], in_=g1[row0:row0 + 128, :])\
                          .then_inc(gld[bsel].h, 16)
                        sy.dma_start(out=xt[bsel], in_=x1_ap[row0:row0 + 128, :])\
                          .then_inc(gld[bsel].h, 16)
                    gld[bsel].n += 32
                    gfen = gld[bsel].n

                    @blk.vector
                    def _(v, isub=isub, bsel=bsel, pv_pe=pv_pe, gfen=gfen):
                        v.wait_ge(pe.h, pv_pe)
                        v.wait_ge(gld[bsel].h, gfen)
                        v.wait_ge(yso[bsel].h, yso[bsel].n)
                        for half in range(2):
                            v.wait_ge(dv.h, dv.n)
                            dv.inc(v.tensor_mul(
                                yt[bsel][:, half * 512:(half + 1) * 512],
                                ppv[isub * 2 + half],
                                gt[bsel][:, half * 512:(half + 1) * 512]))
                        v.wait_ge(dv.h, dv.n)
                        dv.inc(v.tensor_add(yt[bsel], yt[bsel], xt[bsel]))
                    g_free[idx] = dv.n
                    ydv = dv.n

                    @blk.sync
                    def _(sy, bsel=bsel, row0=row0, ydv=ydv):
                        sy.wait_ge(dv.h, ydv)
                        sy.dma_start(out=y_ap[row0:row0 + 128, :], in_=yt[bsel])\
                          .then_inc(yso[bsel].h, 16)
                    yso[bsel].n += 16

            @blk.sync
            def _(sy):
                sy.wait_ge(yso[0].h, yso[0].n)
                sy.wait_ge(yso[1].h, yso[1].n)


RG = [list(range(NCORES))]


def fence_all(nc, deps, name):
    with nc.Block(name) as blk:
        for eng in ("sync", "gpsimd", "scalar", "vector", "tensor"):
            @getattr(blk, eng)
            def _(e, deps=deps):
                for (h, n) in deps:
                    e.wait_ge(h, n)


def emit_collectives(nc, ops, deps, cc, tag):
    with nc.Block(tag) as blk:
        @blk.gpsimd
        def _(g):
            for (h, n) in deps:
                g.wait_ge(h, n)
            with allow_ici_all_to_all():
                for kind, src, dst in ops:
                    g.collective_compute(kind, mybir.AluOpType.bypass,
                                         replica_groups=RG, ins=[src.opt()],
                                         outs=[dst.opt()]).then_inc(cc.h, 1)
    cc.n += len(ops)


def build_fused():
    nc = bass.Bass("TRN2", target_bir_lowering=False)
    DT = lambda n, sh, dt, kind: nc.dram_tensor(n, sh, dt, kind=kind).ap()
    x_ap = DT("x", [S, D], F32, "ExternalInput")
    wq = DT("wq", [D, 256], F32, "ExternalInput")
    wqc = DT("wqc", [D, 256], F32, "ExternalInput")
    wk = DT("wk", [D, 256], F32, "ExternalInput")
    wkc = DT("wkc", [D, 256], F32, "ExternalInput")
    wv = DT("wv", [D, D], F32, "ExternalInput")
    wvc = DT("wvc", [D, D], F32, "ExternalInput")
    bq = DT("bq", [256, 1], F32, "ExternalInput")
    bqc = DT("bqc", [256, 1], F32, "ExternalInput")
    bk = DT("bk", [256, 1], F32, "ExternalInput")
    bkc = DT("bkc", [256, 1], F32, "ExternalInput")
    bv = DT("bv", [1, D], F32, "ExternalInput")
    bvc = DT("bvc", [1, D], F32, "ExternalInput")
    wgs = DT("wgs", [D, 2048], F32, "ExternalInput")
    wus = DT("wus", [D, 2048], F32, "ExternalInput")
    wds = DT("wds", [2048, D], F32, "ExternalInput")
    wge = DT("wge", [8, D, 2048], F32, "ExternalInput")
    wue = DT("wue", [8, D, 2048], F32, "ExternalInput")
    wde = DT("wde", [8, 2048, D], F32, "ExternalInput")
    wr = DT("wr", [D, 8], F32, "ExternalInput")
    rb = DT("rb", [1, 8], F32, "ExternalInput")
    wkt = DT("wkt", [D, D], F32, "ExternalInput")
    wvt = DT("wvt", [D, D], F32, "ExternalInput")
    wqt = DT("wqt", [D, D], F32, "ExternalInput")
    mask = DT("mask", [S, QW], I8, "ExternalInput")
    sut = DT("sut", [2, D, 512], F32, "ExternalInput")
    sdt = DT("sdt", [2, 512, D], F32, "ExternalInput")
    rut = DT("rut", [16, D, 512], F32, "ExternalInput")
    rdt = DT("rdt", [16, 512, D], F32, "ExternalInput")
    rwt = DT("rwt", [D, 16], F32, "ExternalInput")
    y_ap = DT("y", [S, D], F32, "ExternalOutput")

    xnf = DT("xnf", [D, SP], BF16, "Internal")
    s_f = DT("s_f", [D, S], F32, "Internal")
    x1 = DT("x1", [S, D], F32, "Internal")
    xn1f = DT("xn1f", [D, S], BF16, "Internal")
    xn1send = DT("xn1send", [8 * D, QW], BF16, "Internal")
    xn1q = DT("xn1q", [8 * D, QW], BF16, "Internal")
    k1loc = DT("k1loc", [S, D], BF16, "Internal")
    k1all = DT("k1all", [B * S, D], BF16, "Internal")
    v1loc = DT("v1loc", [S, D], BF16, "Internal")
    a_send = DT("a_send", [B * S, QW], BF16, "Internal")
    a_recv = DT("a_recv", [B * S, QW], BF16, "Internal")
    g1 = DT("g1", [S, D], BF16, "Internal")

    cc = Ctr(nc, "cc")

    f1 = emit_ln(nc, x_ap, xnf, "ln0", SP)
    fence_all(nc, f1, "fn1")
    f2 = emit_l0attn(nc, xnf, (wq, wqc, wk, wkc, wv, wvc),
                     (bq, bqc, bk, bkc, bv, bvc), s_f, "a0")
    fence_all(nc, f2, "fn2")
    f3 = emit_l0ffn(nc, xnf, s_f, x_ap, (wgs, wus, wds, wge, wue, wde, wr, rb),
                    x1, xn1f, "f0", xn1send_ap=xn1send)
    f4 = emit_kvproj(nc, xn1f, wkt, wvt, k1loc, v1loc, "kv")
    fence_all(nc, f3 + f4, "fn3")
    emit_collectives(nc, [("AllToAll", xn1send, xn1q),
                          ("AllGather", k1loc, k1all)], [], cc, "cc1")
    f5 = emit_l1ffn_g(nc, xn1f, (sut, sdt, rut, rdt, rwt), g1, "e2")
    fence_all(nc, f5 + [(cc.h, cc.n)], "fn4")
    f6 = emit_l1scores(nc, xn1q, wqt, k1all, mask, a_send, "e1",
                       a_recv=a_recv, cc=cc)
    fence_all(nc, f6, "fn5")
    emit_pv_final(nc, a_recv, v1loc, g1, x1, y_ap, "pf", cc=cc, ccbase=2)
    return nc


_FUSED = None


def _fused_prog():
    global _FUSED
    if _FUSED is None:
        _FUSED = build_fused()
    return _FUSED


def kernel(**inputs):
    T = lambda a: np.ascontiguousarray(np.asarray(a).swapaxes(-1, -2))
    C = lambda a: np.ascontiguousarray(np.asarray(a))
    x = np.asarray(inputs["L0_attn_x"])

    base = {
        "wq": C(inputs["L0_attn_Wq"]), "wqc": C(inputs["L0_attn_Wqc"]),
        "wk": C(inputs["L0_attn_Wk"]), "wkc": C(inputs["L0_attn_Wkc"]),
        "wv": C(inputs["L0_attn_Wv"]), "wvc": C(inputs["L0_attn_Wvc"]),
        "bq": np.asarray(inputs["L0_attn_bq"]).reshape(256, 1).copy(),
        "bqc": np.asarray(inputs["L0_attn_bqc"]).reshape(256, 1).copy(),
        "bk": np.asarray(inputs["L0_attn_bk"]).reshape(256, 1).copy(),
        "bkc": np.asarray(inputs["L0_attn_bkc"]).reshape(256, 1).copy(),
        "bv": np.asarray(inputs["L0_attn_bv"]).reshape(1, D).copy(),
        "bvc": np.asarray(inputs["L0_attn_bvc"]).reshape(1, D).copy(),
        "wgs": T(inputs["L0_ffn_Wg_s"]), "wus": T(inputs["L0_ffn_Wu_s"]),
        "wds": T(inputs["L0_ffn_Wd_s"]),
        "wge": T(inputs["L0_ffn_Wg"]), "wue": T(inputs["L0_ffn_Wu"]),
        "wde": T(inputs["L0_ffn_Wd"]),
        "wr": T(inputs["L0_ffn_Wr"]),
        "rb": np.asarray(inputs["L0_ffn_rb"]).reshape(1, 8).copy(),
        "wkt": T(inputs["L1_attn_Wk"]), "wvt": T(inputs["L1_attn_Wv"]),
        "wqt": T(inputs["L1_attn_Wq"]),
        "sut": T(inputs["L1_ffn_shared_up"]), "sdt": T(inputs["L1_ffn_shared_down"]),
        "rut": T(inputs["L1_ffn_routed_up"]), "rdt": T(inputs["L1_ffn_routed_down"]),
        "rwt": T(inputs["L1_ffn_router_w"]),
    }
    jj = np.arange(S)[:, None]
    ii = np.arange(QW)[None, :]
    in_maps = []
    for c in range(NCORES):
        maskc = (jj <= c * QW + ii).astype(np.int8)
        in_maps.append(dict(base, x=C(x[c]), mask=maskc))
    res = run_bass_kernel_spmd(_fused_prog(), in_maps, core_ids=list(range(NCORES)))
    return np.stack([res.results[c]["y"] for c in range(NCORES)])


# version 08
# speedup vs baseline: 1.1293x; 1.1293x over previous
import numpy as np
import ml_dtypes
from contextlib import ExitStack

import concourse.bass as bass
import concourse.mybir as mybir
from concourse.bass_utils import run_bass_kernel_spmd
from concourse.aot_env import allow_ici_all_to_all

F32 = mybir.dt.float32
BF16 = mybir.dt.bfloat16
I8 = mybir.dt.int8
AF = mybir.ActivationFunctionType
OP = mybir.AluOpType
AX = mybir.AxisListType.X

B = 8
S = 4096
SP = 4112
D = 1024
NCORES = 8
QW = S // NCORES
LN_EPS = 1e-5


class Ctr:
    def __init__(self, nc, name):
        self.h = nc.alloc_semaphore(name=name)
        self.n = 0

    def inc(self, inst, k=1):
        inst.then_inc(self.h, k)
        self.n += k
        return self.n


def _mk_identity(g, ap, ctr):
    ctr.inc(g.memset(ap, 0.0))
    g.wait_ge(ctr.h, ctr.n)
    return g.affine_select(out=ap, in_=ap, compare_op=OP.not_equal, fill=1.0,
                           base=0, pattern=[[-1, ap.shape[1]]], channel_multiplier=1)


def build_ln0():
    nc = bass.Bass("TRN2", target_bir_lowering=False)
    x_ap = nc.dram_tensor("x", [S, D], F32, kind="ExternalInput").ap()
    xnf_ap = nc.dram_tensor("xnf", [D, SP], BF16, kind="ExternalOutput").ap()
    emit_ln(nc, x_ap, xnf_ap, "ln0", SP)
    return nc


def emit_ln(nc, x_ap, xnf_ap, tag, tcols):
    NT = S // 128
    with ExitStack() as es:
        sb = lambda n, sh, dt: es.enter_context(nc.sbuf_tensor(tag + n, sh, dt)).ap()
        ps = lambda n, sh, dt: es.enter_context(nc.psum_tensor(tag + n, sh, dt)).ap()
        xt = [sb(f"xt{i}", [128, D], F32) for i in range(3)]
        sq = sb("sq", [128, D], F32)
        stt = [sb(f"stt{i}", [128, 8], F32) for i in range(3)]
        epst = sb("epst", [128, 1], F32)
        xn = [sb(f"xn{i}", [128, D], BF16) for i in range(2)]
        ptr = [ps(f"ptr{i}", [128, D], BF16) for i in range(2)]
        stw = [sb(f"stw{i}", [128, 8 * 1024], BF16) for i in range(2)]
        zpad = sb("zpad", [128, 16], BF16)
        ident = sb("id", [128, 128], BF16)

        lds = [Ctr(nc, tag + f"ld{i}") for i in range(3)]
        ac = Ctr(nc, tag + "ac")
        dv = Ctr(nc, tag + "dv")
        pe = Ctr(nc, tag + "pe")
        ev = Ctr(nc, tag + "ev")
        sos = [Ctr(nc, tag + f"so{i}") for i in range(2)]
        sop = Ctr(nc, tag + "sop")
        gi_ = Ctr(nc, tag + "gi")

        xn_done = {}
        stage_done = {}
        tr_fence = {}
        out_done_grp = {}

        with nc.Block(tag) as blk:
            @blk.gpsimd
            def _(g):
                g.memset(epst, LN_EPS)
                g.memset(zpad, 0.0)
                _mk_identity(g, ident, gi_).then_inc(gi_.h, 1)
            gi_.n += 1
            gi_all = gi_.n

            if tcols > S:
                @blk.sync
                def _(sy):
                    sy.wait_ge(gi_.h, gi_all)
                    for j in range(8):
                        sy.dma_start(out=xnf_ap[j * 128:(j + 1) * 128, S:tcols],
                                     in_=zpad).then_inc(sop.h, 16)
                sop.n += 16 * 8

            for i in range(NT):
                ldc = lds[i % 3]

                @blk.sync
                def _(sy, i=i, ldc=ldc):
                    if i >= 3:
                        sy.wait_ge(dv.h, xn_done[i - 3])
                    sy.dma_start(out=xt[i % 3],
                                 in_=x_ap[i * 128:(i + 1) * 128, :]).then_inc(ldc.h, 16)
                ldc.n += 16
                ld_i = ldc.n
                b = stt[i % 3]

                @blk.scalar
                def _(sc, i=i, b=b, ld_i=ld_i, ldc=ldc):
                    sc.wait_ge(ldc.h, ld_i)
                    if i >= 3:
                        sc.wait_ge(dv.h, xn_done[i - 3])
                    ac.inc(sc.activation(out=sq, in_=xt[i % 3], func=AF.Square,
                                         accum_out=b[:, 1:2]))
                s2_ready = ac.n

                @blk.vector
                def _(v, i=i, b=b, ld_i=ld_i, s2=s2_ready, ldc=ldc):
                    v.wait_ge(ldc.h, ld_i)
                    if i >= 3:
                        v.wait_ge(dv.h, xn_done[i - 3])
                    dv.inc(v.reduce_sum(b[:, 0:1], xt[i % 3], axis=AX))
                    v.wait_ge(dv.h, dv.n)
                    dv.inc(v.tensor_scalar(out=b[:, 2:3], in0=b[:, 0:1],
                                           scalar1=1.0 / D, scalar2=None,
                                           op0=OP.mult))
                    v.wait_ge(dv.h, dv.n)
                    dv.inc(v.scalar_tensor_tensor(out=b[:, 5:6], in0=b[:, 2:3],
                                                  scalar=1.0, in1=b[:, 2:3],
                                                  op0=OP.mult, op1=OP.mult))
                    v.wait_ge(ac.h, s2)
                    v.wait_ge(dv.h, dv.n)
                    dv.inc(v.scalar_tensor_tensor(out=b[:, 3:4], in0=b[:, 1:2],
                                                  scalar=1.0 / D, in1=b[:, 5:6],
                                                  op0=OP.mult, op1=OP.subtract))
                var_ready = dv.n

                @blk.scalar
                def _(sc, b=b, var_ready=var_ready):
                    sc.wait_ge(dv.h, var_ready)
                    ac.inc(sc.activation(out=b[:, 4:5], in_=b[:, 3:4], func=AF.Sqrt,
                                         bias=epst[:, 0:1]))
                std_ready = ac.n

                @blk.vector
                def _(v, i=i, b=b, std_ready=std_ready):
                    v.wait_ge(ac.h, std_ready)
                    dv.inc(v.reciprocal(b[:, 5:6], b[:, 4:5]))
                    v.wait_ge(dv.h, dv.n)
                    if i >= 2:
                        v.wait_ge(pe.h, tr_fence[i - 2])
                    dv.inc(v.tensor_scalar(out=xn[i % 2], in0=xt[i % 3],
                                           scalar1=b[:, 2:3], scalar2=b[:, 5:6],
                                           op0=OP.subtract, op1=OP.mult))
                xn_done[i] = dv.n
                xnr = dv.n

                @blk.tensor
                def _(t, i=i, xnr=xnr):
                    t.wait_ge(gi_.h, gi_all)
                    t.wait_ge(dv.h, xnr)
                    if i >= 2:
                        t.wait_ge(ev.h, stage_done[i - 2])
                    for j in range(8):
                        ins = t.transpose(ptr[i % 2][:, j * 128:(j + 1) * 128],
                                          xn[i % 2][:, j * 128:(j + 1) * 128], ident)
                    pe.inc(ins)
                tr_done = pe.n
                tr_fence[i] = tr_done
                g = i // 8
                ti = i % 8

                @blk.vector
                def _(v, i=i, g=g, ti=ti, tr=tr_done):
                    v.wait_ge(pe.h, tr)
                    if g >= 2 and ti == 0:
                        v.wait_ge(sos[g % 2].h, out_done_grp[g - 2])
                    ev.inc(v.tensor_copy(
                        stw[g % 2].rearrange("p (j tb t) -> p j tb t",
                                             j=8, tb=8)[:, :, ti, :],
                        ptr[i % 2].rearrange("p (j t) -> p j t", j=8)))
                stage_done[i] = ev.n
                stg = ev.n

                if ti == 7:
                    @blk.sync
                    def _(sy, g=g, stg=stg):
                        sy.wait_ge(ev.h, stg)
                        for j in range(8):
                            sy.dma_start(
                                out=xnf_ap[j * 128:(j + 1) * 128,
                                           g * 1024:(g + 1) * 1024],
                                in_=stw[g % 2][:, j * 1024:(j + 1) * 1024])\
                              .then_inc(sos[g % 2].h, 16)
                    sos[g % 2].n += 16 * 8
                    out_done_grp[g] = sos[g % 2].n

            @blk.sync
            def _(sy):
                sy.wait_ge(sos[0].h, sos[0].n)
                sy.wait_ge(sos[1].h, sos[1].n)
                if tcols > S:
                    sy.wait_ge(sop.h, sop.n)
        fences = [(sos[0].h, sos[0].n), (sos[1].h, sos[1].n)]
        if tcols > S:
            fences.append((sop.h, sop.n))
        return fences


def build_l0attn(debug_qkv=False):
    nc = bass.Bass("TRN2", target_bir_lowering=False)
    xnf = nc.dram_tensor("xnf", [D, SP], BF16, kind="ExternalInput").ap()
    wq = nc.dram_tensor("wq", [D, 256], F32, kind="ExternalInput").ap()
    wqc = nc.dram_tensor("wqc", [D, 256], F32, kind="ExternalInput").ap()
    wk = nc.dram_tensor("wk", [D, 256], F32, kind="ExternalInput").ap()
    wkc = nc.dram_tensor("wkc", [D, 256], F32, kind="ExternalInput").ap()
    wv = nc.dram_tensor("wv", [D, D], F32, kind="ExternalInput").ap()
    wvc = nc.dram_tensor("wvc", [D, D], F32, kind="ExternalInput").ap()
    bq = nc.dram_tensor("bq", [256, 1], F32, kind="ExternalInput").ap()
    bqc = nc.dram_tensor("bqc", [256, 1], F32, kind="ExternalInput").ap()
    bk = nc.dram_tensor("bk", [256, 1], F32, kind="ExternalInput").ap()
    bkc = nc.dram_tensor("bkc", [256, 1], F32, kind="ExternalInput").ap()
    bv = nc.dram_tensor("bv", [1, D], F32, kind="ExternalInput").ap()
    bvc = nc.dram_tensor("bvc", [1, D], F32, kind="ExternalInput").ap()
    s_f = nc.dram_tensor("s_f", [D, S], F32, kind="ExternalOutput").ap()
    dbg = None
    if debug_qkv:
        dbg = (nc.dram_tensor("dbg_q", [128, 2 * S], BF16, kind="ExternalOutput").ap(),
               nc.dram_tensor("dbg_k", [128, 2 * SP], BF16, kind="ExternalOutput").ap(),
               nc.dram_tensor("dbg_v", [128, 33 * 1040], BF16, kind="ExternalOutput").ap())
    emit_l0attn(nc, xnf, (wq, wqc, wk, wkc, wv, wvc),
                (bq, bqc, bk, bkc, bv, bvc), s_f, "a0", dbg=dbg)
    return nc


def emit_l0attn(nc, xnf, ws, bs, s_f, tag, dbg=None):
    wq, wqc, wk, wkc, wv, wvc = ws
    bq, bqc, bk, bkc, bv, bvc = bs
    NH = 16
    QL = 16
    VL = 64
    NKC = SP // 128
    KCW = [128] * 32 + [16]
    GRP3 = [(3 * g, min(3, 33 - 3 * g)) for g in range((33 + 2) // 3)]

    with ExitStack() as es:
        sb = lambda n, sh, dt: es.enter_context(nc.sbuf_tensor(tag + n, sh, dt)).ap()
        ps = lambda n, sh, dt: es.enter_context(nc.psum_tensor(tag + n, sh, dt)).ap()
        q_sb = sb("q", [128, 2 * S], BF16)
        k_sb = sb("k", [128, 2 * SP], BF16)
        v_sb = sb("v", [128, 33 * 1040], BF16)

        ac = Ctr(nc, tag + "ac")
        dv = Ctr(nc, tag + "dv")
        pe = Ctr(nc, tag + "pe")
        so = Ctr(nc, tag + "so")

        with ExitStack() as es2:
            sb2 = lambda n, sh, dt: es2.enter_context(nc.sbuf_tensor(tag + n, sh, dt)).ap()
            xnc = [sb2(f"xnc{i}", [128, 8 * 512], BF16) for i in range(2)]
            wq_sb = sb2("wqs", [128, 8 * 256], BF16)
            wqc_sb = sb2("wqcs", [128, 8 * 256], BF16)
            wk_sb = sb2("wks", [128, 8 * 256], BF16)
            wkc_sb = sb2("wkcs", [128, 8 * 256], BF16)
            wv_sb = sb2("wvs", [128, 8 * 1024], BF16)
            wvc_sb = sb2("wvcs", [128, 8 * 1024], BF16)
            bq_sb = sb2("bqs", [128, 2], F32)
            bqc_sb = sb2("bqcs", [128, 2], F32)
            bk_sb = sb2("bks", [128, 2], F32)
            bkc_sb = sb2("bkcs", [128, 2], F32)
            brow = sb2("brow", [1, 2 * D], F32)
            ones1 = sb2("ones1", [1, 128], F32)
            bvb = sb2("bvb", [128, D], F32)
            bvcb = sb2("bvcb", [128, D], F32)
            tmp1 = sb2("tmp1", [128, 512], F32)
            tmp2 = sb2("tmp2", [128, 512], F32)
            tmp3 = sb2("tmp3", [128, 512], F32)
            ps2 = lambda n, sh, dt: es2.enter_context(nc.psum_tensor(tag + n, sh, dt)).ap()
            pql = ps2("pql", [128, 512], F32)
            pqg = ps2("pqg", [128, 512], F32)
            pvl = ps2("pvl", [128, 512], F32)
            pvg = ps2("pvg", [128, 512], F32)
            pbc = ps2("pbc", [128, 1024], F32)

            wgld = Ctr(nc, tag + "wgld")
            gmk = Ctr(nc, tag + "gmk")

            @_one_block(nc, tag + "prep")
            def _(blk):

                @blk.gpsimd
                def _(g):
                    g.memset(ones1, 1.0).then_inc(gmk.h, 1)
                    g.memset(v_sb[:, :], 0.0).then_inc(gmk.h, 1)
                    g.wait_ge(gmk.h, 2)
                    for kc in range(33):
                        ap = v_sb[:, kc * 1040:(kc + 1) * 1040]\
                            .rearrange("p (h c) -> p h c", h=16, c=65)[:, :, 64:65]
                        ins = g.memset(ap, 1.0)
                    ins.then_inc(gmk.h, 1)
                    for hc in range(8):
                        g.dma_start(out=wq_sb[:, hc * 256:(hc + 1) * 256],
                                    in_=wq[hc * 128:(hc + 1) * 128, :]).then_inc(wgld.h, 16)
                        g.dma_start(out=wqc_sb[:, hc * 256:(hc + 1) * 256],
                                    in_=wqc[hc * 128:(hc + 1) * 128, :]).then_inc(wgld.h, 16)
                        g.dma_start(out=wk_sb[:, hc * 256:(hc + 1) * 256],
                                    in_=wk[hc * 128:(hc + 1) * 128, :]).then_inc(wgld.h, 16)
                        g.dma_start(out=wkc_sb[:, hc * 256:(hc + 1) * 256],
                                    in_=wkc[hc * 128:(hc + 1) * 128, :]).then_inc(wgld.h, 16)
                        g.dma_start(out=wv_sb[:, hc * 1024:(hc + 1) * 1024],
                                    in_=wv[hc * 128:(hc + 1) * 128, :]).then_inc(wgld.h, 16)
                        g.dma_start(out=wvc_sb[:, hc * 1024:(hc + 1) * 1024],
                                    in_=wvc[hc * 128:(hc + 1) * 128, :]).then_inc(wgld.h, 16)
                wgld.n += 16 * 48
                gmk.n += 3

                bld = Ctr(nc, tag + "bld")

                @blk.sync
                def _(sy):
                    for name, srcw, dst in [("bq", bq, bq_sb), ("bqc", bqc, bqc_sb),
                                            ("bk", bk, bk_sb), ("bkc", bkc, bkc_sb)]:
                        for oc in range(2):
                            sy.dma_start(out=dst[:, oc:oc + 1],
                                         in_=srcw[oc * 128:(oc + 1) * 128, :])\
                              .then_inc(bld.h, 16)
                    sy.dma_start(out=brow[0:1, 0:D], in_=bv).then_inc(bld.h, 16)
                    sy.dma_start(out=brow[0:1, D:2 * D], in_=bvc).then_inc(bld.h, 16)
                bld.n += 16 * 10
                wld = bld.n

                @blk.tensor
                def _(t, wld=wld):
                    t.wait_ge(bld.h, wld)
                    t.wait_ge(gmk.h, 1)
                    t.matmul(pbc[:, 0:512], ones1[0:1, :], brow[0:1, 0:512],
                             start=True, stop=True)
                    pe.inc(t.matmul(pbc[:, 512:1024], ones1[0:1, :],
                                    brow[0:1, 512:1024], start=True, stop=True))
                bv_pe = pe.n

                @blk.vector
                def _(v, bv_pe=bv_pe):
                    v.wait_ge(pe.h, bv_pe)
                    dv.inc(v.tensor_copy(bvb, pbc[:, 0:1024]))
                bvb_dv = dv.n

                @blk.tensor
                def _(t, bvb_dv=bvb_dv):
                    t.wait_ge(dv.h, bvb_dv)
                    t.matmul(pbc[:, 0:512], ones1[0:1, :], brow[0:1, D:D + 512],
                             start=True, stop=True)
                    pe.inc(t.matmul(pbc[:, 512:1024], ones1[0:1, :],
                                    brow[0:1, D + 512:2 * D], start=True, stop=True))
                bvc_pe = pe.n

                @blk.vector
                def _(v, bvc_pe=bvc_pe):
                    v.wait_ge(pe.h, bvc_pe)
                    dv.inc(v.tensor_copy(bvcb, pbc[:, 0:1024]))
                bvcb_dv = dv.n

                NTT = 9
                xlds = [Ctr(nc, tag + f"xld{i}") for i in range(2)]
                evac_done = {}
                for tt in range(NTT):
                    t0 = tt * 512
                    tw = min(512, SP - t0)
                    xldc = xlds[tt % 2]

                    @blk.sync
                    def _(sy, tt=tt, t0=t0, tw=tw, xldc=xldc):
                        if tt >= 2:
                            sy.wait_ge(dv.h, evac_done[tt - 2])
                        for hc in range(8):
                            sy.dma_start(out=xnc[tt % 2][:, hc * 512:hc * 512 + tw],
                                         in_=xnf[hc * 128:(hc + 1) * 128, t0:t0 + tw])\
                              .then_inc(xldc.h, 16)
                    xldc.n += 16 * 8
                    xl = xldc.n

                    for kind, wlin, wgate, blin, bgate, dst, dw in (
                            [("q", wq_sb, wqc_sb, bq_sb, bqc_sb, q_sb, S)] if tt < 8 else []
                    ) + [("k", wk_sb, wkc_sb, bk_sb, bkc_sb, k_sb, SP)]:
                        for oc in range(2):
                            @blk.tensor
                            def _(t, tt=tt, tw=tw, oc=oc, wlin=wlin, wgate=wgate, xl=xl, xldc=xldc):
                                t.wait_ge(xldc.h, xl)
                                t.wait_ge(wgld.h, wgld.n)
                                t.wait_ge(ac.h, ac.n)
                                t.wait_ge(dv.h, dv.n)
                                for hc in range(8):
                                    t.matmul(pql[:, 0:tw],
                                             wlin[:, hc * 256 + oc * 128: hc * 256 + (oc + 1) * 128],
                                             xnc[tt % 2][:, hc * 512:hc * 512 + tw],
                                             start=(hc == 0), stop=(hc == 7))
                                for hc in range(8):
                                    ins = t.matmul(pqg[:, 0:tw],
                                                   wgate[:, hc * 256 + oc * 128: hc * 256 + (oc + 1) * 128],
                                                   xnc[tt % 2][:, hc * 512:hc * 512 + tw],
                                                   start=(hc == 0), stop=(hc == 7))
                                pe.inc(ins)
                            mm = pe.n

                            @blk.scalar
                            def _(sc, tw=tw, oc=oc, bgate=bgate, mm=mm):
                                sc.wait_ge(pe.h, mm)
                                sc.wait_ge(dv.h, dv.n)
                                ac.inc(sc.activation(out=tmp2[:, 0:tw], in_=pqg[:, 0:tw],
                                                     func=AF.Sigmoid,
                                                     bias=bgate[:, oc:oc + 1]))
                            gq = ac.n

                            @blk.vector
                            def _(v, t0=t0, tw=tw, oc=oc, blin=blin, dst=dst, dw=dw, gq=gq):
                                v.wait_ge(ac.h, gq)
                                dv.inc(v.scalar_tensor_tensor(
                                    out=dst[:, oc * dw + t0: oc * dw + t0 + tw],
                                    in0=pql[:, 0:tw], scalar=blin[:, oc:oc + 1],
                                    in1=tmp2[:, 0:tw], op0=OP.add, op1=OP.mult))

                    nsub = (tw + 127) // 128
                    for su in range(nsub):
                        stw = min(128, tw - su * 128)
                        kc = (t0 + su * 128) // 128
                        for half in range(2):
                            @blk.tensor
                            def _(t, tt=tt, su=su, stw=stw, half=half, xl=xl, xldc=xldc):
                                t.wait_ge(xldc.h, xl)
                                t.wait_ge(wgld.h, wgld.n)
                                t.wait_ge(dv.h, dv.n)
                                for hc in range(8):
                                    t.matmul(pvl[0:stw, :],
                                             xnc[tt % 2][:, hc * 512 + su * 128: hc * 512 + su * 128 + stw],
                                             wv_sb[:, hc * 1024 + half * 512: hc * 1024 + (half + 1) * 512],
                                             start=(hc == 0), stop=(hc == 7))
                                    ins = t.matmul(pvg[0:stw, :],
                                                   xnc[tt % 2][:, hc * 512 + su * 128: hc * 512 + su * 128 + stw],
                                                   wvc_sb[:, hc * 1024 + half * 512: hc * 1024 + (half + 1) * 512],
                                                   start=(hc == 0), stop=(hc == 7))
                                pe.inc(ins)
                            vm = pe.n

                            @blk.vector
                            def _(v, stw=stw, half=half, vm=vm):
                                v.wait_ge(pe.h, vm)
                                dv.inc(v.tensor_add(tmp1[0:stw, :], pvg[0:stw, :],
                                                    bvcb[0:stw, half * 512:(half + 1) * 512]))
                            g1 = dv.n

                            @blk.scalar
                            def _(sc, stw=stw, g1=g1):
                                sc.wait_ge(dv.h, dv.n)
                                ac.inc(sc.activation(out=tmp2[0:stw, :], in_=tmp1[0:stw, :],
                                                     func=AF.Sigmoid))
                            g2 = ac.n

                            @blk.vector
                            def _(v, stw=stw, half=half, kc=kc, g2=g2):
                                v.wait_ge(gmk.h, 3)
                                dv.inc(v.tensor_add(tmp3[0:stw, :], pvl[0:stw, :],
                                                    bvb[0:stw, half * 512:(half + 1) * 512]))
                                v.wait_ge(ac.h, g2)
                                v.wait_ge(dv.h, dv.n)
                                dst = v_sb[0:stw, kc * 1040:(kc + 1) * 1040]\
                                    .rearrange("p (h c) -> p h c", h=16, c=65)\
                                    [:, half * 8:(half + 1) * 8, 0:64]
                                src = tmp3[0:stw, :].rearrange("p (h c) -> p h c", h=8, c=64)
                                dv.inc(v.tensor_mul(dst, src,
                                    tmp2[0:stw, :].rearrange("p (h c) -> p h c", h=8, c=64)))
                    evac_done[tt] = dv.n

        if dbg is not None:
            dso = Ctr(nc, tag + "dso")
            with nc.Block(tag + "dbg") as blk:
                @blk.sync
                def _(sy):
                    sy.dma_start(out=dbg[0], in_=q_sb[:, :]).then_inc(dso.h, 16)
                    sy.dma_start(out=dbg[1], in_=k_sb[:, :]).then_inc(dso.h, 16)
                    sy.dma_start(out=dbg[2], in_=v_sb[:, :]).then_inc(dso.h, 16)
                    sy.wait_ge(dso.h, 48)
            return

        prep_dv = dv.n
        prep_ac = ac.n

        with ExitStack() as es3:
            sb3 = lambda n, sh, dt: es3.enter_context(nc.sbuf_tensor(tag + n, sh, dt)).ap()
            krep = sb3("krep", [128, SP], BF16)
            qrep = sb3("qrep", [128, S], BF16)
            e_sb = [sb3(f"e{i}", [128, 1536], BF16) for i in range(2)]
            den = sb3("den", [1, 512], F32)
            rden = sb3("rden", [1, 512], F32)
            bc_sb = sb3("bc", [64, 512], F32)
            sstg = sb3("sstg", [64, 512], F32)
            ones64 = sb3("o64", [1, 64], F32)
            ps3 = lambda n, sh, dt: es3.enter_context(nc.psum_tensor(tag + n, sh, dt)).ap()
            ppa = ps3("ppa", [128, 1536], F32)
            ppb = ps3("ppb", [128, 1536], F32)
            ppv = ps3("ppv", [65, 512], F32)
            pbc2 = ps3("pbc2", [64, 512], F32)
            pp = [ppa, ppb]

            rep = Ctr(nc, tag + "rep")

            @_one_block(nc, tag + "main")
            def _(blk):
                mk = Ctr(nc, tag + "mk")

                @blk.gpsimd
                def _(g):
                    g.memset(ones64, 1.0).then_inc(mk.h, 1)
                mk.n += 1

                for h in range(NH):
                    oc, row = h // 8, (h % 8) * 16

                    @blk.gpsimd
                    def _(g, oc=oc, row=row):
                        g.wait_ge(dv.h, prep_dv)
                        g.wait_ge(so.h, so.n)
                        for i in range(3):
                            g.dma_start(out=krep[32 * i:32 * i + 16, :],
                                        in_=k_sb[row:row + 16, oc * SP:(oc + 1) * SP])\
                                .then_inc(rep.h, 16)
                            g.dma_start(out=qrep[32 * i:32 * i + 16, :],
                                        in_=q_sb[row:row + 16, oc * S:(oc + 1) * S])\
                                .then_inc(rep.h, 16)
                    rep.n += 16 * 6
                    repd = rep.n

                    for tq in range(8):
                        ngrp = len(GRP3)
                        exp_done = {}
                        pv_done_grp = {}
                        qk_pe = {}
                        for gi in range(ngrp + 1):
                            if gi < ngrp:
                                c0, gw = GRP3[gi]

                                @blk.tensor
                                def _(t, h=h, tq=tq, gi=gi, c0=c0, gw=gw, repd=repd):
                                    t.wait_ge(rep.h, repd)
                                    if gi >= 2:
                                        t.wait_ge(ac.h, exp_done[gi - 2])
                                    for i in range(gw):
                                        kc = c0 + i
                                        kw = KCW[kc]
                                        ins = t.matmul(
                                            pp[gi % 2][0:kw, i * 512:(i + 1) * 512],
                                            krep[32 * i:32 * i + 16, kc * 128:kc * 128 + kw],
                                            qrep[32 * i:32 * i + 16, tq * 512:(tq + 1) * 512],
                                            start=True, stop=True,
                                            tile_position=(32 * i, 0))
                                    pe.inc(ins)
                                qk_pe[gi] = pe.n

                            if gi >= 1:
                                gj = gi - 1
                                c0j, gwj = GRP3[gj]

                                @blk.tensor
                                def _(t, h=h, gj=gj, c0j=c0j, gwj=gwj):
                                    t.wait_ge(ac.h, exp_done[gj])
                                    if gj == 0:
                                        t.wait_ge(gmk.h, 3)
                                        t.wait_ge(dv.h, dv.n)
                                    for i in range(gwj):
                                        kc = c0j + i
                                        kw = KCW[kc]
                                        ins = t.matmul(
                                            ppv[:, :],
                                            v_sb[0:kw, kc * 1040 + 65 * h: kc * 1040 + 65 * h + 65],
                                            e_sb[gj % 2][0:kw, i * 512:(i + 1) * 512],
                                            start=(gj == 0 and i == 0),
                                            stop=(gj == ngrp - 1 and i == gwj - 1))
                                    pe.inc(ins)
                                pv_done_grp[gj] = pe.n

                            if gi < ngrp:
                                c0, gw = GRP3[gi]
                                qk = qk_pe[gi]

                                @blk.scalar
                                def _(sc, gi=gi, c0=c0, gw=gw, qk=qk):
                                    sc.wait_ge(pe.h, qk)
                                    if gi >= 2:
                                        sc.wait_ge(pe.h, pv_done_grp.get(gi - 2, 0))
                                    ac.inc(sc.activation(out=e_sb[gi % 2][:, 0:gw * 512],
                                                         in_=pp[gi % 2][:, 0:gw * 512],
                                                         func=AF.Exp, scale=0.125))
                                exp_done[gi] = ac.n
                        pvd = pe.n

                        sworn = so.n
                        @blk.vector
                        def _(v, pvd=pvd, sworn=sworn):
                            v.wait_ge(pe.h, pvd)
                            v.wait_ge(so.h, sworn)
                            dv.inc(v.tensor_copy(den, ppv[64:65, :]))
                            v.wait_ge(dv.h, dv.n)
                            dv.inc(v.reciprocal(rden, den))
                        rd = dv.n

                        @blk.tensor
                        def _(t, rd=rd):
                            t.wait_ge(mk.h, 1)
                            t.wait_ge(dv.h, rd)
                            pe.inc(t.matmul(pbc2, ones64, rden, start=True, stop=True))
                        bcd = pe.n

                        @blk.vector
                        def _(v, bcd=bcd):
                            v.wait_ge(pe.h, bcd)
                            dv.inc(v.tensor_copy(bc_sb, pbc2))
                            v.wait_ge(dv.h, dv.n)
                            dv.inc(v.tensor_mul(sstg, ppv[0:64, :], bc_sb))
                        sd = dv.n

                        @blk.sync
                        def _(sy, h=h, tq=tq, sd=sd):
                            sy.wait_ge(dv.h, sd)
                            sy.dma_start(out=s_f[h * 64:(h + 1) * 64,
                                                 tq * 512:(tq + 1) * 512],
                                         in_=sstg).then_inc(so.h, 16)
                        so.n += 16

                @blk.sync
                def _(sy):
                    sy.wait_ge(so.h, so.n)
    return [(so.h, so.n)]


def build_l0ffn():
    nc = bass.Bass("TRN2", target_bir_lowering=False)
    xnf = nc.dram_tensor("xnf", [D, SP], BF16, kind="ExternalInput").ap()
    s_f = nc.dram_tensor("s_f", [D, S], F32, kind="ExternalInput").ap()
    x_ap = nc.dram_tensor("x", [S, D], F32, kind="ExternalInput").ap()
    wgs = nc.dram_tensor("wgs", [D, 2048], F32, kind="ExternalInput").ap()
    wus = nc.dram_tensor("wus", [D, 2048], F32, kind="ExternalInput").ap()
    wds = nc.dram_tensor("wds", [2048, D], F32, kind="ExternalInput").ap()
    wge = nc.dram_tensor("wge", [8, D, 2048], F32, kind="ExternalInput").ap()
    wue = nc.dram_tensor("wue", [8, D, 2048], F32, kind="ExternalInput").ap()
    wde = nc.dram_tensor("wde", [8, 2048, D], F32, kind="ExternalInput").ap()
    wr = nc.dram_tensor("wr", [D, 8], F32, kind="ExternalInput").ap()
    rb = nc.dram_tensor("rb", [1, 8], F32, kind="ExternalInput").ap()
    wkt = nc.dram_tensor("wkt", [D, D], F32, kind="ExternalInput").ap()
    wvt = nc.dram_tensor("wvt", [D, D], F32, kind="ExternalInput").ap()
    x1_ap = nc.dram_tensor("x1", [S, D], F32, kind="ExternalOutput").ap()
    xn1f_ap = nc.dram_tensor("xn1f", [D, S], BF16, kind="ExternalOutput").ap()
    k1_ap = nc.dram_tensor("k1", [S, D], BF16, kind="ExternalOutput").ap()
    v1_ap = nc.dram_tensor("v1", [S, D], BF16, kind="ExternalOutput").ap()
    emit_l0ffn(nc, xnf, s_f, x_ap, (wgs, wus, wds, wge, wue, wde, wr, rb),
               x1_ap, xn1f_ap, "f0")
    emit_kvproj(nc, xn1f_ap, wkt, wvt, k1_ap, v1_ap, "kv")
    return nc


def emit_l0ffn(nc, xnf, s_f, x_ap, ws, x1_ap, xn1f_ap, tag, xn1send_ap=None):
    wgs, wus, wds, wge, wue, wde, wr, rb = ws
    NTILE = 4
    TT = 1024
    NE = 9
    NIC = 16
    with ExitStack() as es:
        sb = lambda n, sh, dt: es.enter_context(nc.sbuf_tensor(tag + n, sh, dt)).ap()
        ps = lambda n, sh, dt: es.enter_context(nc.psum_tensor(tag + n, sh, dt)).ap()
        xnt = sb("xnt", [128, 8 * TT], BF16)
        wg_sb = [sb(f"wg{i}", [128, 1024], BF16) for i in range(2)]
        wu_sb = [sb(f"wu{i}", [128, 1024], BF16) for i in range(2)]
        wd_sb = [sb(f"wd{i}", [128, 16 * 1024], BF16) for i in range(2)]
        act_sb = sb("act", [128, 16 * TT], BF16)
        acc_sb = sb("acc", [128, 8 * 1024], F32)
        sg_sb = [sb(f"sg{i}", [128, 1024], F32) for i in range(2)]
        wr_sb = sb("wrs", [128, 8 * 8], BF16)
        rbb = sb("rbb", [128, 8], F32)
        rrow = sb("rrow", [1, 8], F32)
        ones1 = sb("ones1", [1, 128], F32)
        cw_sb = sb("cw", [128, 8 * 8], F32)
        rt = sb("rt", [128, 5 * 8], F32)
        rm = sb("rm", [128, 4], F32)
        sin_sb = [sb(f"sin{i}", [128, 1024], F32) for i in range(2)]
        stm_sb = sb("stm", [128, 1024], F32)
        xin_sb = [sb(f"xin{i}", [128, 1024], F32) for i in range(2)]
        x1t_sb = [sb(f"x1t{i}", [128, 1024], F32) for i in range(2)]
        xn1_sb = sb("xn1", [128, 1024], F32)
        nst_sb = [sb(f"nst{i}", [128, 1024], BF16) for i in range(2)]
        lstt = sb("lstt", [128, 8], F32)
        sqs = sb("sqs", [128, 1024], F32)
        epst = sb("epst", [128, 1], F32)
        ident = sb("idb", [128, 128], BF16)
        identf = sb("idf", [128, 128], F32)

        pg = [ps(f"pg{i}", [128, 1024], F32) for i in range(2)]
        pu = [ps(f"pu{i}", [128, 1024], F32) for i in range(2)]
        pd = [pg[0][:, 0:512], pg[1][:, 0:512]]
        prt = pu[0][:, 0:64]
        ptr_s = pu[1][:, 0:512]
        ptr_n = pg[1][:, 512:1024]

        ac = Ctr(nc, tag + "ac")
        dv = Ctr(nc, tag + "dv")
        pe = Ctr(nc, tag + "pe")
        gmk = Ctr(nc, tag + "gmk")
        bld = Ctr(nc, tag + "bld")
        rld = Ctr(nc, tag + "rld")
        xld = Ctr(nc, tag + "xld")
        wlds = [Ctr(nc, tag + f"wld{i}") for i in range(2)]
        wdld = [Ctr(nc, tag + f"wdld{i}") for i in range(2)]
        slds = [Ctr(nc, tag + f"sld{i}") for i in range(2)]
        xilds = [Ctr(nc, tag + f"xild{i}") for i in range(2)]
        sox = [Ctr(nc, tag + f"sox{i}") for i in range(2)]
        son = [Ctr(nc, tag + f"son{i}") for i in range(2)]

        wd_free = {}
        epi_done = {}

        with nc.Block(tag) as blk:
            @blk.gpsimd
            def _(g):
                g.memset(epst, LN_EPS)
                g.memset(ones1, 1.0)
                _mk_identity(g, ident, gmk).then_inc(gmk.h, 1)
                gmk.n += 1
                g.wait_ge(gmk.h, gmk.n)
                _mk_identity(g, identf, gmk).then_inc(gmk.h, 1)
                gmk.n += 1
                for hc in range(8):
                    g.dma_start(out=wr_sb[:, hc * 8:(hc + 1) * 8],
                                in_=wr[hc * 128:(hc + 1) * 128, :]).then_inc(rld.h, 16)
            gmk_all = gmk.n
            rld.n += 16 * 8

            @blk.sync
            def _(sy):
                sy.dma_start(out=rrow, in_=rb).then_inc(bld.h, 16)
            bld.n += 16
            rbld = bld.n

            @blk.tensor
            def _(t, rbld=rbld):
                t.wait_ge(bld.h, rbld)
                t.wait_ge(gmk.h, gmk_all)
                pe.inc(t.matmul(prt[:, 0:8], ones1, rrow, start=True, stop=True))
            rb_pe = pe.n

            @blk.vector
            def _(v, rb_pe=rb_pe):
                v.wait_ge(pe.h, rb_pe)
                dv.inc(v.tensor_copy(rbb, prt[:, 0:8]))
            rbb_dv = dv.n

            wgfen = [0, 0]
            pgfen = [0, 0]
            pufen = [0, 0]
            prev_dn_pe = 0
            prev_dn_dv = 0
            for ti in range(NTILE):
                t0 = ti * TT

                prev_tile_fence = pe.n

                @blk.sync
                def _(sy, ti=ti, t0=t0, fence=prev_tile_fence):
                    sy.wait_ge(pe.h, fence)
                    for hc in range(8):
                        sy.dma_start(out=xnt[:, hc * TT:(hc + 1) * TT],
                                     in_=xnf[hc * 128:(hc + 1) * 128, t0:t0 + TT])\
                          .then_inc(xld.h, 16)
                xld.n += 16 * 8
                xfence = xld.n

                @blk.tensor
                def _(t, xfence=xfence, rbld=rbld):
                    t.wait_ge(xld.h, xfence)
                    t.wait_ge(bld.h, rbld)
                    t.wait_ge(rld.h, rld.n)
                    t.wait_ge(dv.h, dv.n)
                    for ts in range(8):
                        for hc in range(8):
                            ins = t.matmul(prt[:, ts * 8:(ts + 1) * 8],
                                           xnt[:, hc * TT + ts * 128: hc * TT + (ts + 1) * 128],
                                           wr_sb[:, hc * 8:(hc + 1) * 8],
                                           start=(hc == 0), stop=(hc == 7))
                    pe.inc(ins)
                rt_pe = pe.n

                for ts in range(8):
                    sl = slice(ts * 8, (ts + 1) * 8)

                    @blk.vector
                    def _(v, ts=ts, sl=sl, rt_pe=rt_pe, rbb_dv=rbb_dv, dfen=None):
                        v.wait_ge(pe.h, rt_pe)
                        v.wait_ge(dv.h, dv.n)
                        dv.inc(v.tensor_add(rt[:, 0:8], prt[:, sl], rbb))
                    lg_dv = dv.n

                    @blk.scalar
                    def _(sc, lg_dv=lg_dv):
                        sc.wait_ge(dv.h, lg_dv)
                        ac.inc(sc.activation(out=rt[:, 8:16], in_=rt[:, 0:8],
                                             func=AF.Sigmoid))
                    pr_ac = ac.n

                    @blk.vector
                    def _(v, ts=ts, sl=sl, pr_ac=pr_ac):
                        v.wait_ge(ac.h, pr_ac)
                        dv.inc(v.reduce_max(rm[:, 0:1], rt[:, 8:16], axis=AX))
                        v.wait_ge(dv.h, dv.n)
                        dv.inc(v.tensor_scalar(out=rt[:, 16:24], in0=rt[:, 8:16],
                                               scalar1=rm[:, 0:1], scalar2=None,
                                               op0=OP.is_ge))
                        v.wait_ge(dv.h, dv.n)
                        dv.inc(v.tensor_sub(rt[:, 24:32], rt[:, 8:16], rt[:, 16:24]))
                        v.wait_ge(dv.h, dv.n)
                        dv.inc(v.reduce_max(rm[:, 1:2], rt[:, 24:32], axis=AX))
                        v.wait_ge(dv.h, dv.n)
                        dv.inc(v.tensor_scalar(out=rt[:, 32:40], in0=rt[:, 8:16],
                                               scalar1=rm[:, 1:2], scalar2=None,
                                               op0=OP.is_ge))
                        v.wait_ge(dv.h, dv.n)
                        dv.inc(v.tensor_mul(cw_sb[:, sl], rt[:, 8:16], rt[:, 32:40]))
                cw_fence = dv.n

                for e in range(NE):
                    wgt = wgs if e == 0 else wge[e - 1]
                    wut = wus if e == 0 else wue[e - 1]
                    wdt = wds if e == 0 else wde[e - 1]

                    key = ti * NE + e
                    kk = key % 2

                    @blk.gpsimd
                    def _(g, e=e, ti=ti, wdt=wdt, key=key, kk=kk):
                        if key >= 2:
                            g.wait_ge(dv.h, wd_free[key - 2])
                        for icc in range(16):
                            g.dma_start(out=wd_sb[kk][:, icc * 1024:(icc + 1) * 1024],
                                        in_=wdt[icc * 128:(icc + 1) * 128, :])\
                              .then_inc(wdld[kk].h, 16)
                    wdld[kk].n += 16 * 16
                    wd_fence = wdld[kk].n

                    for ic in range(NIC):
                        wguard = wgfen[ic % 2]

                        @blk.gpsimd
                        def _(g, e=e, ic=ic, wgt=wgt, wut=wut, wguard=wguard):
                            if wguard:
                                g.wait_ge(dv.h, wguard)
                            src_g = wgt[:, ic * 128:(ic + 1) * 128]\
                                .rearrange("(hc p) i -> p hc i", p=128)
                            src_u = wut[:, ic * 128:(ic + 1) * 128]\
                                .rearrange("(hc p) i -> p hc i", p=128)
                            dst_g = wg_sb[ic % 2].rearrange("p (hc i) -> p hc i", hc=8)
                            dst_u = wu_sb[ic % 2].rearrange("p (hc i) -> p hc i", hc=8)
                            g.dma_start(out=dst_g, in_=src_g)\
                             .then_inc(wlds[ic % 2].h, 16)
                            g.dma_start(out=dst_u, in_=src_u)\
                             .then_inc(wlds[ic % 2].h, 16)
                        wlds[ic % 2].n += 32
                        wfence = wlds[ic % 2].n
                        pguard, puguard = pgfen[ic % 2], pufen[ic % 2]

                        @blk.tensor
                        def _(t, e=e, ic=ic, wfence=wfence, xfence=xfence,
                              pguard=pguard, puguard=puguard, cwf=cw_fence):
                            t.wait_ge(xld.h, xfence)
                            t.wait_ge(wlds[ic % 2].h, wfence)
                            if e == 0 and ic < 2:
                                t.wait_ge(dv.h, cwf)
                            if prev_dn_dv and ic < 2:
                                t.wait_ge(dv.h, prev_dn_dv)
                            if pguard:
                                t.wait_ge(ac.h, pguard)
                            if puguard:
                                t.wait_ge(dv.h, puguard)
                            for hc in range(8):
                                for half in range(2):
                                    t.matmul(pg[ic % 2][:, half * 512:(half + 1) * 512],
                                             wg_sb[ic % 2][:, hc * 128:(hc + 1) * 128],
                                             xnt[:, hc * TT + half * 512: hc * TT + (half + 1) * 512],
                                             start=(hc == 0), stop=(hc == 7))
                            for hc in range(8):
                                for half in range(2):
                                    ins = t.matmul(pu[ic % 2][:, half * 512:(half + 1) * 512],
                                                   wu_sb[ic % 2][:, hc * 128:(hc + 1) * 128],
                                                   xnt[:, hc * TT + half * 512: hc * TT + (half + 1) * 512],
                                                   start=(hc == 0), stop=(hc == 7))
                            pe.inc(ins)
                        up_pe = pe.n

                        @blk.scalar
                        def _(sc, ic=ic, up_pe=up_pe):
                            sc.wait_ge(pe.h, up_pe)
                            sc.wait_ge(dv.h, dv.n)
                            ac.inc(sc.activation(out=sg_sb[ic % 2], in_=pg[ic % 2],
                                                 func=AF.Silu))
                        pgfen[ic % 2] = ac.n
                        sil_ac = ac.n
                        dnguard = prev_dn_pe

                        @blk.vector
                        def _(v, ic=ic, sil_ac=sil_ac, dnguard=dnguard):
                            v.wait_ge(ac.h, sil_ac)
                            if dnguard:
                                v.wait_ge(pe.h, dnguard)
                            dv.inc(v.tensor_mul(act_sb[:, ic * TT:(ic + 1) * TT],
                                                pu[ic % 2], sg_sb[ic % 2]))
                        pufen[ic % 2] = dv.n
                        wgfen[ic % 2] = dv.n
                    act_fence = dv.n

                    dn_dv = {}
                    for ts in range(8):
                        for half in range(2):
                            j = ts * 2 + half

                            acfence_dn = ac.n

                            @blk.tensor
                            def _(t, e=e, ts=ts, half=half, j=j, kk=kk,
                                  act_fence=act_fence, wd_fence=wd_fence,
                                  acfence_dn=acfence_dn):
                                t.wait_ge(dv.h, act_fence)
                                t.wait_ge(ac.h, acfence_dn)
                                t.wait_ge(wdld[kk].h, wd_fence)
                                if j >= 2:
                                    t.wait_ge(dv.h, dn_dv[j - 2])
                                for icc in range(16):
                                    ins = t.matmul(
                                        pd[j % 2],
                                        act_sb[:, icc * TT + ts * 128: icc * TT + (ts + 1) * 128],
                                        wd_sb[kk][:, icc * 1024 + half * 512: icc * 1024 + (half + 1) * 512],
                                        start=(icc == 0), stop=(icc == 15))
                                pe.inc(ins)
                            dn_pe = pe.n

                            acguard = ac.n if e == 0 else 0

                            @blk.vector
                            def _(v, e=e, ts=ts, half=half, j=j, dn_pe=dn_pe,
                                  cw_fence=cw_fence, acguard=acguard):
                                v.wait_ge(pe.h, dn_pe)
                                v.wait_ge(dv.h, cw_fence)
                                if acguard:
                                    v.wait_ge(ac.h, acguard)
                                asl = acc_sb[:, ts * 1024 + half * 512: ts * 1024 + (half + 1) * 512]
                                if e == 0:
                                    dv.inc(v.tensor_copy(asl, pd[j % 2]))
                                else:
                                    dv.inc(v.scalar_tensor_tensor(
                                        out=asl, in0=pd[j % 2],
                                        scalar=cw_sb[:, ts * 8 + (e - 1): ts * 8 + e],
                                        in1=asl, op0=OP.mult, op1=OP.add))
                            dn_dv[j] = dv.n
                    wd_free[key] = dv.n
                    prev_dn_pe = pe.n
                    prev_dn_dv = dv.n
                ffn_fence = dv.n

                for ts in range(8):
                    tglob = t0 + ts * 128
                    bsel = ts % 2

                    @blk.sync
                    def _(sy, ti=ti, ts=ts, tglob=tglob, bsel=bsel):
                        if ti * 8 + ts >= 2:
                            sy.wait_ge(dv.h, epi_done[ti * 8 + ts - 2])
                        for j in range(8):
                            sy.dma_start(out=sin_sb[bsel][:, j * 128:(j + 1) * 128],
                                         in_=s_f[j * 128:(j + 1) * 128, tglob:tglob + 128])\
                              .then_inc(slds[bsel].h, 16)
                        sy.dma_start(out=xin_sb[bsel], in_=x_ap[tglob:tglob + 128, :])\
                          .then_inc(xilds[bsel].h, 16)
                    slds[bsel].n += 16 * 8
                    xilds[bsel].n += 16
                    sfence = slds[bsel].n
                    xifence = xilds[bsel].n

                    stm_dv = None
                    for rnd in range(2):
                        @blk.tensor
                        def _(t, ts=ts, bsel=bsel, sfence=sfence, rnd=rnd):
                            t.wait_ge(slds[bsel].h, sfence)
                            t.wait_ge(gmk.h, gmk_all)
                            t.wait_ge(dv.h, dv.n)
                            for j in range(4):
                                jj = rnd * 4 + j
                                ins = t.transpose(ptr_s[:, j * 128:(j + 1) * 128],
                                                  sin_sb[bsel][:, jj * 128:(jj + 1) * 128],
                                                  identf)
                            pe.inc(ins)
                        str_pe = pe.n

                        @blk.vector
                        def _(v, str_pe=str_pe, rnd=rnd):
                            v.wait_ge(pe.h, str_pe)
                            v.wait_ge(dv.h, dv.n)
                            dv.inc(v.tensor_copy(stm_sb[:, rnd * 512:(rnd + 1) * 512],
                                                 ptr_s))
                    stm_dv = dv.n

                    @blk.scalar
                    def _(sc, ts=ts, ffn_fence=ffn_fence, acfen=None):
                        sc.wait_ge(dv.h, ffn_fence)
                        sc.wait_ge(dv.h, dv.n)
                        sc.wait_ge(ac.h, ac.n)
                        ac.inc(sc.activation(out=sqs, in_=acc_sb[:, ts * 1024:(ts + 1) * 1024],
                                             func=AF.Silu))
                    sil_ac2 = ac.n

                    @blk.vector
                    def _(v, ts=ts, bsel=bsel, stm_dv=stm_dv, sil_ac2=sil_ac2,
                          xifence=xifence):
                        v.wait_ge(dv.h, stm_dv)
                        v.wait_ge(ac.h, sil_ac2)
                        dv.inc(v.tensor_mul(stm_sb, stm_sb, sqs))
                        v.wait_ge(xilds[bsel].h, xifence)
                        v.wait_ge(sox[bsel].h, sox[bsel].n)
                        v.wait_ge(dv.h, dv.n)
                        dv.inc(v.tensor_add(x1t_sb[bsel], xin_sb[bsel], stm_sb))
                    x1_dv = dv.n

                    @blk.scalar
                    def _(sc, bsel=bsel, x1_dv=x1_dv):
                        sc.wait_ge(dv.h, x1_dv)
                        ac.inc(sc.activation(out=sqs, in_=x1t_sb[bsel], func=AF.Square,
                                             accum_out=lstt[:, 1:2]))
                    s2_ac = ac.n

                    @blk.vector
                    def _(v, bsel=bsel, x1_dv=x1_dv, s2_ac=s2_ac):
                        v.wait_ge(dv.h, x1_dv)
                        dv.inc(v.reduce_sum(lstt[:, 0:1], x1t_sb[bsel], axis=AX))
                        v.wait_ge(dv.h, dv.n)
                        dv.inc(v.tensor_scalar(out=lstt[:, 2:3], in0=lstt[:, 0:1],
                                               scalar1=1.0 / D, scalar2=None, op0=OP.mult))
                        v.wait_ge(dv.h, dv.n)
                        dv.inc(v.scalar_tensor_tensor(out=lstt[:, 5:6], in0=lstt[:, 2:3],
                                                      scalar=1.0, in1=lstt[:, 2:3],
                                                      op0=OP.mult, op1=OP.mult))
                        v.wait_ge(ac.h, s2_ac)
                        v.wait_ge(dv.h, dv.n)
                        dv.inc(v.scalar_tensor_tensor(out=lstt[:, 3:4], in0=lstt[:, 1:2],
                                                      scalar=1.0 / D, in1=lstt[:, 5:6],
                                                      op0=OP.mult, op1=OP.subtract))
                    var_dv = dv.n

                    @blk.scalar
                    def _(sc, var_dv=var_dv):
                        sc.wait_ge(dv.h, var_dv)
                        ac.inc(sc.activation(out=lstt[:, 4:5], in_=lstt[:, 3:4],
                                             func=AF.Sqrt, bias=epst[:, 0:1]))
                    std_ac = ac.n

                    @blk.vector
                    def _(v, bsel=bsel, std_ac=std_ac):
                        v.wait_ge(ac.h, std_ac)
                        dv.inc(v.reciprocal(lstt[:, 5:6], lstt[:, 4:5]))
                        v.wait_ge(dv.h, dv.n)
                        dv.inc(v.tensor_scalar(out=xn1_sb, in0=x1t_sb[bsel],
                                               scalar1=lstt[:, 2:3], scalar2=lstt[:, 5:6],
                                               op0=OP.subtract, op1=OP.mult))
                    xn1_dv = dv.n

                    for rnd in range(2):
                        @blk.tensor
                        def _(t, xn1_dv=xn1_dv, rnd=rnd):
                            t.wait_ge(dv.h, xn1_dv)
                            t.wait_ge(dv.h, dv.n)
                            for j in range(4):
                                jj = rnd * 4 + j
                                ins = t.transpose(ptr_n[:, j * 128:(j + 1) * 128],
                                                  xn1_sb[:, jj * 128:(jj + 1) * 128],
                                                  identf)
                            pe.inc(ins)
                        ntr_pe = pe.n

                        @blk.vector
                        def _(v, ti=ti, ts=ts, bsel=bsel, ntr_pe=ntr_pe, rnd=rnd):
                            v.wait_ge(pe.h, ntr_pe)
                            if ti * 8 + ts >= 2:
                                v.wait_ge(son[bsel].h, son[bsel].n)
                            dv.inc(v.tensor_copy(nst_sb[bsel][:, rnd * 512:(rnd + 1) * 512],
                                                 ptr_n))
                    nst_dv = dv.n

                    @blk.sync
                    def _(sy, ti=ti, ts=ts, tglob=tglob, bsel=bsel, x1_dv=x1_dv,
                          nst_dv=nst_dv):
                        sy.wait_ge(dv.h, x1_dv)
                        sy.dma_start(out=x1_ap[tglob:tglob + 128, :],
                                     in_=x1t_sb[bsel]).then_inc(sox[bsel].h, 16)
                        sy.wait_ge(dv.h, nst_dv)
                        for j in range(8):
                            sy.dma_start(out=xn1f_ap[j * 128:(j + 1) * 128,
                                                     tglob:tglob + 128],
                                         in_=nst_sb[bsel][:, j * 128:(j + 1) * 128])\
                              .then_inc(son[bsel].h, 16)
                        if xn1send_ap is not None:
                            w = tglob // 512
                            c0 = tglob - w * 512
                            for j in range(8):
                                sy.dma_start(
                                    out=xn1send_ap[w * 1024 + j * 128:
                                                   w * 1024 + (j + 1) * 128,
                                                   c0:c0 + 128],
                                    in_=nst_sb[bsel][:, j * 128:(j + 1) * 128])\
                                  .then_inc(son[bsel].h, 16)
                    sox[bsel].n += 16
                    son[bsel].n += 16 * 8 * (2 if xn1send_ap is not None else 1)
                    epi_done[ti * 8 + ts] = dv.n
                pgfen = [0, 0]
                pufen = [dv.n, dv.n]
                prev_dn_pe = pe.n

            @blk.sync
            def _(sy):
                for cs in sox + son:
                    sy.wait_ge(cs.h, cs.n)
        return [(cs.h, cs.n) for cs in sox + son]


def emit_kvproj(nc, xn1f, wkt, wvt, k1_ap, v1_ap, tag):
    with ExitStack() as es:
        sb = lambda n, sh, dt: es.enter_context(nc.sbuf_tensor(tag + n, sh, dt)).ap()
        ps = lambda n, sh, dt: es.enter_context(nc.psum_tensor(tag + n, sh, dt)).ap()
        xnt = sb("xnt", [128, 8 * 1024], BF16)
        wk_sb = sb("wk", [128, 8 * 1024], BF16)
        wv_sb = sb("wv", [128, 8 * 1024], BF16)
        kst = [sb(f"kst{i}", [128, 1024], BF16) for i in range(2)]
        vst = [sb(f"vst{i}", [128, 1024], BF16) for i in range(2)]
        pk = ps("pk", [128, 1024], F32)
        pv = ps("pv", [128, 1024], F32)

        wld = Ctr(nc, tag + "wld")
        xld = Ctr(nc, tag + "xld")
        pe = Ctr(nc, tag + "pe")
        dv = Ctr(nc, tag + "dv")
        sos = [Ctr(nc, tag + f"so{i}") for i in range(2)]

        with nc.Block(tag) as blk:
            @blk.gpsimd
            def _(g):
                for hc in range(8):
                    g.dma_start(out=wk_sb[:, hc * 1024:(hc + 1) * 1024],
                                in_=wkt[hc * 128:(hc + 1) * 128, :]).then_inc(wld.h, 16)
                    g.dma_start(out=wv_sb[:, hc * 1024:(hc + 1) * 1024],
                                in_=wvt[hc * 128:(hc + 1) * 128, :]).then_inc(wld.h, 16)
            wld.n += 16 * 16
            wfence = wld.n

            out_done = {}
            for ti in range(4):
                t0 = ti * 1024

                @blk.sync
                def _(sy, ti=ti, t0=t0, pfen=None):
                    sy.wait_ge(pe.h, pe.n)
                    for hc in range(8):
                        sy.dma_start(out=xnt[:, hc * 1024:(hc + 1) * 1024],
                                     in_=xn1f[hc * 128:(hc + 1) * 128, t0:t0 + 1024])\
                          .then_inc(xld.h, 16)
                xld.n += 16 * 8
                xfence = xld.n

                for ts in range(8):
                    bsel = ts % 2
                    idx = ti * 8 + ts

                    @blk.tensor
                    def _(t, ts=ts, xfence=xfence, wfence=wfence):
                        t.wait_ge(xld.h, xfence)
                        t.wait_ge(wld.h, wfence)
                        t.wait_ge(dv.h, dv.n)
                        for hc in range(8):
                            for half in range(2):
                                t.matmul(pk[:, half * 512:(half + 1) * 512],
                                         xnt[:, hc * 1024 + ts * 128: hc * 1024 + (ts + 1) * 128],
                                         wk_sb[:, hc * 1024 + half * 512: hc * 1024 + (half + 1) * 512],
                                         start=(hc == 0), stop=(hc == 7))
                        for hc in range(8):
                            for half in range(2):
                                ins = t.matmul(pv[:, half * 512:(half + 1) * 512],
                                               xnt[:, hc * 1024 + ts * 128: hc * 1024 + (ts + 1) * 128],
                                               wv_sb[:, hc * 1024 + half * 512: hc * 1024 + (half + 1) * 512],
                                               start=(hc == 0), stop=(hc == 7))
                        pe.inc(ins)
                    kv_pe = pe.n

                    @blk.vector
                    def _(v, bsel=bsel, idx=idx, kv_pe=kv_pe):
                        v.wait_ge(pe.h, kv_pe)
                        if idx >= 2:
                            v.wait_ge(sos[bsel].h, out_done[idx - 2])
                        dv.inc(v.tensor_copy(kst[bsel], pk))
                        dv.inc(v.tensor_copy(vst[bsel], pv))
                    kv_dv = dv.n

                    @blk.sync
                    def _(sy, ti=ti, ts=ts, t0=t0, bsel=bsel, kv_dv=kv_dv):
                        sy.wait_ge(dv.h, kv_dv)
                        tg = t0 + ts * 128
                        sy.dma_start(out=k1_ap[tg:tg + 128, :], in_=kst[bsel])\
                          .then_inc(sos[bsel].h, 16)
                        sy.dma_start(out=v1_ap[tg:tg + 128, :], in_=vst[bsel])\
                          .then_inc(sos[bsel].h, 16)
                    sos[bsel].n += 32
                    out_done[idx] = sos[bsel].n

            @blk.sync
            def _(sy):
                sy.wait_ge(sos[0].h, sos[0].n)
                sy.wait_ge(sos[1].h, sos[1].n)
        return [(sos[0].h, sos[0].n), (sos[1].h, sos[1].n)]


def _one_block(nc, name):
    def deco(f):
        with nc.Block(name) as blk:
            f(blk)
    return deco



def emit_l1scores(nc, xn1q_sh, wqt, k1all, mask, a_send, tag, a_recv=None, cc=None):
    NB = 8
    with ExitStack() as es:
        dv = Ctr(nc, tag + "dv")
        ac = Ctr(nc, tag + "ac")
        pe = Ctr(nc, tag + "pe")

        with ExitStack() as es2:
            sb2 = lambda n, sh, dt: es2.enter_context(nc.sbuf_tensor(tag + n, sh, dt)).ap()
            ps2 = lambda n, sh, dt: es2.enter_context(nc.psum_tensor(tag + n, sh, dt)).ap()
            q1_sb = sb2("q1", [128, 8 * S], BF16)
            wq_sb = sb2("wq", [128, 8 * 1024], BF16)
            xq_sb = sb2("xq", [128, 8 * 512], BF16)
            kt_sb = [sb2(f"kt{i}", [128, 8 * 512], BF16) for i in range(3)]
            e_sb = sb2("e", [128, 32 * 512], F32)
            mk_sb = sb2("mk", [128, 4 * 512], I8)
            tiny = sb2("tiny", [128, 512], F32)
            d_sb = sb2("d", [128, 512], F32)
            r_sb = sb2("r", [128, 512], F32)
            a_st = [sb2(f"ast{i}", [128, 512], BF16) for i in range(2)]
            pq = [ps2(f"pq{i}", [128, 512], F32) for i in range(2)]

            wld = Ctr(nc, tag + "wld")
            xqld = Ctr(nc, tag + "xqld")
            ktld = [Ctr(nc, tag + f"ktld{i}") for i in range(3)]
            mkld = Ctr(nc, tag + "mkld")
            aso = [Ctr(nc, tag + f"aso{i}") for i in range(2)]
            gmk = Ctr(nc, tag + "gmk")

            with nc.Block(tag + "qs") as blk:
                @blk.gpsimd
                def _(g):
                    g.memset(tiny, 1e-9).then_inc(gmk.h, 1)
                    for hc in range(8):
                        g.dma_start(out=wq_sb[:, hc * 1024:(hc + 1) * 1024],
                                    in_=wqt[hc * 128:(hc + 1) * 128, :])\
                          .then_inc(wld.h, 16)
                gmk.n += 1
                wld.n += 16 * 8
                wfen = wld.n

                for b in range(NB):
                    @blk.sync
                    def _(sy, b=b):
                        sy.wait_ge(pe.h, pe.n)
                        for hc in range(8):
                            sy.dma_start(out=xq_sb[:, hc * 512:(hc + 1) * 512],
                                         in_=xn1q_sh[b * 1024 + hc * 128:
                                                     b * 1024 + (hc + 1) * 128, 0:512])\
                              .then_inc(xqld.h, 16)
                    xqld.n += 16 * 8
                    xfen = xqld.n

                    for dc in range(8):
                        @blk.tensor
                        def _(t, b=b, dc=dc, xfen=xfen, wfen=wfen):
                            t.wait_ge(xqld.h, xfen)
                            t.wait_ge(wld.h, wfen)
                            t.wait_ge(dv.h, dv.n)
                            for hc in range(8):
                                ins = t.matmul(pq[dc % 2],
                                               wq_sb[:, hc * 1024 + dc * 128: hc * 1024 + (dc + 1) * 128],
                                               xq_sb[:, hc * 512:(hc + 1) * 512],
                                               start=(hc == 0), stop=(hc == 7))
                            pe.inc(ins)
                        qpe = pe.n

                        @blk.vector
                        def _(v, b=b, dc=dc, qpe=qpe):
                            v.wait_ge(pe.h, qpe)
                            dv.inc(v.tensor_copy(
                                q1_sb[:, dc * S + b * 512: dc * S + (b + 1) * 512],
                                pq[dc % 2]))
                q_fence = dv.n

                prev_stripe_dv = dv.n
                ktuse = {}
                for st in range(8):
                    j0 = st * 512

                    @blk.sync
                    def _(sy, st=st, j0=j0):
                        sy.wait_ge(dv.h, dv.n)
                        msrc = mask[j0:j0 + 512, :]\
                            .rearrange("(jc p) i -> p jc i", p=128)
                        mdst = mk_sb.rearrange("p (jc i) -> p jc i", jc=4)
                        sy.dma_start(out=mdst, in_=msrc).then_inc(mkld.h, 16)
                    mkld.n += 16
                    mfen = mkld.n

                    for b in range(NB):
                        kti = st * NB + b
                        ktc = ktld[kti % 3]

                        @blk.gpsimd
                        def _(g, st=st, b=b, j0=j0, kti=kti, ktc=ktc):
                            if kti >= 3:
                                g.wait_ge(pe.h, ktuse[kti - 3])
                            ktv = k1all[b * S:(b + 1) * S, :]\
                                .rearrange("s d -> (s d)")\
                                .rearrange("(dc p j) -> p dc j", dc=8, p=128)
                            dst = kt_sb[kti % 3].rearrange("p (dc j) -> p dc j", dc=8)
                            g.dma_start(out=dst, in_=ktv[:, :, j0:j0 + 512])\
                              .then_inc(ktc.h, 16)
                        ktc.n += 16
                        kfen = ktc.n

                        for jc in range(4):
                            @blk.tensor
                            def _(t, st=st, b=b, jc=jc, kti=kti, kfen=kfen, ktc=ktc,
                                  q_fence=q_fence):
                                t.wait_ge(ktc.h, kfen)
                                t.wait_ge(dv.h, q_fence)
                                t.wait_ge(ac.h, ac.n)
                                for dc in range(8):
                                    ins = t.matmul(
                                        pq[jc % 2],
                                        kt_sb[kti % 3][:, dc * 512 + jc * 128: dc * 512 + (jc + 1) * 128],
                                        q1_sb[:, dc * S + b * 512: dc * S + (b + 1) * 512],
                                        start=(dc == 0), stop=(dc == 7))
                                pe.inc(ins)
                            spe = pe.n

                            @blk.scalar
                            def _(sc, b=b, jc=jc, spe=spe, psd=prev_stripe_dv):
                                sc.wait_ge(pe.h, spe)
                                sc.wait_ge(dv.h, psd)
                                ac.inc(sc.activation(
                                    out=e_sb[:, (jc * 8 + b) * 512:(jc * 8 + b + 1) * 512],
                                    in_=pq[jc % 2], func=AF.Exp, scale=0.03125))
                        ktuse[kti] = pe.n
                    stripe_ac = ac.n

                    for jc in range(4):
                        @blk.vector
                        def _(v, jc=jc, stripe_ac=stripe_ac):
                            v.wait_ge(ac.h, stripe_ac)
                            esl = lambda b: e_sb[:, (jc * 8 + b) * 512:(jc * 8 + b + 1) * 512]
                            dv.inc(v.tensor_add(d_sb, esl(0), esl(1)))
                            for b in (2, 3, 4, 5, 6, 7):
                                v.wait_ge(dv.h, dv.n)
                                dv.inc(v.tensor_add(d_sb, d_sb, esl(b)))
                            v.wait_ge(dv.h, dv.n)
                            dv.inc(v.reciprocal(r_sb, d_sb))
                        r_fence = dv.n

                        for b in range(NB):
                            asb = a_st[b % 2]
                            asoc = aso[b % 2]

                            @blk.vector
                            def _(v, st=st, b=b, jc=jc, asb=asb, asoc=asoc,
                                  r_fence=r_fence, mfen=mfen):
                                v.wait_ge(dv.h, r_fence)
                                v.wait_ge(mkld.h, mfen)
                                v.wait_ge(gmk.h, 1)
                                v.wait_ge(asoc.h, asoc.n)
                                esl = e_sb[:, (jc * 8 + b) * 512:(jc * 8 + b + 1) * 512]
                                dv.inc(v.tensor_mul(esl, esl, r_sb))
                                v.wait_ge(dv.h, dv.n)
                                dv.inc(v.tensor_copy(asb, tiny))
                                v.wait_ge(dv.h, dv.n)
                                dv.inc(v.copy_predicated(asb,
                                                         mk_sb[:, jc * 512:(jc + 1) * 512],
                                                         esl))
                            a_dv = dv.n

                            @blk.sync
                            def _(sy, st=st, b=b, jc=jc, j0=j0, asb=asb,
                                  asoc=asoc, a_dv=a_dv):
                                sy.wait_ge(dv.h, a_dv)
                                sy.dma_start(
                                    out=a_send[st * S + b * 512 + jc * 128:
                                               st * S + b * 512 + (jc + 1) * 128, :],
                                    in_=asb).then_inc(asoc.h, 16)
                            asoc.n += 16
                    prev_stripe_dv = dv.n

                    if cc is not None:
                        @blk.gpsimd
                        def _(g, st=st, n0=aso[0].n, n1=aso[1].n):
                            g.wait_ge(aso[0].h, n0)
                            g.wait_ge(aso[1].h, n1)
                            with allow_ici_all_to_all():
                                g.collective_compute(
                                    "AllToAll", mybir.AluOpType.bypass,
                                    replica_groups=RG,
                                    ins=[a_send[st * S:(st + 1) * S, :].opt()],
                                    outs=[a_recv[st * S:(st + 1) * S, :].opt()])\
                                 .then_inc(cc.h, 1)
                        cc.n += 1

                @blk.sync
                def _(sy):
                    sy.wait_ge(aso[0].h, aso[0].n)
                    sy.wait_ge(aso[1].h, aso[1].n)
        return [(aso[0].h, aso[0].n), (aso[1].h, aso[1].n)]


def emit_l1ffn_g(nc, xn1f, ws, g1_ap, tag):
    sut, sdt, rut, rdt, rwt = ws
    NE = 18
    TT = 2048
    NTILE = 2
    NIC = 4
    with ExitStack() as es:
        sb = lambda n, sh, dt: es.enter_context(nc.sbuf_tensor(tag + n, sh, dt)).ap()
        ps = lambda n, sh, dt: es.enter_context(nc.psum_tensor(tag + n, sh, dt)).ap()
        xnt = sb("xnt", [128, 8 * TT], BF16)
        wu_sb = [sb(f"wu{i}", [128, 1024], BF16) for i in range(2)]
        wd_sb = [sb(f"wd{i}", [128, 4 * 1024], BF16) for i in range(2)]
        act_sb = sb("act", [128, 4 * TT], BF16)
        acc_sb = sb("acc", [128, 16 * 1024], F32)
        wr_sb = sb("wrs", [128, 8 * 16], BF16)
        gate_sb = sb("gate", [128, 16 * 16], F32)
        rt = sb("rt", [128, 6 * 16], F32)
        rm = sb("rm", [128, 4], F32)
        sqs = sb("sqs", [128, 1024], F32)
        gst = [sb(f"gst{i}", [128, 1024], BF16) for i in range(2)]

        pgx = [ps(f"pgx{i}", [128, 2048], F32) for i in range(2)]
        pd = [pgx[0][:, 0:512], pgx[1][:, 0:512]]
        prt = pgx[0][:, 512:1024]

        ac = Ctr(nc, tag + "ac")
        dv = Ctr(nc, tag + "dv")
        pe = Ctr(nc, tag + "pe")
        rld = Ctr(nc, tag + "rld")
        xld = Ctr(nc, tag + "xld")
        wlds = [Ctr(nc, tag + f"wld{i}") for i in range(2)]
        wdld = [Ctr(nc, tag + f"wdld{i}") for i in range(2)]
        dso = [Ctr(nc, tag + f"dso{i}") for i in range(2)]

        wd_free = {}
        epi_done = {}

        with nc.Block(tag) as blk:
            @blk.gpsimd
            def _(g):
                for hc in range(8):
                    g.dma_start(out=wr_sb[:, hc * 16:(hc + 1) * 16],
                                in_=rwt[hc * 128:(hc + 1) * 128, :]).then_inc(rld.h, 16)
            rld.n += 16 * 8

            prev_dn_pe = 0
            prev_dn_dv = 0
            pgfen = [0, 0]
            for ti in range(NTILE):
                t0 = ti * TT

                @blk.sync
                def _(sy, ti=ti, t0=t0):
                    sy.wait_ge(pe.h, pe.n)
                    for hc in range(8):
                        sy.dma_start(out=xnt[:, hc * TT:(hc + 1) * TT],
                                     in_=xn1f[hc * 128:(hc + 1) * 128, t0:t0 + TT])\
                          .then_inc(xld.h, 16)
                xld.n += 16 * 8
                xfence = xld.n

                for ts in range(16):
                    @blk.tensor
                    def _(t, ts=ts, xfence=xfence):
                        t.wait_ge(xld.h, xfence)
                        t.wait_ge(rld.h, rld.n)
                        t.wait_ge(dv.h, dv.n)
                        for hc in range(8):
                            ins = t.matmul(prt[:, 0:16],
                                           xnt[:, hc * TT + ts * 128: hc * TT + (ts + 1) * 128],
                                           wr_sb[:, hc * 16:(hc + 1) * 16],
                                           start=(hc == 0), stop=(hc == 7))
                        pe.inc(ins)
                    rpe = pe.n

                    @blk.vector
                    def _(v, ts=ts, rpe=rpe):
                        v.wait_ge(pe.h, rpe)
                        dv.inc(v.reduce_max(rm[:, 0:1], prt[:, 0:16], axis=AX))
                        v.wait_ge(dv.h, dv.n)
                        dv.inc(v.tensor_scalar(out=rt[:, 0:16], in0=prt[:, 0:16],
                                               scalar1=rm[:, 0:1], scalar2=None,
                                               op0=OP.subtract))
                    lg_dv = dv.n

                    @blk.scalar
                    def _(sc, lg_dv=lg_dv):
                        sc.wait_ge(dv.h, lg_dv)
                        sc.wait_ge(ac.h, ac.n)
                        ac.inc(sc.activation(out=rt[:, 16:32], in_=rt[:, 0:16],
                                             func=AF.Exp))
                    ex_ac = ac.n

                    @blk.vector
                    def _(v, ts=ts, ex_ac=ex_ac):
                        v.wait_ge(ac.h, ex_ac)
                        dv.inc(v.reduce_sum(rm[:, 1:2], rt[:, 16:32], axis=AX))
                        v.wait_ge(dv.h, dv.n)
                        dv.inc(v.reciprocal(rm[:, 2:3], rm[:, 1:2]))
                        v.wait_ge(dv.h, dv.n)
                        dv.inc(v.tensor_scalar(out=rt[:, 16:32], in0=rt[:, 16:32],
                                               scalar1=rm[:, 2:3], scalar2=None,
                                               op0=OP.mult))
                        v.wait_ge(dv.h, dv.n)
                        dv.inc(v.tensor_copy(rt[:, 32:48], rt[:, 16:32]))
                        for r in range(4):
                            v.wait_ge(dv.h, dv.n)
                            dv.inc(v.reduce_max(rm[:, 3:4], rt[:, 32:48], axis=AX))
                            if r < 3:
                                v.wait_ge(dv.h, dv.n)
                                dv.inc(v.tensor_scalar(out=rt[:, 48:64], in0=rt[:, 32:48],
                                                       scalar1=rm[:, 3:4], scalar2=None,
                                                       op0=OP.is_ge))
                                v.wait_ge(dv.h, dv.n)
                                dv.inc(v.tensor_sub(rt[:, 32:48], rt[:, 32:48],
                                                    rt[:, 48:64]))
                        v.wait_ge(dv.h, dv.n)
                        dv.inc(v.tensor_scalar(out=rt[:, 48:64], in0=rt[:, 16:32],
                                               scalar1=rm[:, 3:4], scalar2=None,
                                               op0=OP.is_ge))
                        v.wait_ge(dv.h, dv.n)
                        dv.inc(v.tensor_mul(gate_sb[:, ts * 16:(ts + 1) * 16],
                                            rt[:, 16:32], rt[:, 48:64]))
                gate_fence = dv.n

                for e in range(NE):
                    wut = sut[e] if e < 2 else rut[e - 2]
                    wdt = sdt[e] if e < 2 else rdt[e - 2]
                    key = ti * NE + e
                    kk = key % 2

                    @blk.gpsimd
                    def _(g, wdt=wdt, key=key, kk=kk):
                        if key >= 2:
                            g.wait_ge(dv.h, wd_free[key - 2])
                        for icc in range(4):
                            g.dma_start(out=wd_sb[kk][:, icc * 1024:(icc + 1) * 1024],
                                        in_=wdt[icc * 128:(icc + 1) * 128, :])\
                              .then_inc(wdld[kk].h, 16)
                    wdld[kk].n += 16 * 4
                    wd_fence = wdld[kk].n

                    for ic in range(NIC):
                        wguard = pgfen[ic % 2]

                        @blk.gpsimd
                        def _(g, ic=ic, wut=wut, wguard=wguard):
                            if wguard:
                                g.wait_ge(ac.h, wguard)
                            srcw = wut[:, ic * 128:(ic + 1) * 128]\
                                .rearrange("(hc p) i -> p hc i", p=128)
                            dstw = wu_sb[ic % 2].rearrange("p (hc i) -> p hc i", hc=8)
                            g.dma_start(out=dstw, in_=srcw).then_inc(wlds[ic % 2].h, 16)
                        wlds[ic % 2].n += 16
                        wfen = wlds[ic % 2].n

                        @blk.tensor
                        def _(t, e=e, ic=ic, wfen=wfen, xfence=xfence,
                              pguard=pgfen[ic % 2], pdd=prev_dn_dv, gf=gate_fence):
                            t.wait_ge(xld.h, xfence)
                            t.wait_ge(wlds[ic % 2].h, wfen)
                            if e == 0 and ic == 0:
                                t.wait_ge(dv.h, gf)
                            if pdd and ic < 2:
                                t.wait_ge(dv.h, pdd)
                            if pguard:
                                t.wait_ge(ac.h, pguard)
                            for hc in range(8):
                                for q in range(4):
                                    ins = t.matmul(
                                        pgx[ic % 2][:, q * 512:(q + 1) * 512],
                                        wu_sb[ic % 2][:, hc * 128:(hc + 1) * 128],
                                        xnt[:, hc * TT + q * 512: hc * TT + (q + 1) * 512],
                                        start=(hc == 0), stop=(hc == 7))
                            pe.inc(ins)
                        up_pe = pe.n

                        @blk.scalar
                        def _(sc, e=e, ic=ic, up_pe=up_pe, dnp=prev_dn_pe):
                            sc.wait_ge(pe.h, up_pe)
                            if dnp:
                                sc.wait_ge(pe.h, dnp)
                            ac.inc(sc.activation(out=act_sb[:, ic * TT:(ic + 1) * TT],
                                                 in_=pgx[ic % 2], func=AF.Gelu))
                        pgfen[ic % 2] = ac.n
                    act_fence = ac.n

                    dn_dv = {}
                    for ts in range(16):
                        for half in range(2):
                            j = ts * 2 + half

                            @blk.tensor
                            def _(t, ts=ts, half=half, j=j, kk=kk,
                                  act_fence=act_fence, wd_fence=wd_fence):
                                t.wait_ge(ac.h, act_fence)
                                t.wait_ge(wdld[kk].h, wd_fence)
                                if j >= 2:
                                    t.wait_ge(dv.h, dn_dv[j - 2])
                                for icc in range(4):
                                    ins = t.matmul(
                                        pd[j % 2],
                                        act_sb[:, icc * TT + ts * 128: icc * TT + (ts + 1) * 128],
                                        wd_sb[kk][:, icc * 1024 + half * 512: icc * 1024 + (half + 1) * 512],
                                        start=(icc == 0), stop=(icc == 3))
                                pe.inc(ins)
                            dn_pe = pe.n
                            acguard = ac.n if e == 0 else 0

                            @blk.vector
                            def _(v, e=e, ts=ts, half=half, j=j, dn_pe=dn_pe,
                                  gate_fence=gate_fence, acguard=acguard):
                                v.wait_ge(pe.h, dn_pe)
                                v.wait_ge(dv.h, gate_fence)
                                if acguard:
                                    v.wait_ge(ac.h, acguard)
                                asl = acc_sb[:, ts * 1024 + half * 512: ts * 1024 + (half + 1) * 512]
                                if e == 0:
                                    dv.inc(v.tensor_copy(asl, pd[j % 2]))
                                elif e == 1:
                                    dv.inc(v.scalar_tensor_tensor(
                                        out=asl, in0=pd[j % 2], scalar=1.0,
                                        in1=asl, op0=OP.mult, op1=OP.add))
                                else:
                                    dv.inc(v.scalar_tensor_tensor(
                                        out=asl, in0=pd[j % 2],
                                        scalar=gate_sb[:, ts * 16 + (e - 2): ts * 16 + e - 1],
                                        in1=asl, op0=OP.mult, op1=OP.add))
                            dn_dv[j] = dv.n
                    wd_free[key] = dv.n
                    prev_dn_pe = pe.n
                    prev_dn_dv = dv.n
                ffn_fence = dv.n

                for ts in range(16):
                    tglob = t0 + ts * 128
                    bsel = ts % 2

                    @blk.scalar
                    def _(sc, ts=ts, ffn_fence=ffn_fence):
                        sc.wait_ge(dv.h, ffn_fence)
                        sc.wait_ge(dv.h, dv.n)
                        sc.wait_ge(ac.h, ac.n)
                        ac.inc(sc.activation(out=sqs,
                                             in_=acc_sb[:, ts * 1024:(ts + 1) * 1024],
                                             func=AF.Silu))
                    sil_ac = ac.n

                    @blk.vector
                    def _(v, ts=ts, bsel=bsel, sil_ac=sil_ac):
                        v.wait_ge(ac.h, sil_ac)
                        v.wait_ge(dso[bsel].h, dso[bsel].n)
                        dv.inc(v.tensor_copy(gst[bsel], sqs))
                    g_dv = dv.n

                    @blk.sync
                    def _(sy, tglob=tglob, bsel=bsel, g_dv=g_dv):
                        sy.wait_ge(dv.h, g_dv)
                        sy.dma_start(out=g1_ap[tglob:tglob + 128, :],
                                     in_=gst[bsel]).then_inc(dso[bsel].h, 16)
                    dso[bsel].n += 16
                    epi_done[ti * 16 + ts] = dv.n

            @blk.sync
            def _(sy):
                sy.wait_ge(dso[0].h, dso[0].n)
                sy.wait_ge(dso[1].h, dso[1].n)
        return [(dso[0].h, dso[0].n), (dso[1].h, dso[1].n)]


def emit_pv_final(nc, a_recv, v1loc, g1, x1_ap, y_ap, tag, cc=None, ccbase=2):
    NJ = 32
    with ExitStack() as es:
        sb = lambda n, sh, dt: es.enter_context(nc.sbuf_tensor(tag + n, sh, dt)).ap()
        ps = lambda n, sh, dt: es.enter_context(nc.psum_tensor(tag + n, sh, dt)).ap()
        v_all = sb("vall", [128, NJ * 1024], BF16)
        a_sb = [sb(f"a{i}", [128, 512], BF16) for i in range(4)]
        gt = [sb(f"g{i}", [128, 1024], BF16) for i in range(2)]
        xt = [sb(f"x{i}", [128, 1024], F32) for i in range(2)]
        yt = [sb(f"yy{i}", [128, 1024], F32) for i in range(2)]
        ppv = [ps(f"p{i}", [128, 512], F32) for i in range(8)]

        pe = Ctr(nc, tag + "pe")
        dv = Ctr(nc, tag + "dv")
        vld = Ctr(nc, tag + "vld")
        ald = [Ctr(nc, tag + f"ald{i}") for i in range(4)]
        gld = [Ctr(nc, tag + f"gld{i}") for i in range(2)]
        yso = [Ctr(nc, tag + f"yso{i}") for i in range(2)]

        with nc.Block(tag) as blk:
            @blk.gpsimd
            def _(g):
                for j in range(NJ):
                    g.dma_start(out=v_all[:, j * 1024:(j + 1) * 1024],
                                in_=v1loc[j * 128:(j + 1) * 128, :]).then_inc(vld.h, 16)
            vld.n += 16 * NJ

            g_free = {}
            use_pe = {}
            for cw in range(8):
                ev_prev = dv.n
                for j in range(NJ):
                    idx = cw * NJ + j
                    alc = ald[idx % 4]

                    @blk.sync
                    def _(sy, cw=cw, j=j, idx=idx, alc=alc):
                        if idx >= 4:
                            sy.wait_ge(pe.h, use_pe[idx - 4])
                        st, jl = j // 4, j % 4
                        if cc is not None:
                            sy.wait_ge(cc.h, ccbase + st + 1)
                        sy.dma_start(out=a_sb[idx % 4],
                                     in_=a_recv[st * S + cw * 512 + jl * 128:
                                                st * S + cw * 512 + (jl + 1) * 128, :])\
                          .then_inc(alc.h, 16)
                    alc.n += 16
                    afen = alc.n

                    @blk.tensor
                    def _(t, cw=cw, j=j, idx=idx, afen=afen, alc=alc, ev_prev=ev_prev):
                        t.wait_ge(alc.h, afen)
                        if cw == 0 and j == 0:
                            t.wait_ge(vld.h, vld.n)
                        if j == 0:
                            t.wait_ge(dv.h, ev_prev)
                        for isub in range(4):
                            for half in range(2):
                                ins = t.matmul(
                                    ppv[isub * 2 + half],
                                    a_sb[idx % 4][:, isub * 128:(isub + 1) * 128],
                                    v_all[:, j * 1024 + half * 512:
                                          j * 1024 + (half + 1) * 512],
                                    start=(j == 0), stop=(j == NJ - 1))
                        pe.inc(ins)
                    use_pe[idx] = pe.n
                pv_pe = pe.n

                for isub in range(4):
                    bsel = isub % 2
                    idx = cw * 4 + isub
                    row0 = cw * 512 + isub * 128

                    @blk.sync
                    def _(sy, idx=idx, bsel=bsel, row0=row0):
                        if idx >= 2:
                            sy.wait_ge(dv.h, g_free[idx - 2])
                        sy.dma_start(out=gt[bsel], in_=g1[row0:row0 + 128, :])\
                          .then_inc(gld[bsel].h, 16)
                        sy.dma_start(out=xt[bsel], in_=x1_ap[row0:row0 + 128, :])\
                          .then_inc(gld[bsel].h, 16)
                    gld[bsel].n += 32
                    gfen = gld[bsel].n

                    @blk.vector
                    def _(v, isub=isub, bsel=bsel, pv_pe=pv_pe, gfen=gfen):
                        v.wait_ge(pe.h, pv_pe)
                        v.wait_ge(gld[bsel].h, gfen)
                        v.wait_ge(yso[bsel].h, yso[bsel].n)
                        for half in range(2):
                            v.wait_ge(dv.h, dv.n)
                            dv.inc(v.tensor_mul(
                                yt[bsel][:, half * 512:(half + 1) * 512],
                                ppv[isub * 2 + half],
                                gt[bsel][:, half * 512:(half + 1) * 512]))
                        v.wait_ge(dv.h, dv.n)
                        dv.inc(v.tensor_add(yt[bsel], yt[bsel], xt[bsel]))
                    g_free[idx] = dv.n
                    ydv = dv.n

                    @blk.sync
                    def _(sy, bsel=bsel, row0=row0, ydv=ydv):
                        sy.wait_ge(dv.h, ydv)
                        sy.dma_start(out=y_ap[row0:row0 + 128, :], in_=yt[bsel])\
                          .then_inc(yso[bsel].h, 16)
                    yso[bsel].n += 16

            @blk.sync
            def _(sy):
                sy.wait_ge(yso[0].h, yso[0].n)
                sy.wait_ge(yso[1].h, yso[1].n)


RG = [list(range(NCORES))]


def fence_all(nc, deps, name):
    with nc.Block(name) as blk:
        for eng in ("sync", "gpsimd", "scalar", "vector", "tensor"):
            @getattr(blk, eng)
            def _(e, deps=deps):
                for (h, n) in deps:
                    e.wait_ge(h, n)


def emit_collectives(nc, ops, deps, cc, tag):
    with nc.Block(tag) as blk:
        @blk.gpsimd
        def _(g):
            for (h, n) in deps:
                g.wait_ge(h, n)
            with allow_ici_all_to_all():
                for kind, src, dst in ops:
                    g.collective_compute(kind, mybir.AluOpType.bypass,
                                         replica_groups=RG, ins=[src.opt()],
                                         outs=[dst.opt()]).then_inc(cc.h, 1)
    cc.n += len(ops)


def build_fused():
    nc = bass.Bass("TRN2", target_bir_lowering=False)
    DT = lambda n, sh, dt, kind: nc.dram_tensor(n, sh, dt, kind=kind).ap()
    x_ap = DT("x", [S, D], F32, "ExternalInput")
    wq = DT("wq", [D, 256], F32, "ExternalInput")
    wqc = DT("wqc", [D, 256], F32, "ExternalInput")
    wk = DT("wk", [D, 256], F32, "ExternalInput")
    wkc = DT("wkc", [D, 256], F32, "ExternalInput")
    wv = DT("wv", [D, D], F32, "ExternalInput")
    wvc = DT("wvc", [D, D], F32, "ExternalInput")
    bq = DT("bq", [256, 1], F32, "ExternalInput")
    bqc = DT("bqc", [256, 1], F32, "ExternalInput")
    bk = DT("bk", [256, 1], F32, "ExternalInput")
    bkc = DT("bkc", [256, 1], F32, "ExternalInput")
    bv = DT("bv", [1, D], F32, "ExternalInput")
    bvc = DT("bvc", [1, D], F32, "ExternalInput")
    wgs = DT("wgs", [D, 2048], F32, "ExternalInput")
    wus = DT("wus", [D, 2048], F32, "ExternalInput")
    wds = DT("wds", [2048, D], F32, "ExternalInput")
    wge = DT("wge", [8, D, 2048], F32, "ExternalInput")
    wue = DT("wue", [8, D, 2048], F32, "ExternalInput")
    wde = DT("wde", [8, 2048, D], F32, "ExternalInput")
    wr = DT("wr", [D, 8], F32, "ExternalInput")
    rb = DT("rb", [1, 8], F32, "ExternalInput")
    wkt = DT("wkt", [D, D], F32, "ExternalInput")
    wvt = DT("wvt", [D, D], F32, "ExternalInput")
    wqt = DT("wqt", [D, D], F32, "ExternalInput")
    mask = DT("mask", [S, QW], I8, "ExternalInput")
    sut = DT("sut", [2, D, 512], F32, "ExternalInput")
    sdt = DT("sdt", [2, 512, D], F32, "ExternalInput")
    rut = DT("rut", [16, D, 512], F32, "ExternalInput")
    rdt = DT("rdt", [16, 512, D], F32, "ExternalInput")
    rwt = DT("rwt", [D, 16], F32, "ExternalInput")
    y_ap = DT("y", [S, D], F32, "ExternalOutput")

    xnf = DT("xnf", [D, SP], BF16, "Internal")
    s_f = DT("s_f", [D, S], F32, "Internal")
    x1 = DT("x1", [S, D], F32, "Internal")
    xn1f = DT("xn1f", [D, S], BF16, "Internal")
    xn1send = DT("xn1send", [8 * D, QW], BF16, "Internal")
    xn1q = DT("xn1q", [8 * D, QW], BF16, "Internal")
    k1loc = DT("k1loc", [S, D], BF16, "Internal")
    k1all = DT("k1all", [B * S, D], BF16, "Internal")
    v1loc = DT("v1loc", [S, D], BF16, "Internal")
    a_send = DT("a_send", [B * S, QW], BF16, "Internal")
    a_recv = DT("a_recv", [B * S, QW], BF16, "Internal")
    g1 = DT("g1", [S, D], BF16, "Internal")

    cc = Ctr(nc, "cc")

    f1 = emit_ln(nc, x_ap, xnf, "ln0", SP)
    fence_all(nc, f1, "fn1")
    f2 = emit_l0attn(nc, xnf, (wq, wqc, wk, wkc, wv, wvc),
                     (bq, bqc, bk, bkc, bv, bvc), s_f, "a0")
    fence_all(nc, f2, "fn2")
    f3 = emit_l0ffn(nc, xnf, s_f, x_ap, (wgs, wus, wds, wge, wue, wde, wr, rb),
                    x1, xn1f, "f0", xn1send_ap=xn1send)
    f4 = emit_kvproj(nc, xn1f, wkt, wvt, k1loc, v1loc, "kv")
    fence_all(nc, f3 + f4, "fn3")
    emit_collectives(nc, [("AllToAll", xn1send, xn1q),
                          ("AllGather", k1loc, k1all)], [], cc, "cc1")
    f5 = emit_l1ffn_g(nc, xn1f, (sut, sdt, rut, rdt, rwt), g1, "e2")
    fence_all(nc, f5 + [(cc.h, cc.n)], "fn4")
    f6 = emit_l1scores(nc, xn1q, wqt, k1all, mask, a_send, "e1",
                       a_recv=a_recv, cc=cc)
    fence_all(nc, f6, "fn5")
    emit_pv_final(nc, a_recv, v1loc, g1, x1, y_ap, "pf", cc=cc, ccbase=2)
    return nc


_FUSED = None


def _fused_prog():
    global _FUSED
    if _FUSED is None:
        _FUSED = build_fused()
    return _FUSED


def kernel(**inputs):
    T = lambda a: np.ascontiguousarray(np.asarray(a).swapaxes(-1, -2))
    C = lambda a: np.ascontiguousarray(np.asarray(a))
    x = np.asarray(inputs["L0_attn_x"])

    base = {
        "wq": C(inputs["L0_attn_Wq"]), "wqc": C(inputs["L0_attn_Wqc"]),
        "wk": C(inputs["L0_attn_Wk"]), "wkc": C(inputs["L0_attn_Wkc"]),
        "wv": C(inputs["L0_attn_Wv"]), "wvc": C(inputs["L0_attn_Wvc"]),
        "bq": np.asarray(inputs["L0_attn_bq"]).reshape(256, 1).copy(),
        "bqc": np.asarray(inputs["L0_attn_bqc"]).reshape(256, 1).copy(),
        "bk": np.asarray(inputs["L0_attn_bk"]).reshape(256, 1).copy(),
        "bkc": np.asarray(inputs["L0_attn_bkc"]).reshape(256, 1).copy(),
        "bv": np.asarray(inputs["L0_attn_bv"]).reshape(1, D).copy(),
        "bvc": np.asarray(inputs["L0_attn_bvc"]).reshape(1, D).copy(),
        "wgs": T(inputs["L0_ffn_Wg_s"]), "wus": T(inputs["L0_ffn_Wu_s"]),
        "wds": T(inputs["L0_ffn_Wd_s"]),
        "wge": T(inputs["L0_ffn_Wg"]), "wue": T(inputs["L0_ffn_Wu"]),
        "wde": T(inputs["L0_ffn_Wd"]),
        "wr": T(inputs["L0_ffn_Wr"]),
        "rb": np.asarray(inputs["L0_ffn_rb"]).reshape(1, 8).copy(),
        "wkt": T(inputs["L1_attn_Wk"]), "wvt": T(inputs["L1_attn_Wv"]),
        "wqt": T(inputs["L1_attn_Wq"]),
        "sut": T(inputs["L1_ffn_shared_up"]), "sdt": T(inputs["L1_ffn_shared_down"]),
        "rut": T(inputs["L1_ffn_routed_up"]), "rdt": T(inputs["L1_ffn_routed_down"]),
        "rwt": T(inputs["L1_ffn_router_w"]),
    }
    jj = np.arange(S)[:, None]
    ii = np.arange(QW)[None, :]
    in_maps = []
    for c in range(NCORES):
        maskc = (jj <= c * QW + ii).astype(np.int8)
        in_maps.append(dict(base, x=C(x[c]), mask=maskc))
    res = run_bass_kernel_spmd(_fused_prog(), in_maps, core_ids=list(range(NCORES)))
    return np.stack([res.results[c]["y"] for c in range(NCORES)])


# version 09
# speedup vs baseline: 1.1615x; 1.0285x over previous
import numpy as np
import ml_dtypes
from contextlib import ExitStack

import concourse.bass as bass
import concourse.mybir as mybir
from concourse.bass_utils import run_bass_kernel_spmd
from concourse.aot_env import allow_ici_all_to_all

F32 = mybir.dt.float32
BF16 = mybir.dt.bfloat16
I8 = mybir.dt.int8
AF = mybir.ActivationFunctionType
OP = mybir.AluOpType
AX = mybir.AxisListType.X

B = 8
S = 4096
SP = 4112
D = 1024
NCORES = 8
QW = S // NCORES
LN_EPS = 1e-5


class Ctr:
    def __init__(self, nc, name):
        self.h = nc.alloc_semaphore(name=name)
        self.n = 0

    def inc(self, inst, k=1):
        inst.then_inc(self.h, k)
        self.n += k
        return self.n


def _mk_identity(g, ap, ctr):
    ctr.inc(g.memset(ap, 0.0))
    g.wait_ge(ctr.h, ctr.n)
    return g.affine_select(out=ap, in_=ap, compare_op=OP.not_equal, fill=1.0,
                           base=0, pattern=[[-1, ap.shape[1]]], channel_multiplier=1)


def build_ln0():
    nc = bass.Bass("TRN2", target_bir_lowering=False)
    x_ap = nc.dram_tensor("x", [S, D], F32, kind="ExternalInput").ap()
    xnf_ap = nc.dram_tensor("xnf", [D, SP], BF16, kind="ExternalOutput").ap()
    emit_ln(nc, x_ap, xnf_ap, "ln0", SP)
    return nc


def emit_ln(nc, x_ap, xnf_ap, tag, tcols):
    NT = S // 128
    with ExitStack() as es:
        sb = lambda n, sh, dt: es.enter_context(nc.sbuf_tensor(tag + n, sh, dt)).ap()
        ps = lambda n, sh, dt: es.enter_context(nc.psum_tensor(tag + n, sh, dt)).ap()
        xt = [sb(f"xt{i}", [128, D], F32) for i in range(3)]
        sq = sb("sq", [128, D], F32)
        stt = [sb(f"stt{i}", [128, 8], F32) for i in range(3)]
        epst = sb("epst", [128, 1], F32)
        xn = [sb(f"xn{i}", [128, D], BF16) for i in range(2)]
        ptr = [ps(f"ptr{i}", [128, D], BF16) for i in range(2)]
        stw = [sb(f"stw{i}", [128, 8 * 1024], BF16) for i in range(2)]
        zpad = sb("zpad", [128, 16], BF16)
        ident = sb("id", [128, 128], BF16)

        lds = [Ctr(nc, tag + f"ld{i}") for i in range(3)]
        ac = Ctr(nc, tag + "ac")
        dv = Ctr(nc, tag + "dv")
        pe = Ctr(nc, tag + "pe")
        ev = Ctr(nc, tag + "ev")
        sos = [Ctr(nc, tag + f"so{i}") for i in range(2)]
        sop = Ctr(nc, tag + "sop")
        gi_ = Ctr(nc, tag + "gi")

        xn_done = {}
        stage_done = {}
        tr_fence = {}
        out_done_grp = {}

        with nc.Block(tag) as blk:
            @blk.gpsimd
            def _(g):
                g.memset(epst, LN_EPS)
                g.memset(zpad, 0.0)
                _mk_identity(g, ident, gi_).then_inc(gi_.h, 1)
            gi_.n += 1
            gi_all = gi_.n

            if tcols > S:
                @blk.sync
                def _(sy):
                    sy.wait_ge(gi_.h, gi_all)
                    for j in range(8):
                        sy.dma_start(out=xnf_ap[j * 128:(j + 1) * 128, S:tcols],
                                     in_=zpad).then_inc(sop.h, 16)
                sop.n += 16 * 8

            for i in range(NT):
                ldc = lds[i % 3]

                @blk.sync
                def _(sy, i=i, ldc=ldc):
                    if i >= 3:
                        sy.wait_ge(dv.h, xn_done[i - 3])
                    sy.dma_start(out=xt[i % 3],
                                 in_=x_ap[i * 128:(i + 1) * 128, :]).then_inc(ldc.h, 16)
                ldc.n += 16
                ld_i = ldc.n
                b = stt[i % 3]

                @blk.scalar
                def _(sc, i=i, b=b, ld_i=ld_i, ldc=ldc):
                    sc.wait_ge(ldc.h, ld_i)
                    if i >= 3:
                        sc.wait_ge(dv.h, xn_done[i - 3])
                    ac.inc(sc.activation(out=sq, in_=xt[i % 3], func=AF.Square,
                                         accum_out=b[:, 1:2]))
                s2_ready = ac.n

                @blk.vector
                def _(v, i=i, b=b, ld_i=ld_i, s2=s2_ready, ldc=ldc):
                    v.wait_ge(ldc.h, ld_i)
                    if i >= 3:
                        v.wait_ge(dv.h, xn_done[i - 3])
                    dv.inc(v.reduce_sum(b[:, 0:1], xt[i % 3], axis=AX))
                    v.wait_ge(dv.h, dv.n)
                    dv.inc(v.tensor_scalar(out=b[:, 2:3], in0=b[:, 0:1],
                                           scalar1=1.0 / D, scalar2=None,
                                           op0=OP.mult))
                    v.wait_ge(dv.h, dv.n)
                    dv.inc(v.scalar_tensor_tensor(out=b[:, 5:6], in0=b[:, 2:3],
                                                  scalar=1.0, in1=b[:, 2:3],
                                                  op0=OP.mult, op1=OP.mult))
                    v.wait_ge(ac.h, s2)
                    v.wait_ge(dv.h, dv.n)
                    dv.inc(v.scalar_tensor_tensor(out=b[:, 3:4], in0=b[:, 1:2],
                                                  scalar=1.0 / D, in1=b[:, 5:6],
                                                  op0=OP.mult, op1=OP.subtract))
                var_ready = dv.n

                @blk.scalar
                def _(sc, b=b, var_ready=var_ready):
                    sc.wait_ge(dv.h, var_ready)
                    ac.inc(sc.activation(out=b[:, 4:5], in_=b[:, 3:4], func=AF.Sqrt,
                                         bias=epst[:, 0:1]))
                std_ready = ac.n

                @blk.vector
                def _(v, i=i, b=b, std_ready=std_ready):
                    v.wait_ge(ac.h, std_ready)
                    dv.inc(v.reciprocal(b[:, 5:6], b[:, 4:5]))
                    v.wait_ge(dv.h, dv.n)
                    if i >= 2:
                        v.wait_ge(pe.h, tr_fence[i - 2])
                    dv.inc(v.tensor_scalar(out=xn[i % 2], in0=xt[i % 3],
                                           scalar1=b[:, 2:3], scalar2=b[:, 5:6],
                                           op0=OP.subtract, op1=OP.mult))
                xn_done[i] = dv.n
                xnr = dv.n

                @blk.tensor
                def _(t, i=i, xnr=xnr):
                    t.wait_ge(gi_.h, gi_all)
                    t.wait_ge(dv.h, xnr)
                    if i >= 2:
                        t.wait_ge(ev.h, stage_done[i - 2])
                    for j in range(8):
                        ins = t.transpose(ptr[i % 2][:, j * 128:(j + 1) * 128],
                                          xn[i % 2][:, j * 128:(j + 1) * 128], ident)
                    pe.inc(ins)
                tr_done = pe.n
                tr_fence[i] = tr_done
                g = i // 8
                ti = i % 8

                @blk.vector
                def _(v, i=i, g=g, ti=ti, tr=tr_done):
                    v.wait_ge(pe.h, tr)
                    if g >= 2 and ti == 0:
                        v.wait_ge(sos[g % 2].h, out_done_grp[g - 2])
                    ev.inc(v.tensor_copy(
                        stw[g % 2].rearrange("p (j tb t) -> p j tb t",
                                             j=8, tb=8)[:, :, ti, :],
                        ptr[i % 2].rearrange("p (j t) -> p j t", j=8)))
                stage_done[i] = ev.n
                stg = ev.n

                if ti == 7:
                    @blk.sync
                    def _(sy, g=g, stg=stg):
                        sy.wait_ge(ev.h, stg)
                        for j in range(8):
                            sy.dma_start(
                                out=xnf_ap[j * 128:(j + 1) * 128,
                                           g * 1024:(g + 1) * 1024],
                                in_=stw[g % 2][:, j * 1024:(j + 1) * 1024])\
                              .then_inc(sos[g % 2].h, 16)
                    sos[g % 2].n += 16 * 8
                    out_done_grp[g] = sos[g % 2].n

            @blk.sync
            def _(sy):
                sy.wait_ge(sos[0].h, sos[0].n)
                sy.wait_ge(sos[1].h, sos[1].n)
                if tcols > S:
                    sy.wait_ge(sop.h, sop.n)
        fences = [(sos[0].h, sos[0].n), (sos[1].h, sos[1].n)]
        if tcols > S:
            fences.append((sop.h, sop.n))
        return fences


def build_l0attn(debug_qkv=False):
    nc = bass.Bass("TRN2", target_bir_lowering=False)
    xnf = nc.dram_tensor("xnf", [D, SP], BF16, kind="ExternalInput").ap()
    wq = nc.dram_tensor("wq", [D, 256], F32, kind="ExternalInput").ap()
    wqc = nc.dram_tensor("wqc", [D, 256], F32, kind="ExternalInput").ap()
    wk = nc.dram_tensor("wk", [D, 256], F32, kind="ExternalInput").ap()
    wkc = nc.dram_tensor("wkc", [D, 256], F32, kind="ExternalInput").ap()
    wv = nc.dram_tensor("wv", [D, D], F32, kind="ExternalInput").ap()
    wvc = nc.dram_tensor("wvc", [D, D], F32, kind="ExternalInput").ap()
    bq = nc.dram_tensor("bq", [256, 1], F32, kind="ExternalInput").ap()
    bqc = nc.dram_tensor("bqc", [256, 1], F32, kind="ExternalInput").ap()
    bk = nc.dram_tensor("bk", [256, 1], F32, kind="ExternalInput").ap()
    bkc = nc.dram_tensor("bkc", [256, 1], F32, kind="ExternalInput").ap()
    bv = nc.dram_tensor("bv", [1, D], F32, kind="ExternalInput").ap()
    bvc = nc.dram_tensor("bvc", [1, D], F32, kind="ExternalInput").ap()
    s_f = nc.dram_tensor("s_f", [D, S], F32, kind="ExternalOutput").ap()
    dbg = None
    if debug_qkv:
        dbg = (nc.dram_tensor("dbg_q", [128, 2 * S], BF16, kind="ExternalOutput").ap(),
               nc.dram_tensor("dbg_k", [128, 2 * SP], BF16, kind="ExternalOutput").ap(),
               nc.dram_tensor("dbg_v", [128, 33 * 1040], BF16, kind="ExternalOutput").ap())
    emit_l0attn(nc, xnf, (wq, wqc, wk, wkc, wv, wvc),
                (bq, bqc, bk, bkc, bv, bvc), s_f, "a0", dbg=dbg)
    return nc


def emit_l0attn(nc, xnf, ws, bs, s_f, tag, dbg=None):
    wq, wqc, wk, wkc, wv, wvc = ws
    bq, bqc, bk, bkc, bv, bvc = bs
    NH = 16
    QL = 16
    VL = 64
    NKC = SP // 128
    KCW = [128] * 32 + [16]
    GRP3 = [(3 * g, min(3, 33 - 3 * g)) for g in range((33 + 2) // 3)]

    with ExitStack() as es:
        sb = lambda n, sh, dt: es.enter_context(nc.sbuf_tensor(tag + n, sh, dt)).ap()
        ps = lambda n, sh, dt: es.enter_context(nc.psum_tensor(tag + n, sh, dt)).ap()
        q_sb = sb("q", [128, 2 * S], BF16)
        k_sb = sb("k", [128, 2 * SP], BF16)
        v_sb = sb("v", [128, 33 * 1040], BF16)

        ac = Ctr(nc, tag + "ac")
        dv = Ctr(nc, tag + "dv")
        pe = Ctr(nc, tag + "pe")
        so = Ctr(nc, tag + "so")

        with ExitStack() as es2:
            sb2 = lambda n, sh, dt: es2.enter_context(nc.sbuf_tensor(tag + n, sh, dt)).ap()
            xnc = [sb2(f"xnc{i}", [128, 8 * 512], BF16) for i in range(2)]
            wq_sb = sb2("wqs", [128, 8 * 256], BF16)
            wqc_sb = sb2("wqcs", [128, 8 * 256], BF16)
            wk_sb = sb2("wks", [128, 8 * 256], BF16)
            wkc_sb = sb2("wkcs", [128, 8 * 256], BF16)
            wv_sb = sb2("wvs", [128, 8 * 1024], BF16)
            wvc_sb = sb2("wvcs", [128, 8 * 1024], BF16)
            bq_sb = sb2("bqs", [128, 2], F32)
            bqc_sb = sb2("bqcs", [128, 2], F32)
            bk_sb = sb2("bks", [128, 2], F32)
            bkc_sb = sb2("bkcs", [128, 2], F32)
            brow = sb2("brow", [1, 2 * D], F32)
            ones1 = sb2("ones1", [1, 128], F32)
            bvb = sb2("bvb", [128, D], F32)
            bvcb = sb2("bvcb", [128, D], F32)
            tmp1 = sb2("tmp1", [128, 512], F32)
            tmp2 = sb2("tmp2", [128, 512], F32)
            tmp3 = sb2("tmp3", [128, 512], F32)
            ps2 = lambda n, sh, dt: es2.enter_context(nc.psum_tensor(tag + n, sh, dt)).ap()
            pql = ps2("pql", [128, 512], F32)
            pqg = ps2("pqg", [128, 512], F32)
            pvl = ps2("pvl", [128, 512], F32)
            pvg = ps2("pvg", [128, 512], F32)
            pbc = ps2("pbc", [128, 1024], F32)

            wgld = Ctr(nc, tag + "wgld")
            gmk = Ctr(nc, tag + "gmk")

            @_one_block(nc, tag + "prep")
            def _(blk):

                @blk.gpsimd
                def _(g):
                    g.memset(ones1, 1.0).then_inc(gmk.h, 1)
                    g.memset(v_sb[:, :], 0.0).then_inc(gmk.h, 1)
                    g.wait_ge(gmk.h, 2)
                    for kc in range(33):
                        ap = v_sb[:, kc * 1040:(kc + 1) * 1040]\
                            .rearrange("p (h c) -> p h c", h=16, c=65)[:, :, 64:65]
                        ins = g.memset(ap, 1.0)
                    ins.then_inc(gmk.h, 1)
                    for hc in range(8):
                        g.dma_start(out=wq_sb[:, hc * 256:(hc + 1) * 256],
                                    in_=wq[hc * 128:(hc + 1) * 128, :]).then_inc(wgld.h, 16)
                        g.dma_start(out=wqc_sb[:, hc * 256:(hc + 1) * 256],
                                    in_=wqc[hc * 128:(hc + 1) * 128, :]).then_inc(wgld.h, 16)
                        g.dma_start(out=wk_sb[:, hc * 256:(hc + 1) * 256],
                                    in_=wk[hc * 128:(hc + 1) * 128, :]).then_inc(wgld.h, 16)
                        g.dma_start(out=wkc_sb[:, hc * 256:(hc + 1) * 256],
                                    in_=wkc[hc * 128:(hc + 1) * 128, :]).then_inc(wgld.h, 16)
                        g.dma_start(out=wv_sb[:, hc * 1024:(hc + 1) * 1024],
                                    in_=wv[hc * 128:(hc + 1) * 128, :]).then_inc(wgld.h, 16)
                        g.dma_start(out=wvc_sb[:, hc * 1024:(hc + 1) * 1024],
                                    in_=wvc[hc * 128:(hc + 1) * 128, :]).then_inc(wgld.h, 16)
                wgld.n += 16 * 48
                gmk.n += 3

                bld = Ctr(nc, tag + "bld")

                @blk.sync
                def _(sy):
                    for name, srcw, dst in [("bq", bq, bq_sb), ("bqc", bqc, bqc_sb),
                                            ("bk", bk, bk_sb), ("bkc", bkc, bkc_sb)]:
                        for oc in range(2):
                            sy.dma_start(out=dst[:, oc:oc + 1],
                                         in_=srcw[oc * 128:(oc + 1) * 128, :])\
                              .then_inc(bld.h, 16)
                    sy.dma_start(out=brow[0:1, 0:D], in_=bv).then_inc(bld.h, 16)
                    sy.dma_start(out=brow[0:1, D:2 * D], in_=bvc).then_inc(bld.h, 16)
                bld.n += 16 * 10
                wld = bld.n

                @blk.tensor
                def _(t, wld=wld):
                    t.wait_ge(bld.h, wld)
                    t.wait_ge(gmk.h, 1)
                    t.matmul(pbc[:, 0:512], ones1[0:1, :], brow[0:1, 0:512],
                             start=True, stop=True)
                    pe.inc(t.matmul(pbc[:, 512:1024], ones1[0:1, :],
                                    brow[0:1, 512:1024], start=True, stop=True))
                bv_pe = pe.n

                @blk.vector
                def _(v, bv_pe=bv_pe):
                    v.wait_ge(pe.h, bv_pe)
                    dv.inc(v.tensor_copy(bvb, pbc[:, 0:1024]))
                bvb_dv = dv.n

                @blk.tensor
                def _(t, bvb_dv=bvb_dv):
                    t.wait_ge(dv.h, bvb_dv)
                    t.matmul(pbc[:, 0:512], ones1[0:1, :], brow[0:1, D:D + 512],
                             start=True, stop=True)
                    pe.inc(t.matmul(pbc[:, 512:1024], ones1[0:1, :],
                                    brow[0:1, D + 512:2 * D], start=True, stop=True))
                bvc_pe = pe.n

                @blk.vector
                def _(v, bvc_pe=bvc_pe):
                    v.wait_ge(pe.h, bvc_pe)
                    dv.inc(v.tensor_copy(bvcb, pbc[:, 0:1024]))
                bvcb_dv = dv.n

                NTT = 9
                xlds = [Ctr(nc, tag + f"xld{i}") for i in range(2)]
                evac_done = {}
                for tt in range(NTT):
                    t0 = tt * 512
                    tw = min(512, SP - t0)
                    xldc = xlds[tt % 2]

                    @blk.sync
                    def _(sy, tt=tt, t0=t0, tw=tw, xldc=xldc):
                        if tt >= 2:
                            sy.wait_ge(dv.h, evac_done[tt - 2])
                        for hc in range(8):
                            sy.dma_start(out=xnc[tt % 2][:, hc * 512:hc * 512 + tw],
                                         in_=xnf[hc * 128:(hc + 1) * 128, t0:t0 + tw])\
                              .then_inc(xldc.h, 16)
                    xldc.n += 16 * 8
                    xl = xldc.n

                    for kind, wlin, wgate, blin, bgate, dst, dw in (
                            [("q", wq_sb, wqc_sb, bq_sb, bqc_sb, q_sb, S)] if tt < 8 else []
                    ) + [("k", wk_sb, wkc_sb, bk_sb, bkc_sb, k_sb, SP)]:
                        for oc in range(2):
                            @blk.tensor
                            def _(t, tt=tt, tw=tw, oc=oc, wlin=wlin, wgate=wgate, xl=xl, xldc=xldc):
                                t.wait_ge(xldc.h, xl)
                                t.wait_ge(wgld.h, wgld.n)
                                t.wait_ge(ac.h, ac.n)
                                t.wait_ge(dv.h, dv.n)
                                for hc in range(8):
                                    t.matmul(pql[:, 0:tw],
                                             wlin[:, hc * 256 + oc * 128: hc * 256 + (oc + 1) * 128],
                                             xnc[tt % 2][:, hc * 512:hc * 512 + tw],
                                             start=(hc == 0), stop=(hc == 7))
                                for hc in range(8):
                                    ins = t.matmul(pqg[:, 0:tw],
                                                   wgate[:, hc * 256 + oc * 128: hc * 256 + (oc + 1) * 128],
                                                   xnc[tt % 2][:, hc * 512:hc * 512 + tw],
                                                   start=(hc == 0), stop=(hc == 7))
                                pe.inc(ins)
                            mm = pe.n

                            @blk.scalar
                            def _(sc, tw=tw, oc=oc, bgate=bgate, mm=mm):
                                sc.wait_ge(pe.h, mm)
                                sc.wait_ge(dv.h, dv.n)
                                ac.inc(sc.activation(out=tmp2[:, 0:tw], in_=pqg[:, 0:tw],
                                                     func=AF.Sigmoid,
                                                     bias=bgate[:, oc:oc + 1]))
                            gq = ac.n

                            @blk.vector
                            def _(v, t0=t0, tw=tw, oc=oc, blin=blin, dst=dst, dw=dw, gq=gq):
                                v.wait_ge(ac.h, gq)
                                dv.inc(v.scalar_tensor_tensor(
                                    out=dst[:, oc * dw + t0: oc * dw + t0 + tw],
                                    in0=pql[:, 0:tw], scalar=blin[:, oc:oc + 1],
                                    in1=tmp2[:, 0:tw], op0=OP.add, op1=OP.mult))

                    nsub = (tw + 127) // 128
                    for su in range(nsub):
                        stw = min(128, tw - su * 128)
                        kc = (t0 + su * 128) // 128
                        for half in range(2):
                            @blk.tensor
                            def _(t, tt=tt, su=su, stw=stw, half=half, xl=xl, xldc=xldc):
                                t.wait_ge(xldc.h, xl)
                                t.wait_ge(wgld.h, wgld.n)
                                t.wait_ge(dv.h, dv.n)
                                for hc in range(8):
                                    t.matmul(pvl[0:stw, :],
                                             xnc[tt % 2][:, hc * 512 + su * 128: hc * 512 + su * 128 + stw],
                                             wv_sb[:, hc * 1024 + half * 512: hc * 1024 + (half + 1) * 512],
                                             start=(hc == 0), stop=(hc == 7))
                                    ins = t.matmul(pvg[0:stw, :],
                                                   xnc[tt % 2][:, hc * 512 + su * 128: hc * 512 + su * 128 + stw],
                                                   wvc_sb[:, hc * 1024 + half * 512: hc * 1024 + (half + 1) * 512],
                                                   start=(hc == 0), stop=(hc == 7))
                                pe.inc(ins)
                            vm = pe.n

                            @blk.vector
                            def _(v, stw=stw, half=half, vm=vm):
                                v.wait_ge(pe.h, vm)
                                dv.inc(v.tensor_add(tmp1[0:stw, :], pvg[0:stw, :],
                                                    bvcb[0:stw, half * 512:(half + 1) * 512]))
                            g1 = dv.n

                            @blk.scalar
                            def _(sc, stw=stw, g1=g1):
                                sc.wait_ge(dv.h, dv.n)
                                ac.inc(sc.activation(out=tmp2[0:stw, :], in_=tmp1[0:stw, :],
                                                     func=AF.Sigmoid))
                            g2 = ac.n

                            @blk.vector
                            def _(v, stw=stw, half=half, kc=kc, g2=g2):
                                v.wait_ge(gmk.h, 3)
                                dv.inc(v.tensor_add(tmp3[0:stw, :], pvl[0:stw, :],
                                                    bvb[0:stw, half * 512:(half + 1) * 512]))
                                v.wait_ge(ac.h, g2)
                                v.wait_ge(dv.h, dv.n)
                                dst = v_sb[0:stw, kc * 1040:(kc + 1) * 1040]\
                                    .rearrange("p (h c) -> p h c", h=16, c=65)\
                                    [:, half * 8:(half + 1) * 8, 0:64]
                                src = tmp3[0:stw, :].rearrange("p (h c) -> p h c", h=8, c=64)
                                dv.inc(v.tensor_mul(dst, src,
                                    tmp2[0:stw, :].rearrange("p (h c) -> p h c", h=8, c=64)))
                    evac_done[tt] = dv.n

        if dbg is not None:
            dso = Ctr(nc, tag + "dso")
            with nc.Block(tag + "dbg") as blk:
                @blk.sync
                def _(sy):
                    sy.dma_start(out=dbg[0], in_=q_sb[:, :]).then_inc(dso.h, 16)
                    sy.dma_start(out=dbg[1], in_=k_sb[:, :]).then_inc(dso.h, 16)
                    sy.dma_start(out=dbg[2], in_=v_sb[:, :]).then_inc(dso.h, 16)
                    sy.wait_ge(dso.h, 48)
            return

        prep_dv = dv.n
        prep_ac = ac.n

        with ExitStack() as es3:
            sb3 = lambda n, sh, dt: es3.enter_context(nc.sbuf_tensor(tag + n, sh, dt)).ap()
            krep = sb3("krep", [128, SP], BF16)
            qrep = sb3("qrep", [128, S], BF16)
            e_sb = [sb3(f"e{i}", [128, 1536], BF16) for i in range(2)]
            den = sb3("den", [1, 512], F32)
            rden = sb3("rden", [1, 512], F32)
            bc_sb = sb3("bc", [64, 512], F32)
            sstg = sb3("sstg", [64, 512], F32)
            ones64 = sb3("o64", [1, 64], F32)
            ps3 = lambda n, sh, dt: es3.enter_context(nc.psum_tensor(tag + n, sh, dt)).ap()
            ppa = ps3("ppa", [128, 1536], F32)
            ppb = ps3("ppb", [128, 1536], F32)
            ppv = ps3("ppv", [65, 512], F32)
            pbc2 = ps3("pbc2", [64, 512], F32)
            pp = [ppa, ppb]

            rep = Ctr(nc, tag + "rep")

            @_one_block(nc, tag + "main")
            def _(blk):
                mk = Ctr(nc, tag + "mk")

                @blk.gpsimd
                def _(g):
                    g.memset(ones64, 1.0).then_inc(mk.h, 1)
                mk.n += 1

                for h in range(NH):
                    oc, row = h // 8, (h % 8) * 16

                    @blk.gpsimd
                    def _(g, oc=oc, row=row):
                        g.wait_ge(dv.h, prep_dv)
                        g.wait_ge(so.h, so.n)
                        for i in range(3):
                            g.dma_start(out=krep[32 * i:32 * i + 16, :],
                                        in_=k_sb[row:row + 16, oc * SP:(oc + 1) * SP])\
                                .then_inc(rep.h, 16)
                            g.dma_start(out=qrep[32 * i:32 * i + 16, :],
                                        in_=q_sb[row:row + 16, oc * S:(oc + 1) * S])\
                                .then_inc(rep.h, 16)
                    rep.n += 16 * 6
                    repd = rep.n

                    for tq in range(8):
                        ngrp = len(GRP3)
                        exp_done = {}
                        pv_done_grp = {}
                        qk_pe = {}
                        for gi in range(ngrp + 1):
                            if gi < ngrp:
                                c0, gw = GRP3[gi]

                                @blk.tensor
                                def _(t, h=h, tq=tq, gi=gi, c0=c0, gw=gw, repd=repd):
                                    t.wait_ge(rep.h, repd)
                                    if gi >= 2:
                                        t.wait_ge(ac.h, exp_done[gi - 2])
                                    for i in range(gw):
                                        kc = c0 + i
                                        kw = KCW[kc]
                                        ins = t.matmul(
                                            pp[gi % 2][0:kw, i * 512:(i + 1) * 512],
                                            krep[32 * i:32 * i + 16, kc * 128:kc * 128 + kw],
                                            qrep[32 * i:32 * i + 16, tq * 512:(tq + 1) * 512],
                                            start=True, stop=True,
                                            tile_position=(32 * i, 0))
                                    pe.inc(ins)
                                qk_pe[gi] = pe.n

                            if gi >= 1:
                                gj = gi - 1
                                c0j, gwj = GRP3[gj]

                                @blk.tensor
                                def _(t, h=h, gj=gj, c0j=c0j, gwj=gwj):
                                    t.wait_ge(ac.h, exp_done[gj])
                                    if gj == 0:
                                        t.wait_ge(gmk.h, 3)
                                        t.wait_ge(dv.h, dv.n)
                                    for i in range(gwj):
                                        kc = c0j + i
                                        kw = KCW[kc]
                                        ins = t.matmul(
                                            ppv[:, :],
                                            v_sb[0:kw, kc * 1040 + 65 * h: kc * 1040 + 65 * h + 65],
                                            e_sb[gj % 2][0:kw, i * 512:(i + 1) * 512],
                                            start=(gj == 0 and i == 0),
                                            stop=(gj == ngrp - 1 and i == gwj - 1))
                                    pe.inc(ins)
                                pv_done_grp[gj] = pe.n

                            if gi < ngrp:
                                c0, gw = GRP3[gi]
                                qk = qk_pe[gi]

                                @blk.scalar
                                def _(sc, gi=gi, c0=c0, gw=gw, qk=qk):
                                    sc.wait_ge(pe.h, qk)
                                    if gi >= 2:
                                        sc.wait_ge(pe.h, pv_done_grp.get(gi - 2, 0))
                                    ac.inc(sc.activation(out=e_sb[gi % 2][:, 0:gw * 512],
                                                         in_=pp[gi % 2][:, 0:gw * 512],
                                                         func=AF.Exp, scale=0.125))
                                exp_done[gi] = ac.n
                        pvd = pe.n

                        sworn = so.n
                        @blk.vector
                        def _(v, pvd=pvd, sworn=sworn):
                            v.wait_ge(pe.h, pvd)
                            v.wait_ge(so.h, sworn)
                            dv.inc(v.tensor_copy(den, ppv[64:65, :]))
                            v.wait_ge(dv.h, dv.n)
                            dv.inc(v.reciprocal(rden, den))
                        rd = dv.n

                        @blk.tensor
                        def _(t, rd=rd):
                            t.wait_ge(mk.h, 1)
                            t.wait_ge(dv.h, rd)
                            pe.inc(t.matmul(pbc2, ones64, rden, start=True, stop=True))
                        bcd = pe.n

                        @blk.vector
                        def _(v, bcd=bcd):
                            v.wait_ge(pe.h, bcd)
                            dv.inc(v.tensor_copy(bc_sb, pbc2))
                            v.wait_ge(dv.h, dv.n)
                            dv.inc(v.tensor_mul(sstg, ppv[0:64, :], bc_sb))
                        sd = dv.n

                        @blk.sync
                        def _(sy, h=h, tq=tq, sd=sd):
                            sy.wait_ge(dv.h, sd)
                            sy.dma_start(out=s_f[h * 64:(h + 1) * 64,
                                                 tq * 512:(tq + 1) * 512],
                                         in_=sstg).then_inc(so.h, 16)
                        so.n += 16

                @blk.sync
                def _(sy):
                    sy.wait_ge(so.h, so.n)
    return [(so.h, so.n)]


def build_l0ffn():
    nc = bass.Bass("TRN2", target_bir_lowering=False)
    xnf = nc.dram_tensor("xnf", [D, SP], BF16, kind="ExternalInput").ap()
    s_f = nc.dram_tensor("s_f", [D, S], F32, kind="ExternalInput").ap()
    x_ap = nc.dram_tensor("x", [S, D], F32, kind="ExternalInput").ap()
    wgs = nc.dram_tensor("wgs", [D, 2048], F32, kind="ExternalInput").ap()
    wus = nc.dram_tensor("wus", [D, 2048], F32, kind="ExternalInput").ap()
    wds = nc.dram_tensor("wds", [2048, D], F32, kind="ExternalInput").ap()
    wge = nc.dram_tensor("wge", [8, D, 2048], F32, kind="ExternalInput").ap()
    wue = nc.dram_tensor("wue", [8, D, 2048], F32, kind="ExternalInput").ap()
    wde = nc.dram_tensor("wde", [8, 2048, D], F32, kind="ExternalInput").ap()
    wr = nc.dram_tensor("wr", [D, 8], F32, kind="ExternalInput").ap()
    rb = nc.dram_tensor("rb", [1, 8], F32, kind="ExternalInput").ap()
    wkt = nc.dram_tensor("wkt", [D, D], F32, kind="ExternalInput").ap()
    wvt = nc.dram_tensor("wvt", [D, D], F32, kind="ExternalInput").ap()
    x1_ap = nc.dram_tensor("x1", [S, D], F32, kind="ExternalOutput").ap()
    xn1f_ap = nc.dram_tensor("xn1f", [D, S], BF16, kind="ExternalOutput").ap()
    k1_ap = nc.dram_tensor("k1", [S, D], BF16, kind="ExternalOutput").ap()
    v1_ap = nc.dram_tensor("v1", [S, D], BF16, kind="ExternalOutput").ap()
    emit_l0ffn(nc, xnf, s_f, x_ap, (wgs, wus, wds, wge, wue, wde, wr, rb),
               x1_ap, xn1f_ap, "f0")
    emit_kvproj(nc, xn1f_ap, wkt, wvt, k1_ap, v1_ap, "kv")
    return nc


def emit_l0ffn(nc, xnf, s_f, x_ap, ws, x1_ap, xn1f_ap, tag, xn1send_ap=None):
    wgs, wus, wds, wge, wue, wde, wr, rb = ws
    NTILE = 4
    TT = 1024
    NE = 9
    NIC = 16
    with ExitStack() as es:
        sb = lambda n, sh, dt: es.enter_context(nc.sbuf_tensor(tag + n, sh, dt)).ap()
        ps = lambda n, sh, dt: es.enter_context(nc.psum_tensor(tag + n, sh, dt)).ap()
        xnt = sb("xnt", [128, 8 * TT], BF16)
        wg_sb = [sb(f"wg{i}", [128, 1024], BF16) for i in range(2)]
        wu_sb = [sb(f"wu{i}", [128, 1024], BF16) for i in range(2)]
        wd_sb = [sb(f"wd{i}", [128, 16 * 1024], BF16) for i in range(2)]
        act_sb = sb("act", [128, 16 * TT], BF16)
        acc_sb = sb("acc", [128, 8 * 1024], F32)
        sg_sb = [sb(f"sg{i}", [128, 1024], F32) for i in range(2)]
        wr_sb = sb("wrs", [128, 8 * 8], BF16)
        rbb = sb("rbb", [128, 8], F32)
        rrow = sb("rrow", [1, 8], F32)
        ones1 = sb("ones1", [1, 128], F32)
        cw_sb = sb("cw", [128, 8 * 8], F32)
        rt = sb("rt", [128, 5 * 8], F32)
        rm = sb("rm", [128, 4], F32)
        sin_sb = [sb(f"sin{i}", [128, 1024], F32) for i in range(2)]
        stm_sb = sb("stm", [128, 1024], F32)
        xin_sb = [sb(f"xin{i}", [128, 1024], F32) for i in range(2)]
        x1t_sb = [sb(f"x1t{i}", [128, 1024], F32) for i in range(2)]
        xn1_sb = sb("xn1", [128, 1024], F32)
        nst_sb = [sb(f"nst{i}", [128, 1024], BF16) for i in range(2)]
        lstt = sb("lstt", [128, 8], F32)
        sqs = sb("sqs", [128, 1024], F32)
        epst = sb("epst", [128, 1], F32)
        ident = sb("idb", [128, 128], BF16)
        identf = sb("idf", [128, 128], F32)

        pg = [ps(f"pg{i}", [128, 1024], F32) for i in range(2)]
        pu = [ps(f"pu{i}", [128, 1024], F32) for i in range(2)]
        pd = [pg[0][:, 0:512], pg[1][:, 0:512]]
        prt = pu[0][:, 0:64]
        ptr_s = pu[1][:, 0:512]
        ptr_n = pg[1][:, 512:1024]

        ac = Ctr(nc, tag + "ac")
        dv = Ctr(nc, tag + "dv")
        pe = Ctr(nc, tag + "pe")
        gmk = Ctr(nc, tag + "gmk")
        bld = Ctr(nc, tag + "bld")
        rld = Ctr(nc, tag + "rld")
        xld = Ctr(nc, tag + "xld")
        wlds = [Ctr(nc, tag + f"wld{i}") for i in range(2)]
        wdld = [Ctr(nc, tag + f"wdld{i}") for i in range(2)]
        slds = [Ctr(nc, tag + f"sld{i}") for i in range(2)]
        xilds = [Ctr(nc, tag + f"xild{i}") for i in range(2)]
        sox = [Ctr(nc, tag + f"sox{i}") for i in range(2)]
        son = [Ctr(nc, tag + f"son{i}") for i in range(2)]

        wd_free = {}
        epi_done = {}

        with nc.Block(tag) as blk:
            @blk.gpsimd
            def _(g):
                g.memset(epst, LN_EPS)
                g.memset(ones1, 1.0)
                _mk_identity(g, ident, gmk).then_inc(gmk.h, 1)
                gmk.n += 1
                g.wait_ge(gmk.h, gmk.n)
                _mk_identity(g, identf, gmk).then_inc(gmk.h, 1)
                gmk.n += 1
                for hc in range(8):
                    g.dma_start(out=wr_sb[:, hc * 8:(hc + 1) * 8],
                                in_=wr[hc * 128:(hc + 1) * 128, :]).then_inc(rld.h, 16)
            gmk_all = gmk.n
            rld.n += 16 * 8

            @blk.sync
            def _(sy):
                sy.dma_start(out=rrow, in_=rb).then_inc(bld.h, 16)
            bld.n += 16
            rbld = bld.n

            @blk.tensor
            def _(t, rbld=rbld):
                t.wait_ge(bld.h, rbld)
                t.wait_ge(gmk.h, gmk_all)
                pe.inc(t.matmul(prt[:, 0:8], ones1, rrow, start=True, stop=True))
            rb_pe = pe.n

            @blk.vector
            def _(v, rb_pe=rb_pe):
                v.wait_ge(pe.h, rb_pe)
                dv.inc(v.tensor_copy(rbb, prt[:, 0:8]))
            rbb_dv = dv.n

            wgfen = [0, 0]
            pgfen = [0, 0]
            pufen = [0, 0]
            prev_dn_pe = 0
            prev_dn_dv = 0
            for ti in range(NTILE):
                t0 = ti * TT

                prev_tile_fence = pe.n

                @blk.sync
                def _(sy, ti=ti, t0=t0, fence=prev_tile_fence):
                    sy.wait_ge(pe.h, fence)
                    for hc in range(8):
                        sy.dma_start(out=xnt[:, hc * TT:(hc + 1) * TT],
                                     in_=xnf[hc * 128:(hc + 1) * 128, t0:t0 + TT])\
                          .then_inc(xld.h, 16)
                xld.n += 16 * 8
                xfence = xld.n

                @blk.tensor
                def _(t, xfence=xfence, rbld=rbld):
                    t.wait_ge(xld.h, xfence)
                    t.wait_ge(bld.h, rbld)
                    t.wait_ge(rld.h, rld.n)
                    t.wait_ge(dv.h, dv.n)
                    for ts in range(8):
                        for hc in range(8):
                            ins = t.matmul(prt[:, ts * 8:(ts + 1) * 8],
                                           xnt[:, hc * TT + ts * 128: hc * TT + (ts + 1) * 128],
                                           wr_sb[:, hc * 8:(hc + 1) * 8],
                                           start=(hc == 0), stop=(hc == 7))
                    pe.inc(ins)
                rt_pe = pe.n

                for ts in range(8):
                    sl = slice(ts * 8, (ts + 1) * 8)

                    @blk.vector
                    def _(v, ts=ts, sl=sl, rt_pe=rt_pe, rbb_dv=rbb_dv, dfen=None):
                        v.wait_ge(pe.h, rt_pe)
                        v.wait_ge(dv.h, dv.n)
                        dv.inc(v.tensor_add(rt[:, 0:8], prt[:, sl], rbb))
                    lg_dv = dv.n

                    @blk.scalar
                    def _(sc, lg_dv=lg_dv):
                        sc.wait_ge(dv.h, lg_dv)
                        ac.inc(sc.activation(out=rt[:, 8:16], in_=rt[:, 0:8],
                                             func=AF.Sigmoid))
                    pr_ac = ac.n

                    @blk.vector
                    def _(v, ts=ts, sl=sl, pr_ac=pr_ac):
                        v.wait_ge(ac.h, pr_ac)
                        dv.inc(v.reduce_max(rm[:, 0:1], rt[:, 8:16], axis=AX))
                        v.wait_ge(dv.h, dv.n)
                        dv.inc(v.tensor_scalar(out=rt[:, 16:24], in0=rt[:, 8:16],
                                               scalar1=rm[:, 0:1], scalar2=None,
                                               op0=OP.is_ge))
                        v.wait_ge(dv.h, dv.n)
                        dv.inc(v.tensor_sub(rt[:, 24:32], rt[:, 8:16], rt[:, 16:24]))
                        v.wait_ge(dv.h, dv.n)
                        dv.inc(v.reduce_max(rm[:, 1:2], rt[:, 24:32], axis=AX))
                        v.wait_ge(dv.h, dv.n)
                        dv.inc(v.tensor_scalar(out=rt[:, 32:40], in0=rt[:, 8:16],
                                               scalar1=rm[:, 1:2], scalar2=None,
                                               op0=OP.is_ge))
                        v.wait_ge(dv.h, dv.n)
                        dv.inc(v.tensor_mul(cw_sb[:, sl], rt[:, 8:16], rt[:, 32:40]))
                cw_fence = dv.n

                for e in range(NE):
                    wgt = wgs if e == 0 else wge[e - 1]
                    wut = wus if e == 0 else wue[e - 1]
                    wdt = wds if e == 0 else wde[e - 1]

                    key = ti * NE + e
                    kk = key % 2

                    @blk.gpsimd
                    def _(g, e=e, ti=ti, wdt=wdt, key=key, kk=kk):
                        if key >= 2:
                            g.wait_ge(dv.h, wd_free[key - 2])
                        for icc in range(16):
                            g.dma_start(out=wd_sb[kk][:, icc * 1024:(icc + 1) * 1024],
                                        in_=wdt[icc * 128:(icc + 1) * 128, :])\
                              .then_inc(wdld[kk].h, 16)
                    wdld[kk].n += 16 * 16
                    wd_fence = wdld[kk].n

                    for ic in range(NIC):
                        wguard = wgfen[ic % 2]

                        @blk.gpsimd
                        def _(g, e=e, ic=ic, wgt=wgt, wut=wut, wguard=wguard):
                            if wguard:
                                g.wait_ge(dv.h, wguard)
                            g.dma_start(out=wg_sb[ic % 2], in_=wgt[ic])\
                             .then_inc(wlds[ic % 2].h, 16)
                            g.dma_start(out=wu_sb[ic % 2], in_=wut[ic])\
                             .then_inc(wlds[ic % 2].h, 16)
                        wlds[ic % 2].n += 32
                        wfence = wlds[ic % 2].n
                        pguard, puguard = pgfen[ic % 2], pufen[ic % 2]

                        @blk.tensor
                        def _(t, e=e, ic=ic, wfence=wfence, xfence=xfence,
                              pguard=pguard, puguard=puguard, cwf=cw_fence):
                            t.wait_ge(xld.h, xfence)
                            t.wait_ge(wlds[ic % 2].h, wfence)
                            if e == 0 and ic < 2:
                                t.wait_ge(dv.h, cwf)
                            if prev_dn_dv and ic < 2:
                                t.wait_ge(dv.h, prev_dn_dv)
                            if pguard:
                                t.wait_ge(ac.h, pguard)
                            if puguard:
                                t.wait_ge(dv.h, puguard)
                            for hc in range(8):
                                for half in range(2):
                                    t.matmul(pg[ic % 2][:, half * 512:(half + 1) * 512],
                                             wg_sb[ic % 2][:, hc * 128:(hc + 1) * 128],
                                             xnt[:, hc * TT + half * 512: hc * TT + (half + 1) * 512],
                                             start=(hc == 0), stop=(hc == 7))
                            for hc in range(8):
                                for half in range(2):
                                    ins = t.matmul(pu[ic % 2][:, half * 512:(half + 1) * 512],
                                                   wu_sb[ic % 2][:, hc * 128:(hc + 1) * 128],
                                                   xnt[:, hc * TT + half * 512: hc * TT + (half + 1) * 512],
                                                   start=(hc == 0), stop=(hc == 7))
                            pe.inc(ins)
                        up_pe = pe.n

                        @blk.scalar
                        def _(sc, ic=ic, up_pe=up_pe):
                            sc.wait_ge(pe.h, up_pe)
                            sc.wait_ge(dv.h, dv.n)
                            ac.inc(sc.activation(out=sg_sb[ic % 2], in_=pg[ic % 2],
                                                 func=AF.Silu))
                        pgfen[ic % 2] = ac.n
                        sil_ac = ac.n
                        dnguard = prev_dn_pe

                        @blk.vector
                        def _(v, ic=ic, sil_ac=sil_ac, dnguard=dnguard):
                            v.wait_ge(ac.h, sil_ac)
                            if dnguard:
                                v.wait_ge(pe.h, dnguard)
                            dv.inc(v.tensor_mul(act_sb[:, ic * TT:(ic + 1) * TT],
                                                pu[ic % 2], sg_sb[ic % 2]))
                        pufen[ic % 2] = dv.n
                        wgfen[ic % 2] = dv.n
                    act_fence = dv.n

                    dn_dv = {}
                    for ts in range(8):
                        for half in range(2):
                            j = ts * 2 + half

                            acfence_dn = ac.n

                            @blk.tensor
                            def _(t, e=e, ts=ts, half=half, j=j, kk=kk,
                                  act_fence=act_fence, wd_fence=wd_fence,
                                  acfence_dn=acfence_dn):
                                t.wait_ge(dv.h, act_fence)
                                t.wait_ge(ac.h, acfence_dn)
                                t.wait_ge(wdld[kk].h, wd_fence)
                                if j >= 2:
                                    t.wait_ge(dv.h, dn_dv[j - 2])
                                for icc in range(16):
                                    ins = t.matmul(
                                        pd[j % 2],
                                        act_sb[:, icc * TT + ts * 128: icc * TT + (ts + 1) * 128],
                                        wd_sb[kk][:, icc * 1024 + half * 512: icc * 1024 + (half + 1) * 512],
                                        start=(icc == 0), stop=(icc == 15))
                                pe.inc(ins)
                            dn_pe = pe.n

                            acguard = ac.n if e == 0 else 0

                            @blk.vector
                            def _(v, e=e, ts=ts, half=half, j=j, dn_pe=dn_pe,
                                  cw_fence=cw_fence, acguard=acguard):
                                v.wait_ge(pe.h, dn_pe)
                                v.wait_ge(dv.h, cw_fence)
                                if acguard:
                                    v.wait_ge(ac.h, acguard)
                                asl = acc_sb[:, ts * 1024 + half * 512: ts * 1024 + (half + 1) * 512]
                                if e == 0:
                                    dv.inc(v.tensor_copy(asl, pd[j % 2]))
                                else:
                                    dv.inc(v.scalar_tensor_tensor(
                                        out=asl, in0=pd[j % 2],
                                        scalar=cw_sb[:, ts * 8 + (e - 1): ts * 8 + e],
                                        in1=asl, op0=OP.mult, op1=OP.add))
                            dn_dv[j] = dv.n
                    wd_free[key] = dv.n
                    prev_dn_pe = pe.n
                    prev_dn_dv = dv.n
                ffn_fence = dv.n

                for ts in range(8):
                    tglob = t0 + ts * 128
                    bsel = ts % 2

                    @blk.sync
                    def _(sy, ti=ti, ts=ts, tglob=tglob, bsel=bsel):
                        if ti * 8 + ts >= 2:
                            sy.wait_ge(dv.h, epi_done[ti * 8 + ts - 2])
                        for j in range(8):
                            sy.dma_start(out=sin_sb[bsel][:, j * 128:(j + 1) * 128],
                                         in_=s_f[j * 128:(j + 1) * 128, tglob:tglob + 128])\
                              .then_inc(slds[bsel].h, 16)
                        sy.dma_start(out=xin_sb[bsel], in_=x_ap[tglob:tglob + 128, :])\
                          .then_inc(xilds[bsel].h, 16)
                    slds[bsel].n += 16 * 8
                    xilds[bsel].n += 16
                    sfence = slds[bsel].n
                    xifence = xilds[bsel].n

                    stm_dv = None
                    for rnd in range(2):
                        @blk.tensor
                        def _(t, ts=ts, bsel=bsel, sfence=sfence, rnd=rnd):
                            t.wait_ge(slds[bsel].h, sfence)
                            t.wait_ge(gmk.h, gmk_all)
                            t.wait_ge(dv.h, dv.n)
                            for j in range(4):
                                jj = rnd * 4 + j
                                ins = t.transpose(ptr_s[:, j * 128:(j + 1) * 128],
                                                  sin_sb[bsel][:, jj * 128:(jj + 1) * 128],
                                                  identf)
                            pe.inc(ins)
                        str_pe = pe.n

                        @blk.vector
                        def _(v, str_pe=str_pe, rnd=rnd):
                            v.wait_ge(pe.h, str_pe)
                            v.wait_ge(dv.h, dv.n)
                            dv.inc(v.tensor_copy(stm_sb[:, rnd * 512:(rnd + 1) * 512],
                                                 ptr_s))
                    stm_dv = dv.n

                    @blk.scalar
                    def _(sc, ts=ts, ffn_fence=ffn_fence, acfen=None):
                        sc.wait_ge(dv.h, ffn_fence)
                        sc.wait_ge(dv.h, dv.n)
                        sc.wait_ge(ac.h, ac.n)
                        ac.inc(sc.activation(out=sqs, in_=acc_sb[:, ts * 1024:(ts + 1) * 1024],
                                             func=AF.Silu))
                    sil_ac2 = ac.n

                    @blk.vector
                    def _(v, ts=ts, bsel=bsel, stm_dv=stm_dv, sil_ac2=sil_ac2,
                          xifence=xifence):
                        v.wait_ge(dv.h, stm_dv)
                        v.wait_ge(ac.h, sil_ac2)
                        dv.inc(v.tensor_mul(stm_sb, stm_sb, sqs))
                        v.wait_ge(xilds[bsel].h, xifence)
                        v.wait_ge(sox[bsel].h, sox[bsel].n)
                        v.wait_ge(dv.h, dv.n)
                        dv.inc(v.tensor_add(x1t_sb[bsel], xin_sb[bsel], stm_sb))
                    x1_dv = dv.n

                    @blk.scalar
                    def _(sc, bsel=bsel, x1_dv=x1_dv):
                        sc.wait_ge(dv.h, x1_dv)
                        ac.inc(sc.activation(out=sqs, in_=x1t_sb[bsel], func=AF.Square,
                                             accum_out=lstt[:, 1:2]))
                    s2_ac = ac.n

                    @blk.vector
                    def _(v, bsel=bsel, x1_dv=x1_dv, s2_ac=s2_ac):
                        v.wait_ge(dv.h, x1_dv)
                        dv.inc(v.reduce_sum(lstt[:, 0:1], x1t_sb[bsel], axis=AX))
                        v.wait_ge(dv.h, dv.n)
                        dv.inc(v.tensor_scalar(out=lstt[:, 2:3], in0=lstt[:, 0:1],
                                               scalar1=1.0 / D, scalar2=None, op0=OP.mult))
                        v.wait_ge(dv.h, dv.n)
                        dv.inc(v.scalar_tensor_tensor(out=lstt[:, 5:6], in0=lstt[:, 2:3],
                                                      scalar=1.0, in1=lstt[:, 2:3],
                                                      op0=OP.mult, op1=OP.mult))
                        v.wait_ge(ac.h, s2_ac)
                        v.wait_ge(dv.h, dv.n)
                        dv.inc(v.scalar_tensor_tensor(out=lstt[:, 3:4], in0=lstt[:, 1:2],
                                                      scalar=1.0 / D, in1=lstt[:, 5:6],
                                                      op0=OP.mult, op1=OP.subtract))
                    var_dv = dv.n

                    @blk.scalar
                    def _(sc, var_dv=var_dv):
                        sc.wait_ge(dv.h, var_dv)
                        ac.inc(sc.activation(out=lstt[:, 4:5], in_=lstt[:, 3:4],
                                             func=AF.Sqrt, bias=epst[:, 0:1]))
                    std_ac = ac.n

                    @blk.vector
                    def _(v, bsel=bsel, std_ac=std_ac):
                        v.wait_ge(ac.h, std_ac)
                        dv.inc(v.reciprocal(lstt[:, 5:6], lstt[:, 4:5]))
                        v.wait_ge(dv.h, dv.n)
                        dv.inc(v.tensor_scalar(out=xn1_sb, in0=x1t_sb[bsel],
                                               scalar1=lstt[:, 2:3], scalar2=lstt[:, 5:6],
                                               op0=OP.subtract, op1=OP.mult))
                    xn1_dv = dv.n

                    for rnd in range(2):
                        @blk.tensor
                        def _(t, xn1_dv=xn1_dv, rnd=rnd):
                            t.wait_ge(dv.h, xn1_dv)
                            t.wait_ge(dv.h, dv.n)
                            for j in range(4):
                                jj = rnd * 4 + j
                                ins = t.transpose(ptr_n[:, j * 128:(j + 1) * 128],
                                                  xn1_sb[:, jj * 128:(jj + 1) * 128],
                                                  identf)
                            pe.inc(ins)
                        ntr_pe = pe.n

                        @blk.vector
                        def _(v, ti=ti, ts=ts, bsel=bsel, ntr_pe=ntr_pe, rnd=rnd):
                            v.wait_ge(pe.h, ntr_pe)
                            if ti * 8 + ts >= 2:
                                v.wait_ge(son[bsel].h, son[bsel].n)
                            dv.inc(v.tensor_copy(nst_sb[bsel][:, rnd * 512:(rnd + 1) * 512],
                                                 ptr_n))
                    nst_dv = dv.n

                    @blk.sync
                    def _(sy, ti=ti, ts=ts, tglob=tglob, bsel=bsel, x1_dv=x1_dv,
                          nst_dv=nst_dv):
                        sy.wait_ge(dv.h, x1_dv)
                        sy.dma_start(out=x1_ap[tglob:tglob + 128, :],
                                     in_=x1t_sb[bsel]).then_inc(sox[bsel].h, 16)
                        sy.wait_ge(dv.h, nst_dv)
                        for j in range(8):
                            sy.dma_start(out=xn1f_ap[j * 128:(j + 1) * 128,
                                                     tglob:tglob + 128],
                                         in_=nst_sb[bsel][:, j * 128:(j + 1) * 128])\
                              .then_inc(son[bsel].h, 16)
                        if xn1send_ap is not None:
                            w = tglob // 512
                            c0 = tglob - w * 512
                            for j in range(8):
                                sy.dma_start(
                                    out=xn1send_ap[w * 1024 + j * 128:
                                                   w * 1024 + (j + 1) * 128,
                                                   c0:c0 + 128],
                                    in_=nst_sb[bsel][:, j * 128:(j + 1) * 128])\
                                  .then_inc(son[bsel].h, 16)
                    sox[bsel].n += 16
                    son[bsel].n += 16 * 8 * (2 if xn1send_ap is not None else 1)
                    epi_done[ti * 8 + ts] = dv.n
                pgfen = [0, 0]
                pufen = [dv.n, dv.n]
                prev_dn_pe = pe.n

            @blk.sync
            def _(sy):
                for cs in sox + son:
                    sy.wait_ge(cs.h, cs.n)
        return [(cs.h, cs.n) for cs in sox + son]


def emit_kvproj(nc, xn1f, wkt, wvt, k1_ap, v1_ap, tag):
    with ExitStack() as es:
        sb = lambda n, sh, dt: es.enter_context(nc.sbuf_tensor(tag + n, sh, dt)).ap()
        ps = lambda n, sh, dt: es.enter_context(nc.psum_tensor(tag + n, sh, dt)).ap()
        xnt = sb("xnt", [128, 8 * 1024], BF16)
        wk_sb = sb("wk", [128, 8 * 1024], BF16)
        wv_sb = sb("wv", [128, 8 * 1024], BF16)
        kst = [sb(f"kst{i}", [128, 1024], BF16) for i in range(2)]
        vst = [sb(f"vst{i}", [128, 1024], BF16) for i in range(2)]
        pk = ps("pk", [128, 1024], F32)
        pv = ps("pv", [128, 1024], F32)

        wld = Ctr(nc, tag + "wld")
        xld = Ctr(nc, tag + "xld")
        pe = Ctr(nc, tag + "pe")
        dv = Ctr(nc, tag + "dv")
        sos = [Ctr(nc, tag + f"so{i}") for i in range(2)]

        with nc.Block(tag) as blk:
            @blk.gpsimd
            def _(g):
                for hc in range(8):
                    g.dma_start(out=wk_sb[:, hc * 1024:(hc + 1) * 1024],
                                in_=wkt[hc * 128:(hc + 1) * 128, :]).then_inc(wld.h, 16)
                    g.dma_start(out=wv_sb[:, hc * 1024:(hc + 1) * 1024],
                                in_=wvt[hc * 128:(hc + 1) * 128, :]).then_inc(wld.h, 16)
            wld.n += 16 * 16
            wfence = wld.n

            out_done = {}
            for ti in range(4):
                t0 = ti * 1024

                @blk.sync
                def _(sy, ti=ti, t0=t0, pfen=None):
                    sy.wait_ge(pe.h, pe.n)
                    for hc in range(8):
                        sy.dma_start(out=xnt[:, hc * 1024:(hc + 1) * 1024],
                                     in_=xn1f[hc * 128:(hc + 1) * 128, t0:t0 + 1024])\
                          .then_inc(xld.h, 16)
                xld.n += 16 * 8
                xfence = xld.n

                for ts in range(8):
                    bsel = ts % 2
                    idx = ti * 8 + ts

                    @blk.tensor
                    def _(t, ts=ts, xfence=xfence, wfence=wfence):
                        t.wait_ge(xld.h, xfence)
                        t.wait_ge(wld.h, wfence)
                        t.wait_ge(dv.h, dv.n)
                        for hc in range(8):
                            for half in range(2):
                                t.matmul(pk[:, half * 512:(half + 1) * 512],
                                         xnt[:, hc * 1024 + ts * 128: hc * 1024 + (ts + 1) * 128],
                                         wk_sb[:, hc * 1024 + half * 512: hc * 1024 + (half + 1) * 512],
                                         start=(hc == 0), stop=(hc == 7))
                        for hc in range(8):
                            for half in range(2):
                                ins = t.matmul(pv[:, half * 512:(half + 1) * 512],
                                               xnt[:, hc * 1024 + ts * 128: hc * 1024 + (ts + 1) * 128],
                                               wv_sb[:, hc * 1024 + half * 512: hc * 1024 + (half + 1) * 512],
                                               start=(hc == 0), stop=(hc == 7))
                        pe.inc(ins)
                    kv_pe = pe.n

                    @blk.vector
                    def _(v, bsel=bsel, idx=idx, kv_pe=kv_pe):
                        v.wait_ge(pe.h, kv_pe)
                        if idx >= 2:
                            v.wait_ge(sos[bsel].h, out_done[idx - 2])
                        dv.inc(v.tensor_copy(kst[bsel], pk))
                        dv.inc(v.tensor_copy(vst[bsel], pv))
                    kv_dv = dv.n

                    @blk.sync
                    def _(sy, ti=ti, ts=ts, t0=t0, bsel=bsel, kv_dv=kv_dv):
                        sy.wait_ge(dv.h, kv_dv)
                        tg = t0 + ts * 128
                        sy.dma_start(out=k1_ap[tg:tg + 128, :], in_=kst[bsel])\
                          .then_inc(sos[bsel].h, 16)
                        sy.dma_start(out=v1_ap[tg:tg + 128, :], in_=vst[bsel])\
                          .then_inc(sos[bsel].h, 16)
                    sos[bsel].n += 32
                    out_done[idx] = sos[bsel].n

            @blk.sync
            def _(sy):
                sy.wait_ge(sos[0].h, sos[0].n)
                sy.wait_ge(sos[1].h, sos[1].n)
        return [(sos[0].h, sos[0].n), (sos[1].h, sos[1].n)]


def _one_block(nc, name):
    def deco(f):
        with nc.Block(name) as blk:
            f(blk)
    return deco



def emit_l1scores(nc, xn1q_sh, wqt, k1all, mask, a_send, tag, a_recv=None, cc=None):
    NB = 8
    with ExitStack() as es:
        dv = Ctr(nc, tag + "dv")
        ac = Ctr(nc, tag + "ac")
        pe = Ctr(nc, tag + "pe")

        with ExitStack() as es2:
            sb2 = lambda n, sh, dt: es2.enter_context(nc.sbuf_tensor(tag + n, sh, dt)).ap()
            ps2 = lambda n, sh, dt: es2.enter_context(nc.psum_tensor(tag + n, sh, dt)).ap()
            q1_sb = sb2("q1", [128, 8 * S], BF16)
            wq_sb = sb2("wq", [128, 8 * 1024], BF16)
            xq_sb = sb2("xq", [128, 8 * 512], BF16)
            kt_sb = [sb2(f"kt{i}", [128, 8 * 512], BF16) for i in range(3)]
            e_sb = sb2("e", [128, 32 * 512], F32)
            mk_sb = sb2("mk", [128, 4 * 512], I8)
            tiny = sb2("tiny", [128, 512], F32)
            d_sb = sb2("d", [128, 512], F32)
            r_sb = sb2("r", [128, 512], F32)
            a_st = [sb2(f"ast{i}", [128, 512], BF16) for i in range(2)]
            pq = [ps2(f"pq{i}", [128, 512], F32) for i in range(2)]

            wld = Ctr(nc, tag + "wld")
            xqld = Ctr(nc, tag + "xqld")
            ktld = [Ctr(nc, tag + f"ktld{i}") for i in range(3)]
            mkld = Ctr(nc, tag + "mkld")
            aso = [Ctr(nc, tag + f"aso{i}") for i in range(2)]
            gmk = Ctr(nc, tag + "gmk")

            with nc.Block(tag + "qs") as blk:
                @blk.gpsimd
                def _(g):
                    g.memset(tiny, 1e-9).then_inc(gmk.h, 1)
                    for hc in range(8):
                        g.dma_start(out=wq_sb[:, hc * 1024:(hc + 1) * 1024],
                                    in_=wqt[hc * 128:(hc + 1) * 128, :])\
                          .then_inc(wld.h, 16)
                gmk.n += 1
                wld.n += 16 * 8
                wfen = wld.n

                for b in range(NB):
                    @blk.sync
                    def _(sy, b=b):
                        sy.wait_ge(pe.h, pe.n)
                        for hc in range(8):
                            sy.dma_start(out=xq_sb[:, hc * 512:(hc + 1) * 512],
                                         in_=xn1q_sh[b * 1024 + hc * 128:
                                                     b * 1024 + (hc + 1) * 128, 0:512])\
                              .then_inc(xqld.h, 16)
                    xqld.n += 16 * 8
                    xfen = xqld.n

                    for dc in range(8):
                        @blk.tensor
                        def _(t, b=b, dc=dc, xfen=xfen, wfen=wfen):
                            t.wait_ge(xqld.h, xfen)
                            t.wait_ge(wld.h, wfen)
                            t.wait_ge(dv.h, dv.n)
                            for hc in range(8):
                                ins = t.matmul(pq[dc % 2],
                                               wq_sb[:, hc * 1024 + dc * 128: hc * 1024 + (dc + 1) * 128],
                                               xq_sb[:, hc * 512:(hc + 1) * 512],
                                               start=(hc == 0), stop=(hc == 7))
                            pe.inc(ins)
                        qpe = pe.n

                        @blk.vector
                        def _(v, b=b, dc=dc, qpe=qpe):
                            v.wait_ge(pe.h, qpe)
                            dv.inc(v.tensor_copy(
                                q1_sb[:, dc * S + b * 512: dc * S + (b + 1) * 512],
                                pq[dc % 2]))
                q_fence = dv.n

                prev_stripe_dv = dv.n
                ktuse = {}
                for st in range(8):
                    j0 = st * 512

                    @blk.sync
                    def _(sy, st=st, j0=j0):
                        sy.wait_ge(dv.h, dv.n)
                        msrc = mask[j0:j0 + 512, :]\
                            .rearrange("(jc p) i -> p jc i", p=128)
                        mdst = mk_sb.rearrange("p (jc i) -> p jc i", jc=4)
                        sy.dma_start(out=mdst, in_=msrc).then_inc(mkld.h, 16)
                    mkld.n += 16
                    mfen = mkld.n

                    for b in range(NB):
                        kti = st * NB + b
                        ktc = ktld[kti % 3]

                        @blk.gpsimd
                        def _(g, st=st, b=b, j0=j0, kti=kti, ktc=ktc):
                            if kti >= 3:
                                g.wait_ge(pe.h, ktuse[kti - 3])
                            ktv = k1all[b * S:(b + 1) * S, :]\
                                .rearrange("s d -> (s d)")\
                                .rearrange("(dc p j) -> p dc j", dc=8, p=128)
                            dst = kt_sb[kti % 3].rearrange("p (dc j) -> p dc j", dc=8)
                            g.dma_start(out=dst, in_=ktv[:, :, j0:j0 + 512])\
                              .then_inc(ktc.h, 16)
                        ktc.n += 16
                        kfen = ktc.n

                        for jc in range(4):
                            @blk.tensor
                            def _(t, st=st, b=b, jc=jc, kti=kti, kfen=kfen, ktc=ktc,
                                  q_fence=q_fence):
                                t.wait_ge(ktc.h, kfen)
                                t.wait_ge(dv.h, q_fence)
                                t.wait_ge(ac.h, ac.n)
                                for dc in range(8):
                                    ins = t.matmul(
                                        pq[jc % 2],
                                        kt_sb[kti % 3][:, dc * 512 + jc * 128: dc * 512 + (jc + 1) * 128],
                                        q1_sb[:, dc * S + b * 512: dc * S + (b + 1) * 512],
                                        start=(dc == 0), stop=(dc == 7))
                                pe.inc(ins)
                            spe = pe.n

                            @blk.scalar
                            def _(sc, b=b, jc=jc, spe=spe, psd=prev_stripe_dv):
                                sc.wait_ge(pe.h, spe)
                                sc.wait_ge(dv.h, psd)
                                ac.inc(sc.activation(
                                    out=e_sb[:, (jc * 8 + b) * 512:(jc * 8 + b + 1) * 512],
                                    in_=pq[jc % 2], func=AF.Exp, scale=0.03125))
                        ktuse[kti] = pe.n
                    stripe_ac = ac.n

                    for jc in range(4):
                        @blk.vector
                        def _(v, jc=jc, stripe_ac=stripe_ac):
                            v.wait_ge(ac.h, stripe_ac)
                            esl = lambda b: e_sb[:, (jc * 8 + b) * 512:(jc * 8 + b + 1) * 512]
                            dv.inc(v.tensor_add(d_sb, esl(0), esl(1)))
                            for b in (2, 3, 4, 5, 6, 7):
                                v.wait_ge(dv.h, dv.n)
                                dv.inc(v.tensor_add(d_sb, d_sb, esl(b)))
                            v.wait_ge(dv.h, dv.n)
                            dv.inc(v.reciprocal(r_sb, d_sb))
                        r_fence = dv.n

                        for b in range(NB):
                            asb = a_st[b % 2]
                            asoc = aso[b % 2]

                            @blk.vector
                            def _(v, st=st, b=b, jc=jc, asb=asb, asoc=asoc,
                                  r_fence=r_fence, mfen=mfen):
                                v.wait_ge(dv.h, r_fence)
                                v.wait_ge(mkld.h, mfen)
                                v.wait_ge(gmk.h, 1)
                                v.wait_ge(asoc.h, asoc.n)
                                esl = e_sb[:, (jc * 8 + b) * 512:(jc * 8 + b + 1) * 512]
                                dv.inc(v.tensor_mul(esl, esl, r_sb))
                                v.wait_ge(dv.h, dv.n)
                                dv.inc(v.tensor_copy(asb, tiny))
                                v.wait_ge(dv.h, dv.n)
                                dv.inc(v.copy_predicated(asb,
                                                         mk_sb[:, jc * 512:(jc + 1) * 512],
                                                         esl))
                            a_dv = dv.n

                            @blk.sync
                            def _(sy, st=st, b=b, jc=jc, j0=j0, asb=asb,
                                  asoc=asoc, a_dv=a_dv):
                                sy.wait_ge(dv.h, a_dv)
                                sy.dma_start(
                                    out=a_send[st * S + b * 512 + jc * 128:
                                               st * S + b * 512 + (jc + 1) * 128, :],
                                    in_=asb).then_inc(asoc.h, 16)
                            asoc.n += 16
                    prev_stripe_dv = dv.n

                    if cc is not None:
                        @blk.gpsimd
                        def _(g, st=st, n0=aso[0].n, n1=aso[1].n):
                            g.wait_ge(aso[0].h, n0)
                            g.wait_ge(aso[1].h, n1)
                            with allow_ici_all_to_all():
                                g.collective_compute(
                                    "AllToAll", mybir.AluOpType.bypass,
                                    replica_groups=RG,
                                    ins=[a_send[st * S:(st + 1) * S, :].opt()],
                                    outs=[a_recv[st * S:(st + 1) * S, :].opt()])\
                                 .then_inc(cc.h, 1)
                        cc.n += 1

                @blk.sync
                def _(sy):
                    sy.wait_ge(aso[0].h, aso[0].n)
                    sy.wait_ge(aso[1].h, aso[1].n)
        return [(aso[0].h, aso[0].n), (aso[1].h, aso[1].n)]


def emit_l1ffn_g(nc, xn1f, ws, g1_ap, tag):
    sut, sdt, rut, rdt, rwt = ws
    NE = 18
    TT = 2048
    NTILE = 2
    NIC = 4
    with ExitStack() as es:
        sb = lambda n, sh, dt: es.enter_context(nc.sbuf_tensor(tag + n, sh, dt)).ap()
        ps = lambda n, sh, dt: es.enter_context(nc.psum_tensor(tag + n, sh, dt)).ap()
        xnt = sb("xnt", [128, 8 * TT], BF16)
        wu_sb = [sb(f"wu{i}", [128, 1024], BF16) for i in range(2)]
        wd_sb = [sb(f"wd{i}", [128, 4 * 1024], BF16) for i in range(2)]
        act_sb = sb("act", [128, 4 * TT], BF16)
        acc_sb = sb("acc", [128, 16 * 1024], F32)
        wr_sb = sb("wrs", [128, 8 * 16], BF16)
        gate_sb = sb("gate", [128, 16 * 16], F32)
        rt = sb("rt", [128, 6 * 16], F32)
        rm = sb("rm", [128, 4], F32)
        sqs = sb("sqs", [128, 1024], F32)
        gst = [sb(f"gst{i}", [128, 1024], BF16) for i in range(2)]

        pgx = [ps(f"pgx{i}", [128, 2048], F32) for i in range(2)]
        pd = [pgx[0][:, 0:512], pgx[1][:, 0:512]]
        prt = pgx[0][:, 512:1024]

        ac = Ctr(nc, tag + "ac")
        dv = Ctr(nc, tag + "dv")
        pe = Ctr(nc, tag + "pe")
        rld = Ctr(nc, tag + "rld")
        xld = Ctr(nc, tag + "xld")
        wlds = [Ctr(nc, tag + f"wld{i}") for i in range(2)]
        wdld = [Ctr(nc, tag + f"wdld{i}") for i in range(2)]
        dso = [Ctr(nc, tag + f"dso{i}") for i in range(2)]

        wd_free = {}
        epi_done = {}

        with nc.Block(tag) as blk:
            @blk.gpsimd
            def _(g):
                for hc in range(8):
                    g.dma_start(out=wr_sb[:, hc * 16:(hc + 1) * 16],
                                in_=rwt[hc * 128:(hc + 1) * 128, :]).then_inc(rld.h, 16)
            rld.n += 16 * 8

            prev_dn_pe = 0
            prev_dn_dv = 0
            pgfen = [0, 0]
            for ti in range(NTILE):
                t0 = ti * TT

                @blk.sync
                def _(sy, ti=ti, t0=t0):
                    sy.wait_ge(pe.h, pe.n)
                    for hc in range(8):
                        sy.dma_start(out=xnt[:, hc * TT:(hc + 1) * TT],
                                     in_=xn1f[hc * 128:(hc + 1) * 128, t0:t0 + TT])\
                          .then_inc(xld.h, 16)
                xld.n += 16 * 8
                xfence = xld.n

                for ts in range(16):
                    @blk.tensor
                    def _(t, ts=ts, xfence=xfence):
                        t.wait_ge(xld.h, xfence)
                        t.wait_ge(rld.h, rld.n)
                        t.wait_ge(dv.h, dv.n)
                        for hc in range(8):
                            ins = t.matmul(prt[:, 0:16],
                                           xnt[:, hc * TT + ts * 128: hc * TT + (ts + 1) * 128],
                                           wr_sb[:, hc * 16:(hc + 1) * 16],
                                           start=(hc == 0), stop=(hc == 7))
                        pe.inc(ins)
                    rpe = pe.n

                    @blk.vector
                    def _(v, ts=ts, rpe=rpe):
                        v.wait_ge(pe.h, rpe)
                        dv.inc(v.reduce_max(rm[:, 0:1], prt[:, 0:16], axis=AX))
                        v.wait_ge(dv.h, dv.n)
                        dv.inc(v.tensor_scalar(out=rt[:, 0:16], in0=prt[:, 0:16],
                                               scalar1=rm[:, 0:1], scalar2=None,
                                               op0=OP.subtract))
                    lg_dv = dv.n

                    @blk.scalar
                    def _(sc, lg_dv=lg_dv):
                        sc.wait_ge(dv.h, lg_dv)
                        sc.wait_ge(ac.h, ac.n)
                        ac.inc(sc.activation(out=rt[:, 16:32], in_=rt[:, 0:16],
                                             func=AF.Exp))
                    ex_ac = ac.n

                    @blk.vector
                    def _(v, ts=ts, ex_ac=ex_ac):
                        v.wait_ge(ac.h, ex_ac)
                        dv.inc(v.reduce_sum(rm[:, 1:2], rt[:, 16:32], axis=AX))
                        v.wait_ge(dv.h, dv.n)
                        dv.inc(v.reciprocal(rm[:, 2:3], rm[:, 1:2]))
                        v.wait_ge(dv.h, dv.n)
                        dv.inc(v.tensor_scalar(out=rt[:, 16:32], in0=rt[:, 16:32],
                                               scalar1=rm[:, 2:3], scalar2=None,
                                               op0=OP.mult))
                        v.wait_ge(dv.h, dv.n)
                        dv.inc(v.tensor_copy(rt[:, 32:48], rt[:, 16:32]))
                        for r in range(4):
                            v.wait_ge(dv.h, dv.n)
                            dv.inc(v.reduce_max(rm[:, 3:4], rt[:, 32:48], axis=AX))
                            if r < 3:
                                v.wait_ge(dv.h, dv.n)
                                dv.inc(v.tensor_scalar(out=rt[:, 48:64], in0=rt[:, 32:48],
                                                       scalar1=rm[:, 3:4], scalar2=None,
                                                       op0=OP.is_ge))
                                v.wait_ge(dv.h, dv.n)
                                dv.inc(v.tensor_sub(rt[:, 32:48], rt[:, 32:48],
                                                    rt[:, 48:64]))
                        v.wait_ge(dv.h, dv.n)
                        dv.inc(v.tensor_scalar(out=rt[:, 48:64], in0=rt[:, 16:32],
                                               scalar1=rm[:, 3:4], scalar2=None,
                                               op0=OP.is_ge))
                        v.wait_ge(dv.h, dv.n)
                        dv.inc(v.tensor_mul(gate_sb[:, ts * 16:(ts + 1) * 16],
                                            rt[:, 16:32], rt[:, 48:64]))
                gate_fence = dv.n

                for e in range(NE):
                    wut = sut[e] if e < 2 else rut[e - 2]
                    wdt = sdt[e] if e < 2 else rdt[e - 2]
                    key = ti * NE + e
                    kk = key % 2

                    @blk.gpsimd
                    def _(g, wdt=wdt, key=key, kk=kk):
                        if key >= 2:
                            g.wait_ge(dv.h, wd_free[key - 2])
                        for icc in range(4):
                            g.dma_start(out=wd_sb[kk][:, icc * 1024:(icc + 1) * 1024],
                                        in_=wdt[icc * 128:(icc + 1) * 128, :])\
                              .then_inc(wdld[kk].h, 16)
                    wdld[kk].n += 16 * 4
                    wd_fence = wdld[kk].n

                    for ic in range(NIC):
                        wguard = pgfen[ic % 2]

                        @blk.gpsimd
                        def _(g, ic=ic, wut=wut, wguard=wguard):
                            if wguard:
                                g.wait_ge(ac.h, wguard)
                            g.dma_start(out=wu_sb[ic % 2], in_=wut[ic])\
                             .then_inc(wlds[ic % 2].h, 16)
                        wlds[ic % 2].n += 16
                        wfen = wlds[ic % 2].n

                        @blk.tensor
                        def _(t, e=e, ic=ic, wfen=wfen, xfence=xfence,
                              pguard=pgfen[ic % 2], pdd=prev_dn_dv, gf=gate_fence):
                            t.wait_ge(xld.h, xfence)
                            t.wait_ge(wlds[ic % 2].h, wfen)
                            if e == 0 and ic == 0:
                                t.wait_ge(dv.h, gf)
                            if pdd and ic < 2:
                                t.wait_ge(dv.h, pdd)
                            if pguard:
                                t.wait_ge(ac.h, pguard)
                            for hc in range(8):
                                for q in range(4):
                                    ins = t.matmul(
                                        pgx[ic % 2][:, q * 512:(q + 1) * 512],
                                        wu_sb[ic % 2][:, hc * 128:(hc + 1) * 128],
                                        xnt[:, hc * TT + q * 512: hc * TT + (q + 1) * 512],
                                        start=(hc == 0), stop=(hc == 7))
                            pe.inc(ins)
                        up_pe = pe.n

                        @blk.scalar
                        def _(sc, e=e, ic=ic, up_pe=up_pe, dnp=prev_dn_pe):
                            sc.wait_ge(pe.h, up_pe)
                            if dnp:
                                sc.wait_ge(pe.h, dnp)
                            ac.inc(sc.activation(out=act_sb[:, ic * TT:(ic + 1) * TT],
                                                 in_=pgx[ic % 2], func=AF.Gelu))
                        pgfen[ic % 2] = ac.n
                    act_fence = ac.n

                    dn_dv = {}
                    for ts in range(16):
                        for half in range(2):
                            j = ts * 2 + half

                            @blk.tensor
                            def _(t, ts=ts, half=half, j=j, kk=kk,
                                  act_fence=act_fence, wd_fence=wd_fence):
                                t.wait_ge(ac.h, act_fence)
                                t.wait_ge(wdld[kk].h, wd_fence)
                                if j >= 2:
                                    t.wait_ge(dv.h, dn_dv[j - 2])
                                for icc in range(4):
                                    ins = t.matmul(
                                        pd[j % 2],
                                        act_sb[:, icc * TT + ts * 128: icc * TT + (ts + 1) * 128],
                                        wd_sb[kk][:, icc * 1024 + half * 512: icc * 1024 + (half + 1) * 512],
                                        start=(icc == 0), stop=(icc == 3))
                                pe.inc(ins)
                            dn_pe = pe.n
                            acguard = ac.n if e == 0 else 0

                            @blk.vector
                            def _(v, e=e, ts=ts, half=half, j=j, dn_pe=dn_pe,
                                  gate_fence=gate_fence, acguard=acguard):
                                v.wait_ge(pe.h, dn_pe)
                                v.wait_ge(dv.h, gate_fence)
                                if acguard:
                                    v.wait_ge(ac.h, acguard)
                                asl = acc_sb[:, ts * 1024 + half * 512: ts * 1024 + (half + 1) * 512]
                                if e == 0:
                                    dv.inc(v.tensor_copy(asl, pd[j % 2]))
                                elif e == 1:
                                    dv.inc(v.scalar_tensor_tensor(
                                        out=asl, in0=pd[j % 2], scalar=1.0,
                                        in1=asl, op0=OP.mult, op1=OP.add))
                                else:
                                    dv.inc(v.scalar_tensor_tensor(
                                        out=asl, in0=pd[j % 2],
                                        scalar=gate_sb[:, ts * 16 + (e - 2): ts * 16 + e - 1],
                                        in1=asl, op0=OP.mult, op1=OP.add))
                            dn_dv[j] = dv.n
                    wd_free[key] = dv.n
                    prev_dn_pe = pe.n
                    prev_dn_dv = dv.n
                ffn_fence = dv.n

                for ts in range(16):
                    tglob = t0 + ts * 128
                    bsel = ts % 2

                    @blk.scalar
                    def _(sc, ts=ts, ffn_fence=ffn_fence):
                        sc.wait_ge(dv.h, ffn_fence)
                        sc.wait_ge(dv.h, dv.n)
                        sc.wait_ge(ac.h, ac.n)
                        ac.inc(sc.activation(out=sqs,
                                             in_=acc_sb[:, ts * 1024:(ts + 1) * 1024],
                                             func=AF.Silu))
                    sil_ac = ac.n

                    @blk.vector
                    def _(v, ts=ts, bsel=bsel, sil_ac=sil_ac):
                        v.wait_ge(ac.h, sil_ac)
                        v.wait_ge(dso[bsel].h, dso[bsel].n)
                        dv.inc(v.tensor_copy(gst[bsel], sqs))
                    g_dv = dv.n

                    @blk.sync
                    def _(sy, tglob=tglob, bsel=bsel, g_dv=g_dv):
                        sy.wait_ge(dv.h, g_dv)
                        sy.dma_start(out=g1_ap[tglob:tglob + 128, :],
                                     in_=gst[bsel]).then_inc(dso[bsel].h, 16)
                    dso[bsel].n += 16
                    epi_done[ti * 16 + ts] = dv.n

            @blk.sync
            def _(sy):
                sy.wait_ge(dso[0].h, dso[0].n)
                sy.wait_ge(dso[1].h, dso[1].n)
        return [(dso[0].h, dso[0].n), (dso[1].h, dso[1].n)]


def emit_pv_final(nc, a_recv, v1loc, g1, x1_ap, y_ap, tag, cc=None, ccbase=2):
    NJ = 32
    with ExitStack() as es:
        sb = lambda n, sh, dt: es.enter_context(nc.sbuf_tensor(tag + n, sh, dt)).ap()
        ps = lambda n, sh, dt: es.enter_context(nc.psum_tensor(tag + n, sh, dt)).ap()
        v_all = sb("vall", [128, NJ * 1024], BF16)
        a_sb = [sb(f"a{i}", [128, 512], BF16) for i in range(4)]
        gt = [sb(f"g{i}", [128, 1024], BF16) for i in range(2)]
        xt = [sb(f"x{i}", [128, 1024], F32) for i in range(2)]
        yt = [sb(f"yy{i}", [128, 1024], F32) for i in range(2)]
        ppv = [ps(f"p{i}", [128, 512], F32) for i in range(8)]

        pe = Ctr(nc, tag + "pe")
        dv = Ctr(nc, tag + "dv")
        vld = Ctr(nc, tag + "vld")
        ald = [Ctr(nc, tag + f"ald{i}") for i in range(4)]
        gld = [Ctr(nc, tag + f"gld{i}") for i in range(2)]
        yso = [Ctr(nc, tag + f"yso{i}") for i in range(2)]

        with nc.Block(tag) as blk:
            @blk.gpsimd
            def _(g):
                for j in range(NJ):
                    g.dma_start(out=v_all[:, j * 1024:(j + 1) * 1024],
                                in_=v1loc[j * 128:(j + 1) * 128, :]).then_inc(vld.h, 16)
            vld.n += 16 * NJ

            g_free = {}
            use_pe = {}
            for cw in range(8):
                ev_prev = dv.n
                for j in range(NJ):
                    idx = cw * NJ + j
                    alc = ald[idx % 4]

                    @blk.sync
                    def _(sy, cw=cw, j=j, idx=idx, alc=alc):
                        if idx >= 4:
                            sy.wait_ge(pe.h, use_pe[idx - 4])
                        st, jl = j // 4, j % 4
                        if cc is not None:
                            sy.wait_ge(cc.h, ccbase + st + 1)
                        sy.dma_start(out=a_sb[idx % 4],
                                     in_=a_recv[st * S + cw * 512 + jl * 128:
                                                st * S + cw * 512 + (jl + 1) * 128, :])\
                          .then_inc(alc.h, 16)
                    alc.n += 16
                    afen = alc.n

                    @blk.tensor
                    def _(t, cw=cw, j=j, idx=idx, afen=afen, alc=alc, ev_prev=ev_prev):
                        t.wait_ge(alc.h, afen)
                        if cw == 0 and j == 0:
                            t.wait_ge(vld.h, vld.n)
                        if j == 0:
                            t.wait_ge(dv.h, ev_prev)
                        for isub in range(4):
                            for half in range(2):
                                ins = t.matmul(
                                    ppv[isub * 2 + half],
                                    a_sb[idx % 4][:, isub * 128:(isub + 1) * 128],
                                    v_all[:, j * 1024 + half * 512:
                                          j * 1024 + (half + 1) * 512],
                                    start=(j == 0), stop=(j == NJ - 1))
                        pe.inc(ins)
                    use_pe[idx] = pe.n
                pv_pe = pe.n

                for isub in range(4):
                    bsel = isub % 2
                    idx = cw * 4 + isub
                    row0 = cw * 512 + isub * 128

                    @blk.sync
                    def _(sy, idx=idx, bsel=bsel, row0=row0):
                        if idx >= 2:
                            sy.wait_ge(dv.h, g_free[idx - 2])
                        sy.dma_start(out=gt[bsel], in_=g1[row0:row0 + 128, :])\
                          .then_inc(gld[bsel].h, 16)
                        sy.dma_start(out=xt[bsel], in_=x1_ap[row0:row0 + 128, :])\
                          .then_inc(gld[bsel].h, 16)
                    gld[bsel].n += 32
                    gfen = gld[bsel].n

                    @blk.vector
                    def _(v, isub=isub, bsel=bsel, pv_pe=pv_pe, gfen=gfen):
                        v.wait_ge(pe.h, pv_pe)
                        v.wait_ge(gld[bsel].h, gfen)
                        v.wait_ge(yso[bsel].h, yso[bsel].n)
                        for half in range(2):
                            v.wait_ge(dv.h, dv.n)
                            dv.inc(v.tensor_mul(
                                yt[bsel][:, half * 512:(half + 1) * 512],
                                ppv[isub * 2 + half],
                                gt[bsel][:, half * 512:(half + 1) * 512]))
                        v.wait_ge(dv.h, dv.n)
                        dv.inc(v.tensor_add(yt[bsel], yt[bsel], xt[bsel]))
                    g_free[idx] = dv.n
                    ydv = dv.n

                    @blk.sync
                    def _(sy, bsel=bsel, row0=row0, ydv=ydv):
                        sy.wait_ge(dv.h, ydv)
                        sy.dma_start(out=y_ap[row0:row0 + 128, :], in_=yt[bsel])\
                          .then_inc(yso[bsel].h, 16)
                    yso[bsel].n += 16

            @blk.sync
            def _(sy):
                sy.wait_ge(yso[0].h, yso[0].n)
                sy.wait_ge(yso[1].h, yso[1].n)


RG = [list(range(NCORES))]


def fence_all(nc, deps, name):
    with nc.Block(name) as blk:
        for eng in ("sync", "gpsimd", "scalar", "vector", "tensor"):
            @getattr(blk, eng)
            def _(e, deps=deps):
                for (h, n) in deps:
                    e.wait_ge(h, n)


def emit_collectives(nc, ops, deps, cc, tag):
    with nc.Block(tag) as blk:
        @blk.gpsimd
        def _(g):
            for (h, n) in deps:
                g.wait_ge(h, n)
            with allow_ici_all_to_all():
                for kind, src, dst in ops:
                    g.collective_compute(kind, mybir.AluOpType.bypass,
                                         replica_groups=RG, ins=[src.opt()],
                                         outs=[dst.opt()]).then_inc(cc.h, 1)
    cc.n += len(ops)


def build_fused():
    nc = bass.Bass("TRN2", target_bir_lowering=False)
    DT = lambda n, sh, dt, kind: nc.dram_tensor(n, sh, dt, kind=kind).ap()
    x_ap = DT("x", [S, D], F32, "ExternalInput")
    wq = DT("wq", [D, 256], F32, "ExternalInput")
    wqc = DT("wqc", [D, 256], F32, "ExternalInput")
    wk = DT("wk", [D, 256], F32, "ExternalInput")
    wkc = DT("wkc", [D, 256], F32, "ExternalInput")
    wv = DT("wv", [D, D], F32, "ExternalInput")
    wvc = DT("wvc", [D, D], F32, "ExternalInput")
    bq = DT("bq", [256, 1], F32, "ExternalInput")
    bqc = DT("bqc", [256, 1], F32, "ExternalInput")
    bk = DT("bk", [256, 1], F32, "ExternalInput")
    bkc = DT("bkc", [256, 1], F32, "ExternalInput")
    bv = DT("bv", [1, D], F32, "ExternalInput")
    bvc = DT("bvc", [1, D], F32, "ExternalInput")
    wgs = DT("wgs", [16, 128, 1024], F32, "ExternalInput")
    wus = DT("wus", [16, 128, 1024], F32, "ExternalInput")
    wds = DT("wds", [2048, D], F32, "ExternalInput")
    wge = DT("wge", [8, 16, 128, 1024], F32, "ExternalInput")
    wue = DT("wue", [8, 16, 128, 1024], F32, "ExternalInput")
    wde = DT("wde", [8, 2048, D], F32, "ExternalInput")
    wr = DT("wr", [D, 8], F32, "ExternalInput")
    rb = DT("rb", [1, 8], F32, "ExternalInput")
    wkt = DT("wkt", [D, D], F32, "ExternalInput")
    wvt = DT("wvt", [D, D], F32, "ExternalInput")
    wqt = DT("wqt", [D, D], F32, "ExternalInput")
    mask = DT("mask", [S, QW], I8, "ExternalInput")
    sut = DT("sut", [2, 4, 128, 1024], F32, "ExternalInput")
    sdt = DT("sdt", [2, 512, D], F32, "ExternalInput")
    rut = DT("rut", [16, 4, 128, 1024], F32, "ExternalInput")
    rdt = DT("rdt", [16, 512, D], F32, "ExternalInput")
    rwt = DT("rwt", [D, 16], F32, "ExternalInput")
    y_ap = DT("y", [S, D], F32, "ExternalOutput")

    xnf = DT("xnf", [D, SP], BF16, "Internal")
    s_f = DT("s_f", [D, S], F32, "Internal")
    x1 = DT("x1", [S, D], F32, "Internal")
    xn1f = DT("xn1f", [D, S], BF16, "Internal")
    xn1send = DT("xn1send", [8 * D, QW], BF16, "Internal")
    xn1q = DT("xn1q", [8 * D, QW], BF16, "Internal")
    k1loc = DT("k1loc", [S, D], BF16, "Internal")
    k1all = DT("k1all", [B * S, D], BF16, "Internal")
    v1loc = DT("v1loc", [S, D], BF16, "Internal")
    a_send = DT("a_send", [B * S, QW], BF16, "Internal")
    a_recv = DT("a_recv", [B * S, QW], BF16, "Internal")
    g1 = DT("g1", [S, D], BF16, "Internal")

    cc = Ctr(nc, "cc")

    f1 = emit_ln(nc, x_ap, xnf, "ln0", SP)
    fence_all(nc, f1, "fn1")
    f2 = emit_l0attn(nc, xnf, (wq, wqc, wk, wkc, wv, wvc),
                     (bq, bqc, bk, bkc, bv, bvc), s_f, "a0")
    fence_all(nc, f2, "fn2")
    f3 = emit_l0ffn(nc, xnf, s_f, x_ap, (wgs, wus, wds, wge, wue, wde, wr, rb),
                    x1, xn1f, "f0", xn1send_ap=xn1send)
    f4 = emit_kvproj(nc, xn1f, wkt, wvt, k1loc, v1loc, "kv")
    fence_all(nc, f3 + f4, "fn3")
    emit_collectives(nc, [("AllToAll", xn1send, xn1q),
                          ("AllGather", k1loc, k1all)], [], cc, "cc1")
    f5 = emit_l1ffn_g(nc, xn1f, (sut, sdt, rut, rdt, rwt), g1, "e2")
    fence_all(nc, f5 + [(cc.h, cc.n)], "fn4")
    f6 = emit_l1scores(nc, xn1q, wqt, k1all, mask, a_send, "e1",
                       a_recv=a_recv, cc=cc)
    fence_all(nc, f6, "fn5")
    emit_pv_final(nc, a_recv, v1loc, g1, x1, y_ap, "pf", cc=cc, ccbase=2)
    return nc


_FUSED = None


def _fused_prog():
    global _FUSED
    if _FUSED is None:
        _FUSED = build_fused()
    return _FUSED


def kernel(**inputs):
    T = lambda a: np.ascontiguousarray(np.asarray(a).swapaxes(-1, -2))
    C = lambda a: np.ascontiguousarray(np.asarray(a))

    def WT(a):
        a = np.asarray(a)
        Dd, Ii = a.shape[-2], a.shape[-1]
        lead = a.shape[:-2]
        a = a.reshape(lead + (Dd // 128, 128, Ii // 128, 128))
        nl = len(lead)
        perm = tuple(range(nl)) + (nl + 2, nl + 1, nl, nl + 3)
        return np.ascontiguousarray(
            a.transpose(perm).reshape(lead + (Ii // 128, 128, (Dd // 128) * 128)))
    x = np.asarray(inputs["L0_attn_x"])

    base = {
        "wq": C(inputs["L0_attn_Wq"]), "wqc": C(inputs["L0_attn_Wqc"]),
        "wk": C(inputs["L0_attn_Wk"]), "wkc": C(inputs["L0_attn_Wkc"]),
        "wv": C(inputs["L0_attn_Wv"]), "wvc": C(inputs["L0_attn_Wvc"]),
        "bq": np.asarray(inputs["L0_attn_bq"]).reshape(256, 1).copy(),
        "bqc": np.asarray(inputs["L0_attn_bqc"]).reshape(256, 1).copy(),
        "bk": np.asarray(inputs["L0_attn_bk"]).reshape(256, 1).copy(),
        "bkc": np.asarray(inputs["L0_attn_bkc"]).reshape(256, 1).copy(),
        "bv": np.asarray(inputs["L0_attn_bv"]).reshape(1, D).copy(),
        "bvc": np.asarray(inputs["L0_attn_bvc"]).reshape(1, D).copy(),
        "wgs": WT(T(inputs["L0_ffn_Wg_s"])), "wus": WT(T(inputs["L0_ffn_Wu_s"])),
        "wds": T(inputs["L0_ffn_Wd_s"]),
        "wge": WT(T(inputs["L0_ffn_Wg"])), "wue": WT(T(inputs["L0_ffn_Wu"])),
        "wde": T(inputs["L0_ffn_Wd"]),
        "wr": T(inputs["L0_ffn_Wr"]),
        "rb": np.asarray(inputs["L0_ffn_rb"]).reshape(1, 8).copy(),
        "wkt": T(inputs["L1_attn_Wk"]), "wvt": T(inputs["L1_attn_Wv"]),
        "wqt": T(inputs["L1_attn_Wq"]),
        "sut": WT(T(inputs["L1_ffn_shared_up"])), "sdt": T(inputs["L1_ffn_shared_down"]),
        "rut": WT(T(inputs["L1_ffn_routed_up"])), "rdt": T(inputs["L1_ffn_routed_down"]),
        "rwt": T(inputs["L1_ffn_router_w"]),
    }
    jj = np.arange(S)[:, None]
    ii = np.arange(QW)[None, :]
    in_maps = []
    for c in range(NCORES):
        maskc = (jj <= c * QW + ii).astype(np.int8)
        in_maps.append(dict(base, x=C(x[c]), mask=maskc))
    res = run_bass_kernel_spmd(_fused_prog(), in_maps, core_ids=list(range(NCORES)))
    return np.stack([res.results[c]["y"] for c in range(NCORES)])


# version 10 (final)
# speedup vs baseline: 1.1735x; 1.0104x over previous
import numpy as np
import ml_dtypes
from contextlib import ExitStack

import concourse.bass as bass
import concourse.mybir as mybir
from concourse.bass_utils import run_bass_kernel_spmd
from concourse.aot_env import allow_ici_all_to_all

F32 = mybir.dt.float32
BF16 = mybir.dt.bfloat16
I8 = mybir.dt.int8
AF = mybir.ActivationFunctionType
OP = mybir.AluOpType
AX = mybir.AxisListType.X

B = 8
S = 4096
SP = 4112
D = 1024
NCORES = 8
QW = S // NCORES
LN_EPS = 1e-5


class Ctr:
    def __init__(self, nc, name):
        self.h = nc.alloc_semaphore(name=name)
        self.n = 0

    def inc(self, inst, k=1):
        inst.then_inc(self.h, k)
        self.n += k
        return self.n


def _mk_identity(g, ap, ctr):
    ctr.inc(g.memset(ap, 0.0))
    g.wait_ge(ctr.h, ctr.n)
    return g.affine_select(out=ap, in_=ap, compare_op=OP.not_equal, fill=1.0,
                           base=0, pattern=[[-1, ap.shape[1]]], channel_multiplier=1)


def build_ln0():
    nc = bass.Bass("TRN2", target_bir_lowering=False)
    x_ap = nc.dram_tensor("x", [S, D], F32, kind="ExternalInput").ap()
    xnf_ap = nc.dram_tensor("xnf", [D, SP], BF16, kind="ExternalOutput").ap()
    emit_ln(nc, x_ap, xnf_ap, "ln0", SP)
    return nc


def emit_ln(nc, x_ap, xnf_ap, tag, tcols):
    NT = S // 128
    with ExitStack() as es:
        sb = lambda n, sh, dt: es.enter_context(nc.sbuf_tensor(tag + n, sh, dt)).ap()
        ps = lambda n, sh, dt: es.enter_context(nc.psum_tensor(tag + n, sh, dt)).ap()
        xt = [sb(f"xt{i}", [128, D], F32) for i in range(3)]
        sq = sb("sq", [128, D], F32)
        stt = [sb(f"stt{i}", [128, 8], F32) for i in range(3)]
        epst = sb("epst", [128, 1], F32)
        xn = [sb(f"xn{i}", [128, D], BF16) for i in range(2)]
        ptr = [ps(f"ptr{i}", [128, D], BF16) for i in range(2)]
        stw = [sb(f"stw{i}", [128, 8 * 1024], BF16) for i in range(2)]
        zpad = sb("zpad", [128, 16], BF16)
        ident = sb("id", [128, 128], BF16)

        lds = [Ctr(nc, tag + f"ld{i}") for i in range(3)]
        ac = Ctr(nc, tag + "ac")
        dv = Ctr(nc, tag + "dv")
        pe = Ctr(nc, tag + "pe")
        ev = Ctr(nc, tag + "ev")
        sos = [Ctr(nc, tag + f"so{i}") for i in range(2)]
        sop = Ctr(nc, tag + "sop")
        gi_ = Ctr(nc, tag + "gi")

        xn_done = {}
        stage_done = {}
        tr_fence = {}
        out_done_grp = {}

        with nc.Block(tag) as blk:
            @blk.gpsimd
            def _(g):
                g.memset(epst, LN_EPS)
                g.memset(zpad, 0.0)
                _mk_identity(g, ident, gi_).then_inc(gi_.h, 1)
            gi_.n += 1
            gi_all = gi_.n

            if tcols > S:
                @blk.sync
                def _(sy):
                    sy.wait_ge(gi_.h, gi_all)
                    for j in range(8):
                        sy.dma_start(out=xnf_ap[j * 128:(j + 1) * 128, S:tcols],
                                     in_=zpad).then_inc(sop.h, 16)
                sop.n += 16 * 8

            for i in range(NT):
                ldc = lds[i % 3]

                @blk.sync
                def _(sy, i=i, ldc=ldc):
                    if i >= 3:
                        sy.wait_ge(dv.h, xn_done[i - 3])
                    sy.dma_start(out=xt[i % 3],
                                 in_=x_ap[i * 128:(i + 1) * 128, :]).then_inc(ldc.h, 16)
                ldc.n += 16
                ld_i = ldc.n
                b = stt[i % 3]

                @blk.scalar
                def _(sc, i=i, b=b, ld_i=ld_i, ldc=ldc):
                    sc.wait_ge(ldc.h, ld_i)
                    if i >= 3:
                        sc.wait_ge(dv.h, xn_done[i - 3])
                    ac.inc(sc.activation(out=sq, in_=xt[i % 3], func=AF.Square,
                                         accum_out=b[:, 1:2]))
                s2_ready = ac.n

                @blk.vector
                def _(v, i=i, b=b, ld_i=ld_i, s2=s2_ready, ldc=ldc):
                    v.wait_ge(ldc.h, ld_i)
                    if i >= 3:
                        v.wait_ge(dv.h, xn_done[i - 3])
                    dv.inc(v.reduce_sum(b[:, 0:1], xt[i % 3], axis=AX))
                    v.wait_ge(dv.h, dv.n)
                    dv.inc(v.tensor_scalar(out=b[:, 2:3], in0=b[:, 0:1],
                                           scalar1=1.0 / D, scalar2=None,
                                           op0=OP.mult))
                    v.wait_ge(dv.h, dv.n)
                    dv.inc(v.scalar_tensor_tensor(out=b[:, 5:6], in0=b[:, 2:3],
                                                  scalar=1.0, in1=b[:, 2:3],
                                                  op0=OP.mult, op1=OP.mult))
                    v.wait_ge(ac.h, s2)
                    v.wait_ge(dv.h, dv.n)
                    dv.inc(v.scalar_tensor_tensor(out=b[:, 3:4], in0=b[:, 1:2],
                                                  scalar=1.0 / D, in1=b[:, 5:6],
                                                  op0=OP.mult, op1=OP.subtract))
                var_ready = dv.n

                @blk.scalar
                def _(sc, b=b, var_ready=var_ready):
                    sc.wait_ge(dv.h, var_ready)
                    ac.inc(sc.activation(out=b[:, 4:5], in_=b[:, 3:4], func=AF.Sqrt,
                                         bias=epst[:, 0:1]))
                std_ready = ac.n

                @blk.vector
                def _(v, i=i, b=b, std_ready=std_ready):
                    v.wait_ge(ac.h, std_ready)
                    dv.inc(v.reciprocal(b[:, 5:6], b[:, 4:5]))
                    v.wait_ge(dv.h, dv.n)
                    if i >= 2:
                        v.wait_ge(pe.h, tr_fence[i - 2])
                    dv.inc(v.tensor_scalar(out=xn[i % 2], in0=xt[i % 3],
                                           scalar1=b[:, 2:3], scalar2=b[:, 5:6],
                                           op0=OP.subtract, op1=OP.mult))
                xn_done[i] = dv.n
                xnr = dv.n

                @blk.tensor
                def _(t, i=i, xnr=xnr):
                    t.wait_ge(gi_.h, gi_all)
                    t.wait_ge(dv.h, xnr)
                    if i >= 2:
                        t.wait_ge(ev.h, stage_done[i - 2])
                    for j in range(8):
                        ins = t.transpose(ptr[i % 2][:, j * 128:(j + 1) * 128],
                                          xn[i % 2][:, j * 128:(j + 1) * 128], ident)
                    pe.inc(ins)
                tr_done = pe.n
                tr_fence[i] = tr_done
                g = i // 8
                ti = i % 8

                @blk.vector
                def _(v, i=i, g=g, ti=ti, tr=tr_done):
                    v.wait_ge(pe.h, tr)
                    if g >= 2 and ti == 0:
                        v.wait_ge(sos[g % 2].h, out_done_grp[g - 2])
                    ev.inc(v.tensor_copy(
                        stw[g % 2].rearrange("p (j tb t) -> p j tb t",
                                             j=8, tb=8)[:, :, ti, :],
                        ptr[i % 2].rearrange("p (j t) -> p j t", j=8)))
                stage_done[i] = ev.n
                stg = ev.n

                if ti == 7:
                    @blk.sync
                    def _(sy, g=g, stg=stg):
                        sy.wait_ge(ev.h, stg)
                        for j in range(8):
                            sy.dma_start(
                                out=xnf_ap[j * 128:(j + 1) * 128,
                                           g * 1024:(g + 1) * 1024],
                                in_=stw[g % 2][:, j * 1024:(j + 1) * 1024])\
                              .then_inc(sos[g % 2].h, 16)
                    sos[g % 2].n += 16 * 8
                    out_done_grp[g] = sos[g % 2].n

            @blk.sync
            def _(sy):
                sy.wait_ge(sos[0].h, sos[0].n)
                sy.wait_ge(sos[1].h, sos[1].n)
                if tcols > S:
                    sy.wait_ge(sop.h, sop.n)
        fences = [(sos[0].h, sos[0].n), (sos[1].h, sos[1].n)]
        if tcols > S:
            fences.append((sop.h, sop.n))
        return fences


def build_l0attn(debug_qkv=False):
    nc = bass.Bass("TRN2", target_bir_lowering=False)
    xnf = nc.dram_tensor("xnf", [D, SP], BF16, kind="ExternalInput").ap()
    wq = nc.dram_tensor("wq", [D, 256], F32, kind="ExternalInput").ap()
    wqc = nc.dram_tensor("wqc", [D, 256], F32, kind="ExternalInput").ap()
    wk = nc.dram_tensor("wk", [D, 256], F32, kind="ExternalInput").ap()
    wkc = nc.dram_tensor("wkc", [D, 256], F32, kind="ExternalInput").ap()
    wv = nc.dram_tensor("wv", [D, D], F32, kind="ExternalInput").ap()
    wvc = nc.dram_tensor("wvc", [D, D], F32, kind="ExternalInput").ap()
    bq = nc.dram_tensor("bq", [256, 1], F32, kind="ExternalInput").ap()
    bqc = nc.dram_tensor("bqc", [256, 1], F32, kind="ExternalInput").ap()
    bk = nc.dram_tensor("bk", [256, 1], F32, kind="ExternalInput").ap()
    bkc = nc.dram_tensor("bkc", [256, 1], F32, kind="ExternalInput").ap()
    bv = nc.dram_tensor("bv", [1, D], F32, kind="ExternalInput").ap()
    bvc = nc.dram_tensor("bvc", [1, D], F32, kind="ExternalInput").ap()
    s_f = nc.dram_tensor("s_f", [D, S], F32, kind="ExternalOutput").ap()
    dbg = None
    if debug_qkv:
        dbg = (nc.dram_tensor("dbg_q", [128, 2 * S], BF16, kind="ExternalOutput").ap(),
               nc.dram_tensor("dbg_k", [128, 2 * SP], BF16, kind="ExternalOutput").ap(),
               nc.dram_tensor("dbg_v", [128, 33 * 1040], BF16, kind="ExternalOutput").ap())
    emit_l0attn(nc, xnf, (wq, wqc, wk, wkc, wv, wvc),
                (bq, bqc, bk, bkc, bv, bvc), s_f, "a0", dbg=dbg)
    return nc


def emit_l0attn(nc, xnf, ws, bs, s_f, tag, dbg=None):
    wq, wqc, wk, wkc, wv, wvc = ws
    bq, bqc, bk, bkc, bv, bvc = bs
    NH = 16
    QL = 16
    VL = 64
    NKC = SP // 128
    KCW = [128] * 32 + [16]
    GRP3 = [(3 * g, min(3, 33 - 3 * g)) for g in range((33 + 2) // 3)]

    with ExitStack() as es:
        sb = lambda n, sh, dt: es.enter_context(nc.sbuf_tensor(tag + n, sh, dt)).ap()
        ps = lambda n, sh, dt: es.enter_context(nc.psum_tensor(tag + n, sh, dt)).ap()
        q_sb = sb("q", [128, 2 * S], BF16)
        k_sb = sb("k", [128, 2 * SP], BF16)
        v_sb = sb("v", [128, 33 * 1040], BF16)

        ac = Ctr(nc, tag + "ac")
        dv = Ctr(nc, tag + "dv")
        pe = Ctr(nc, tag + "pe")
        so = Ctr(nc, tag + "so")

        with ExitStack() as es2:
            sb2 = lambda n, sh, dt: es2.enter_context(nc.sbuf_tensor(tag + n, sh, dt)).ap()
            xnc = [sb2(f"xnc{i}", [128, 8 * 512], BF16) for i in range(2)]
            wq_sb = sb2("wqs", [128, 8 * 256], BF16)
            wqc_sb = sb2("wqcs", [128, 8 * 256], BF16)
            wk_sb = sb2("wks", [128, 8 * 256], BF16)
            wkc_sb = sb2("wkcs", [128, 8 * 256], BF16)
            wv_sb = sb2("wvs", [128, 8 * 1024], BF16)
            wvc_sb = sb2("wvcs", [128, 8 * 1024], BF16)
            bq_sb = sb2("bqs", [128, 2], F32)
            bqc_sb = sb2("bqcs", [128, 2], F32)
            bk_sb = sb2("bks", [128, 2], F32)
            bkc_sb = sb2("bkcs", [128, 2], F32)
            brow = sb2("brow", [1, 2 * D], F32)
            ones1 = sb2("ones1", [1, 128], F32)
            bvb = sb2("bvb", [128, D], F32)
            bvcb = sb2("bvcb", [128, D], F32)
            tmp1 = sb2("tmp1", [128, 512], F32)
            tmp2 = sb2("tmp2", [128, 512], F32)
            tmp3 = sb2("tmp3", [128, 512], F32)
            ps2 = lambda n, sh, dt: es2.enter_context(nc.psum_tensor(tag + n, sh, dt)).ap()
            pql = ps2("pql", [128, 512], F32)
            pqg = ps2("pqg", [128, 512], F32)
            pvl = ps2("pvl", [128, 512], F32)
            pvg = ps2("pvg", [128, 512], F32)
            pbc = ps2("pbc", [128, 1024], F32)

            wgld = Ctr(nc, tag + "wgld")
            gmk = Ctr(nc, tag + "gmk")

            @_one_block(nc, tag + "prep")
            def _(blk):

                @blk.gpsimd
                def _(g):
                    g.memset(ones1, 1.0).then_inc(gmk.h, 1)
                    g.memset(v_sb[:, :], 0.0).then_inc(gmk.h, 1)
                    g.wait_ge(gmk.h, 2)
                    for kc in range(33):
                        ap = v_sb[:, kc * 1040:(kc + 1) * 1040]\
                            .rearrange("p (h c) -> p h c", h=16, c=65)[:, :, 64:65]
                        ins = g.memset(ap, 1.0)
                    ins.then_inc(gmk.h, 1)
                    for hc in range(8):
                        g.dma_start(out=wq_sb[:, hc * 256:(hc + 1) * 256],
                                    in_=wq[hc * 128:(hc + 1) * 128, :]).then_inc(wgld.h, 16)
                        g.dma_start(out=wqc_sb[:, hc * 256:(hc + 1) * 256],
                                    in_=wqc[hc * 128:(hc + 1) * 128, :]).then_inc(wgld.h, 16)
                        g.dma_start(out=wk_sb[:, hc * 256:(hc + 1) * 256],
                                    in_=wk[hc * 128:(hc + 1) * 128, :]).then_inc(wgld.h, 16)
                        g.dma_start(out=wkc_sb[:, hc * 256:(hc + 1) * 256],
                                    in_=wkc[hc * 128:(hc + 1) * 128, :]).then_inc(wgld.h, 16)
                        g.dma_start(out=wv_sb[:, hc * 1024:(hc + 1) * 1024],
                                    in_=wv[hc * 128:(hc + 1) * 128, :]).then_inc(wgld.h, 16)
                        g.dma_start(out=wvc_sb[:, hc * 1024:(hc + 1) * 1024],
                                    in_=wvc[hc * 128:(hc + 1) * 128, :]).then_inc(wgld.h, 16)
                wgld.n += 16 * 48
                gmk.n += 3

                bld = Ctr(nc, tag + "bld")

                @blk.sync
                def _(sy):
                    for name, srcw, dst in [("bq", bq, bq_sb), ("bqc", bqc, bqc_sb),
                                            ("bk", bk, bk_sb), ("bkc", bkc, bkc_sb)]:
                        for oc in range(2):
                            sy.dma_start(out=dst[:, oc:oc + 1],
                                         in_=srcw[oc * 128:(oc + 1) * 128, :])\
                              .then_inc(bld.h, 16)
                    sy.dma_start(out=brow[0:1, 0:D], in_=bv).then_inc(bld.h, 16)
                    sy.dma_start(out=brow[0:1, D:2 * D], in_=bvc).then_inc(bld.h, 16)
                bld.n += 16 * 10
                wld = bld.n

                @blk.tensor
                def _(t, wld=wld):
                    t.wait_ge(bld.h, wld)
                    t.wait_ge(gmk.h, 1)
                    t.matmul(pbc[:, 0:512], ones1[0:1, :], brow[0:1, 0:512],
                             start=True, stop=True)
                    pe.inc(t.matmul(pbc[:, 512:1024], ones1[0:1, :],
                                    brow[0:1, 512:1024], start=True, stop=True))
                bv_pe = pe.n

                @blk.vector
                def _(v, bv_pe=bv_pe):
                    v.wait_ge(pe.h, bv_pe)
                    dv.inc(v.tensor_copy(bvb, pbc[:, 0:1024]))
                bvb_dv = dv.n

                @blk.tensor
                def _(t, bvb_dv=bvb_dv):
                    t.wait_ge(dv.h, bvb_dv)
                    t.matmul(pbc[:, 0:512], ones1[0:1, :], brow[0:1, D:D + 512],
                             start=True, stop=True)
                    pe.inc(t.matmul(pbc[:, 512:1024], ones1[0:1, :],
                                    brow[0:1, D + 512:2 * D], start=True, stop=True))
                bvc_pe = pe.n

                @blk.vector
                def _(v, bvc_pe=bvc_pe):
                    v.wait_ge(pe.h, bvc_pe)
                    dv.inc(v.tensor_copy(bvcb, pbc[:, 0:1024]))
                bvcb_dv = dv.n

                NTT = 9
                xlds = [Ctr(nc, tag + f"xld{i}") for i in range(2)]
                evac_done = {}
                for tt in range(NTT):
                    t0 = tt * 512
                    tw = min(512, SP - t0)
                    xldc = xlds[tt % 2]

                    @blk.sync
                    def _(sy, tt=tt, t0=t0, tw=tw, xldc=xldc):
                        if tt >= 2:
                            sy.wait_ge(dv.h, evac_done[tt - 2])
                        for hc in range(8):
                            sy.dma_start(out=xnc[tt % 2][:, hc * 512:hc * 512 + tw],
                                         in_=xnf[hc * 128:(hc + 1) * 128, t0:t0 + tw])\
                              .then_inc(xldc.h, 16)
                    xldc.n += 16 * 8
                    xl = xldc.n

                    for kind, wlin, wgate, blin, bgate, dst, dw in (
                            [("q", wq_sb, wqc_sb, bq_sb, bqc_sb, q_sb, S)] if tt < 8 else []
                    ) + [("k", wk_sb, wkc_sb, bk_sb, bkc_sb, k_sb, SP)]:
                        for oc in range(2):
                            @blk.tensor
                            def _(t, tt=tt, tw=tw, oc=oc, wlin=wlin, wgate=wgate, xl=xl, xldc=xldc):
                                t.wait_ge(xldc.h, xl)
                                t.wait_ge(wgld.h, wgld.n)
                                t.wait_ge(ac.h, ac.n)
                                t.wait_ge(dv.h, dv.n)
                                for hc in range(8):
                                    t.matmul(pql[:, 0:tw],
                                             wlin[:, hc * 256 + oc * 128: hc * 256 + (oc + 1) * 128],
                                             xnc[tt % 2][:, hc * 512:hc * 512 + tw],
                                             start=(hc == 0), stop=(hc == 7))
                                for hc in range(8):
                                    ins = t.matmul(pqg[:, 0:tw],
                                                   wgate[:, hc * 256 + oc * 128: hc * 256 + (oc + 1) * 128],
                                                   xnc[tt % 2][:, hc * 512:hc * 512 + tw],
                                                   start=(hc == 0), stop=(hc == 7))
                                pe.inc(ins)
                            mm = pe.n

                            @blk.scalar
                            def _(sc, tw=tw, oc=oc, bgate=bgate, mm=mm):
                                sc.wait_ge(pe.h, mm)
                                sc.wait_ge(dv.h, dv.n)
                                ac.inc(sc.activation(out=tmp2[:, 0:tw], in_=pqg[:, 0:tw],
                                                     func=AF.Sigmoid,
                                                     bias=bgate[:, oc:oc + 1]))
                            gq = ac.n

                            @blk.vector
                            def _(v, t0=t0, tw=tw, oc=oc, blin=blin, dst=dst, dw=dw, gq=gq):
                                v.wait_ge(ac.h, gq)
                                dv.inc(v.scalar_tensor_tensor(
                                    out=dst[:, oc * dw + t0: oc * dw + t0 + tw],
                                    in0=pql[:, 0:tw], scalar=blin[:, oc:oc + 1],
                                    in1=tmp2[:, 0:tw], op0=OP.add, op1=OP.mult))

                    nsub = (tw + 127) // 128
                    for su in range(nsub):
                        stw = min(128, tw - su * 128)
                        kc = (t0 + su * 128) // 128
                        for half in range(2):
                            @blk.tensor
                            def _(t, tt=tt, su=su, stw=stw, half=half, xl=xl, xldc=xldc):
                                t.wait_ge(xldc.h, xl)
                                t.wait_ge(wgld.h, wgld.n)
                                t.wait_ge(dv.h, dv.n)
                                for hc in range(8):
                                    t.matmul(pvl[0:stw, :],
                                             xnc[tt % 2][:, hc * 512 + su * 128: hc * 512 + su * 128 + stw],
                                             wv_sb[:, hc * 1024 + half * 512: hc * 1024 + (half + 1) * 512],
                                             start=(hc == 0), stop=(hc == 7))
                                    ins = t.matmul(pvg[0:stw, :],
                                                   xnc[tt % 2][:, hc * 512 + su * 128: hc * 512 + su * 128 + stw],
                                                   wvc_sb[:, hc * 1024 + half * 512: hc * 1024 + (half + 1) * 512],
                                                   start=(hc == 0), stop=(hc == 7))
                                pe.inc(ins)
                            vm = pe.n

                            @blk.vector
                            def _(v, stw=stw, half=half, vm=vm):
                                v.wait_ge(pe.h, vm)
                                dv.inc(v.tensor_add(tmp1[0:stw, :], pvg[0:stw, :],
                                                    bvcb[0:stw, half * 512:(half + 1) * 512]))
                            g1 = dv.n

                            @blk.scalar
                            def _(sc, stw=stw, g1=g1):
                                sc.wait_ge(dv.h, dv.n)
                                ac.inc(sc.activation(out=tmp2[0:stw, :], in_=tmp1[0:stw, :],
                                                     func=AF.Sigmoid))
                            g2 = ac.n

                            @blk.vector
                            def _(v, stw=stw, half=half, kc=kc, g2=g2):
                                v.wait_ge(gmk.h, 3)
                                dv.inc(v.tensor_add(tmp3[0:stw, :], pvl[0:stw, :],
                                                    bvb[0:stw, half * 512:(half + 1) * 512]))
                                v.wait_ge(ac.h, g2)
                                v.wait_ge(dv.h, dv.n)
                                dst = v_sb[0:stw, kc * 1040:(kc + 1) * 1040]\
                                    .rearrange("p (h c) -> p h c", h=16, c=65)\
                                    [:, half * 8:(half + 1) * 8, 0:64]
                                src = tmp3[0:stw, :].rearrange("p (h c) -> p h c", h=8, c=64)
                                dv.inc(v.tensor_mul(dst, src,
                                    tmp2[0:stw, :].rearrange("p (h c) -> p h c", h=8, c=64)))
                    evac_done[tt] = dv.n

        if dbg is not None:
            dso = Ctr(nc, tag + "dso")
            with nc.Block(tag + "dbg") as blk:
                @blk.sync
                def _(sy):
                    sy.dma_start(out=dbg[0], in_=q_sb[:, :]).then_inc(dso.h, 16)
                    sy.dma_start(out=dbg[1], in_=k_sb[:, :]).then_inc(dso.h, 16)
                    sy.dma_start(out=dbg[2], in_=v_sb[:, :]).then_inc(dso.h, 16)
                    sy.wait_ge(dso.h, 48)
            return

        prep_dv = dv.n
        prep_ac = ac.n

        with ExitStack() as es3:
            sb3 = lambda n, sh, dt: es3.enter_context(nc.sbuf_tensor(tag + n, sh, dt)).ap()
            krep = [sb3(f"krep{i}", [128, SP], BF16) for i in range(2)]
            qrep = [sb3(f"qrep{i}", [128, S], BF16) for i in range(2)]
            e_sb = [sb3(f"e{i}", [128, 1536], BF16) for i in range(2)]
            den = sb3("den", [1, 512], F32)
            rden = sb3("rden", [1, 512], F32)
            bc_sb = sb3("bc", [64, 512], F32)
            sstg = sb3("sstg", [64, 512], F32)
            ones64 = sb3("o64", [1, 64], F32)
            ps3 = lambda n, sh, dt: es3.enter_context(nc.psum_tensor(tag + n, sh, dt)).ap()
            ppa = ps3("ppa", [128, 1536], F32)
            ppb = ps3("ppb", [128, 1536], F32)
            ppv = ps3("ppv", [65, 512], F32)
            pbc2 = ps3("pbc2", [64, 512], F32)
            pp = [ppa, ppb]

            rep = Ctr(nc, tag + "rep")

            @_one_block(nc, tag + "main")
            def _(blk):
                mk = Ctr(nc, tag + "mk")

                @blk.gpsimd
                def _(g):
                    g.memset(ones64, 1.0).then_inc(mk.h, 1)
                mk.n += 1

                head_end = {}
                for h in range(NH):
                    oc, row = h // 8, (h % 8) * 16

                    @blk.gpsimd
                    def _(g, h=h, oc=oc, row=row):
                        g.wait_ge(dv.h, prep_dv)
                        if h >= 2:
                            g.wait_ge(pe.h, head_end[h - 2])
                        for i in range(3):
                            g.dma_start(out=krep[h % 2][32 * i:32 * i + 16, :],
                                        in_=k_sb[row:row + 16, oc * SP:(oc + 1) * SP])\
                                .then_inc(rep.h, 16)
                            g.dma_start(out=qrep[h % 2][32 * i:32 * i + 16, :],
                                        in_=q_sb[row:row + 16, oc * S:(oc + 1) * S])\
                                .then_inc(rep.h, 16)
                    rep.n += 16 * 6
                    repd = rep.n

                    for tq in range(8):
                        ngrp = len(GRP3)
                        exp_done = {}
                        pv_done_grp = {}
                        qk_pe = {}
                        for gi in range(ngrp + 1):
                            if gi < ngrp:
                                c0, gw = GRP3[gi]

                                @blk.tensor
                                def _(t, h=h, tq=tq, gi=gi, c0=c0, gw=gw, repd=repd):
                                    t.wait_ge(rep.h, repd)
                                    if gi >= 2:
                                        t.wait_ge(ac.h, exp_done[gi - 2])
                                    for i in range(gw):
                                        kc = c0 + i
                                        kw = KCW[kc]
                                        ins = t.matmul(
                                            pp[gi % 2][0:kw, i * 512:(i + 1) * 512],
                                            krep[h % 2][32 * i:32 * i + 16, kc * 128:kc * 128 + kw],
                                            qrep[h % 2][32 * i:32 * i + 16, tq * 512:(tq + 1) * 512],
                                            start=True, stop=True,
                                            tile_position=(32 * i, 0))
                                    pe.inc(ins)
                                qk_pe[gi] = pe.n

                            if gi >= 1:
                                gj = gi - 1
                                c0j, gwj = GRP3[gj]

                                @blk.tensor
                                def _(t, h=h, gj=gj, c0j=c0j, gwj=gwj):
                                    t.wait_ge(ac.h, exp_done[gj])
                                    if gj == 0:
                                        t.wait_ge(gmk.h, 3)
                                        t.wait_ge(dv.h, dv.n)
                                    for i in range(gwj):
                                        kc = c0j + i
                                        kw = KCW[kc]
                                        ins = t.matmul(
                                            ppv[:, :],
                                            v_sb[0:kw, kc * 1040 + 65 * h: kc * 1040 + 65 * h + 65],
                                            e_sb[gj % 2][0:kw, i * 512:(i + 1) * 512],
                                            start=(gj == 0 and i == 0),
                                            stop=(gj == ngrp - 1 and i == gwj - 1))
                                    pe.inc(ins)
                                pv_done_grp[gj] = pe.n

                            if gi < ngrp:
                                c0, gw = GRP3[gi]
                                qk = qk_pe[gi]

                                @blk.scalar
                                def _(sc, gi=gi, c0=c0, gw=gw, qk=qk):
                                    sc.wait_ge(pe.h, qk)
                                    if gi >= 2:
                                        sc.wait_ge(pe.h, pv_done_grp.get(gi - 2, 0))
                                    ac.inc(sc.activation(out=e_sb[gi % 2][:, 0:gw * 512],
                                                         in_=pp[gi % 2][:, 0:gw * 512],
                                                         func=AF.Exp, scale=0.125))
                                exp_done[gi] = ac.n
                        pvd = pe.n

                        sworn = so.n
                        @blk.vector
                        def _(v, pvd=pvd, sworn=sworn):
                            v.wait_ge(pe.h, pvd)
                            v.wait_ge(so.h, sworn)
                            dv.inc(v.tensor_copy(den, ppv[64:65, :]))
                            v.wait_ge(dv.h, dv.n)
                            dv.inc(v.reciprocal(rden, den))
                        rd = dv.n

                        @blk.tensor
                        def _(t, rd=rd):
                            t.wait_ge(mk.h, 1)
                            t.wait_ge(dv.h, rd)
                            pe.inc(t.matmul(pbc2, ones64, rden, start=True, stop=True))
                        bcd = pe.n

                        @blk.vector
                        def _(v, bcd=bcd):
                            v.wait_ge(pe.h, bcd)
                            dv.inc(v.tensor_copy(bc_sb, pbc2))
                            v.wait_ge(dv.h, dv.n)
                            dv.inc(v.tensor_mul(sstg, ppv[0:64, :], bc_sb))
                        sd = dv.n

                        @blk.sync
                        def _(sy, h=h, tq=tq, sd=sd):
                            sy.wait_ge(dv.h, sd)
                            sy.dma_start(out=s_f[h * 64:(h + 1) * 64,
                                                 tq * 512:(tq + 1) * 512],
                                         in_=sstg).then_inc(so.h, 16)
                        so.n += 16
                    head_end[h] = pe.n

                @blk.sync
                def _(sy):
                    sy.wait_ge(so.h, so.n)
    return [(so.h, so.n)]


def build_l0ffn():
    nc = bass.Bass("TRN2", target_bir_lowering=False)
    xnf = nc.dram_tensor("xnf", [D, SP], BF16, kind="ExternalInput").ap()
    s_f = nc.dram_tensor("s_f", [D, S], F32, kind="ExternalInput").ap()
    x_ap = nc.dram_tensor("x", [S, D], F32, kind="ExternalInput").ap()
    wgs = nc.dram_tensor("wgs", [D, 2048], F32, kind="ExternalInput").ap()
    wus = nc.dram_tensor("wus", [D, 2048], F32, kind="ExternalInput").ap()
    wds = nc.dram_tensor("wds", [2048, D], F32, kind="ExternalInput").ap()
    wge = nc.dram_tensor("wge", [8, D, 2048], F32, kind="ExternalInput").ap()
    wue = nc.dram_tensor("wue", [8, D, 2048], F32, kind="ExternalInput").ap()
    wde = nc.dram_tensor("wde", [8, 2048, D], F32, kind="ExternalInput").ap()
    wr = nc.dram_tensor("wr", [D, 8], F32, kind="ExternalInput").ap()
    rb = nc.dram_tensor("rb", [1, 8], F32, kind="ExternalInput").ap()
    wkt = nc.dram_tensor("wkt", [D, D], F32, kind="ExternalInput").ap()
    wvt = nc.dram_tensor("wvt", [D, D], F32, kind="ExternalInput").ap()
    x1_ap = nc.dram_tensor("x1", [S, D], F32, kind="ExternalOutput").ap()
    xn1f_ap = nc.dram_tensor("xn1f", [D, S], BF16, kind="ExternalOutput").ap()
    k1_ap = nc.dram_tensor("k1", [S, D], BF16, kind="ExternalOutput").ap()
    v1_ap = nc.dram_tensor("v1", [S, D], BF16, kind="ExternalOutput").ap()
    emit_l0ffn(nc, xnf, s_f, x_ap, (wgs, wus, wds, wge, wue, wde, wr, rb),
               x1_ap, xn1f_ap, "f0")
    emit_kvproj(nc, xn1f_ap, wkt, wvt, k1_ap, v1_ap, "kv")
    return nc


def emit_l0ffn(nc, xnf, s_f, x_ap, ws, x1_ap, xn1f_ap, tag, xn1send_ap=None):
    wgs, wus, wds, wge, wue, wde, wr, rb = ws
    NTILE = 4
    TT = 1024
    NE = 9
    NIC = 16
    with ExitStack() as es:
        sb = lambda n, sh, dt: es.enter_context(nc.sbuf_tensor(tag + n, sh, dt)).ap()
        ps = lambda n, sh, dt: es.enter_context(nc.psum_tensor(tag + n, sh, dt)).ap()
        xnt = sb("xnt", [128, 8 * TT], BF16)
        wg_sb = [sb(f"wg{i}", [128, 1024], BF16) for i in range(2)]
        wu_sb = [sb(f"wu{i}", [128, 1024], BF16) for i in range(2)]
        wd_sb = [sb(f"wd{i}", [128, 16 * 1024], BF16) for i in range(2)]
        act_sb = sb("act", [128, 16 * TT], BF16)
        acc_sb = sb("acc", [128, 8 * 1024], F32)
        sg_sb = [sb(f"sg{i}", [128, 1024], F32) for i in range(2)]
        wr_sb = sb("wrs", [128, 8 * 8], BF16)
        rbb = sb("rbb", [128, 8], F32)
        rrow = sb("rrow", [1, 8], F32)
        ones1 = sb("ones1", [1, 128], F32)
        cw_sb = sb("cw", [128, 8 * 8], F32)
        rt = sb("rt", [128, 5 * 8], F32)
        rm = sb("rm", [128, 4], F32)
        sin_sb = [sb(f"sin{i}", [128, 1024], F32) for i in range(2)]
        stm_sb = sb("stm", [128, 1024], F32)
        xin_sb = [sb(f"xin{i}", [128, 1024], F32) for i in range(2)]
        x1t_sb = [sb(f"x1t{i}", [128, 1024], F32) for i in range(2)]
        xn1_sb = sb("xn1", [128, 1024], F32)
        nst_sb = [sb(f"nst{i}", [128, 1024], BF16) for i in range(2)]
        lstt = sb("lstt", [128, 8], F32)
        sqs = sb("sqs", [128, 1024], F32)
        epst = sb("epst", [128, 1], F32)
        ident = sb("idb", [128, 128], BF16)
        identf = sb("idf", [128, 128], F32)

        pg = [ps(f"pg{i}", [128, 1024], F32) for i in range(2)]
        pu = [ps(f"pu{i}", [128, 1024], F32) for i in range(2)]
        pd = [pg[0][:, 0:512], pg[1][:, 0:512]]
        prt = pu[0][:, 0:64]
        ptr_s = pu[1][:, 0:512]
        ptr_n = pg[1][:, 512:1024]

        ac = Ctr(nc, tag + "ac")
        dv = Ctr(nc, tag + "dv")
        pe = Ctr(nc, tag + "pe")
        gmk = Ctr(nc, tag + "gmk")
        bld = Ctr(nc, tag + "bld")
        rld = Ctr(nc, tag + "rld")
        xld = Ctr(nc, tag + "xld")
        wlds = [Ctr(nc, tag + f"wld{i}") for i in range(2)]
        wdld = [Ctr(nc, tag + f"wdld{i}") for i in range(2)]
        slds = [Ctr(nc, tag + f"sld{i}") for i in range(2)]
        xilds = [Ctr(nc, tag + f"xild{i}") for i in range(2)]
        sox = [Ctr(nc, tag + f"sox{i}") for i in range(2)]
        son = [Ctr(nc, tag + f"son{i}") for i in range(2)]

        wd_free = {}
        epi_done = {}

        with nc.Block(tag) as blk:
            @blk.gpsimd
            def _(g):
                g.memset(epst, LN_EPS)
                g.memset(ones1, 1.0)
                _mk_identity(g, ident, gmk).then_inc(gmk.h, 1)
                gmk.n += 1
                g.wait_ge(gmk.h, gmk.n)
                _mk_identity(g, identf, gmk).then_inc(gmk.h, 1)
                gmk.n += 1
                for hc in range(8):
                    g.dma_start(out=wr_sb[:, hc * 8:(hc + 1) * 8],
                                in_=wr[hc * 128:(hc + 1) * 128, :]).then_inc(rld.h, 16)
            gmk_all = gmk.n
            rld.n += 16 * 8

            @blk.sync
            def _(sy):
                sy.dma_start(out=rrow, in_=rb).then_inc(bld.h, 16)
            bld.n += 16
            rbld = bld.n

            @blk.tensor
            def _(t, rbld=rbld):
                t.wait_ge(bld.h, rbld)
                t.wait_ge(gmk.h, gmk_all)
                pe.inc(t.matmul(prt[:, 0:8], ones1, rrow, start=True, stop=True))
            rb_pe = pe.n

            @blk.vector
            def _(v, rb_pe=rb_pe):
                v.wait_ge(pe.h, rb_pe)
                dv.inc(v.tensor_copy(rbb, prt[:, 0:8]))
            rbb_dv = dv.n

            wgfen = [0, 0]
            pgfen = [0, 0]
            pufen = [0, 0]
            prev_dn_pe = 0
            prev_dn_dv = 0
            for ti in range(NTILE):
                t0 = ti * TT

                prev_tile_fence = pe.n

                @blk.sync
                def _(sy, ti=ti, t0=t0, fence=prev_tile_fence):
                    sy.wait_ge(pe.h, fence)
                    for hc in range(8):
                        sy.dma_start(out=xnt[:, hc * TT:(hc + 1) * TT],
                                     in_=xnf[hc * 128:(hc + 1) * 128, t0:t0 + TT])\
                          .then_inc(xld.h, 16)
                xld.n += 16 * 8
                xfence = xld.n

                @blk.tensor
                def _(t, xfence=xfence, rbld=rbld):
                    t.wait_ge(xld.h, xfence)
                    t.wait_ge(bld.h, rbld)
                    t.wait_ge(rld.h, rld.n)
                    t.wait_ge(dv.h, dv.n)
                    for ts in range(8):
                        for hc in range(8):
                            ins = t.matmul(prt[:, ts * 8:(ts + 1) * 8],
                                           xnt[:, hc * TT + ts * 128: hc * TT + (ts + 1) * 128],
                                           wr_sb[:, hc * 8:(hc + 1) * 8],
                                           start=(hc == 0), stop=(hc == 7))
                    pe.inc(ins)
                rt_pe = pe.n

                for ts in range(8):
                    sl = slice(ts * 8, (ts + 1) * 8)

                    @blk.vector
                    def _(v, ts=ts, sl=sl, rt_pe=rt_pe, rbb_dv=rbb_dv, dfen=None):
                        v.wait_ge(pe.h, rt_pe)
                        v.wait_ge(dv.h, dv.n)
                        dv.inc(v.tensor_add(rt[:, 0:8], prt[:, sl], rbb))
                    lg_dv = dv.n

                    @blk.scalar
                    def _(sc, lg_dv=lg_dv):
                        sc.wait_ge(dv.h, lg_dv)
                        ac.inc(sc.activation(out=rt[:, 8:16], in_=rt[:, 0:8],
                                             func=AF.Sigmoid))
                    pr_ac = ac.n

                    @blk.vector
                    def _(v, ts=ts, sl=sl, pr_ac=pr_ac):
                        v.wait_ge(ac.h, pr_ac)
                        dv.inc(v.reduce_max(rm[:, 0:1], rt[:, 8:16], axis=AX))
                        v.wait_ge(dv.h, dv.n)
                        dv.inc(v.tensor_scalar(out=rt[:, 16:24], in0=rt[:, 8:16],
                                               scalar1=rm[:, 0:1], scalar2=None,
                                               op0=OP.is_ge))
                        v.wait_ge(dv.h, dv.n)
                        dv.inc(v.tensor_sub(rt[:, 24:32], rt[:, 8:16], rt[:, 16:24]))
                        v.wait_ge(dv.h, dv.n)
                        dv.inc(v.reduce_max(rm[:, 1:2], rt[:, 24:32], axis=AX))
                        v.wait_ge(dv.h, dv.n)
                        dv.inc(v.tensor_scalar(out=rt[:, 32:40], in0=rt[:, 8:16],
                                               scalar1=rm[:, 1:2], scalar2=None,
                                               op0=OP.is_ge))
                        v.wait_ge(dv.h, dv.n)
                        dv.inc(v.tensor_mul(cw_sb[:, sl], rt[:, 8:16], rt[:, 32:40]))
                cw_fence = dv.n

                for e in range(NE):
                    wgt = wgs if e == 0 else wge[e - 1]
                    wut = wus if e == 0 else wue[e - 1]
                    wdt = wds if e == 0 else wde[e - 1]

                    key = ti * NE + e
                    kk = key % 2

                    @blk.gpsimd
                    def _(g, e=e, ti=ti, wdt=wdt, key=key, kk=kk):
                        if key >= 2:
                            g.wait_ge(dv.h, wd_free[key - 2])
                        for icc in range(16):
                            g.dma_start(out=wd_sb[kk][:, icc * 1024:(icc + 1) * 1024],
                                        in_=wdt[icc * 128:(icc + 1) * 128, :])\
                              .then_inc(wdld[kk].h, 16)
                    wdld[kk].n += 16 * 16
                    wd_fence = wdld[kk].n

                    for ic in range(NIC):
                        wguard = wgfen[ic % 2]

                        @blk.gpsimd
                        def _(g, e=e, ic=ic, wgt=wgt, wut=wut, wguard=wguard):
                            if wguard:
                                g.wait_ge(dv.h, wguard)
                            g.dma_start(out=wg_sb[ic % 2], in_=wgt[ic])\
                             .then_inc(wlds[ic % 2].h, 16)
                            g.dma_start(out=wu_sb[ic % 2], in_=wut[ic])\
                             .then_inc(wlds[ic % 2].h, 16)
                        wlds[ic % 2].n += 32
                        wfence = wlds[ic % 2].n
                        pguard, puguard = pgfen[ic % 2], pufen[ic % 2]

                        @blk.tensor
                        def _(t, e=e, ic=ic, wfence=wfence, xfence=xfence,
                              pguard=pguard, puguard=puguard, cwf=cw_fence):
                            t.wait_ge(xld.h, xfence)
                            t.wait_ge(wlds[ic % 2].h, wfence)
                            if e == 0 and ic < 2:
                                t.wait_ge(dv.h, cwf)
                            if prev_dn_dv and ic < 2:
                                t.wait_ge(dv.h, prev_dn_dv)
                            if pguard:
                                t.wait_ge(ac.h, pguard)
                            if puguard:
                                t.wait_ge(dv.h, puguard)
                            for hc in range(8):
                                for half in range(2):
                                    t.matmul(pg[ic % 2][:, half * 512:(half + 1) * 512],
                                             wg_sb[ic % 2][:, hc * 128:(hc + 1) * 128],
                                             xnt[:, hc * TT + half * 512: hc * TT + (half + 1) * 512],
                                             start=(hc == 0), stop=(hc == 7))
                            for hc in range(8):
                                for half in range(2):
                                    ins = t.matmul(pu[ic % 2][:, half * 512:(half + 1) * 512],
                                                   wu_sb[ic % 2][:, hc * 128:(hc + 1) * 128],
                                                   xnt[:, hc * TT + half * 512: hc * TT + (half + 1) * 512],
                                                   start=(hc == 0), stop=(hc == 7))
                            pe.inc(ins)
                        up_pe = pe.n

                        @blk.scalar
                        def _(sc, ic=ic, up_pe=up_pe):
                            sc.wait_ge(pe.h, up_pe)
                            sc.wait_ge(dv.h, dv.n)
                            ac.inc(sc.activation(out=sg_sb[ic % 2], in_=pg[ic % 2],
                                                 func=AF.Silu))
                        pgfen[ic % 2] = ac.n
                        sil_ac = ac.n
                        dnguard = prev_dn_pe

                        @blk.vector
                        def _(v, ic=ic, sil_ac=sil_ac, dnguard=dnguard):
                            v.wait_ge(ac.h, sil_ac)
                            if dnguard:
                                v.wait_ge(pe.h, dnguard)
                            dv.inc(v.tensor_mul(act_sb[:, ic * TT:(ic + 1) * TT],
                                                pu[ic % 2], sg_sb[ic % 2]))
                        pufen[ic % 2] = dv.n
                        wgfen[ic % 2] = dv.n
                    act_fence = dv.n

                    dn_dv = {}
                    for ts in range(8):
                        for half in range(2):
                            j = ts * 2 + half

                            acfence_dn = ac.n

                            @blk.tensor
                            def _(t, e=e, ts=ts, half=half, j=j, kk=kk,
                                  act_fence=act_fence, wd_fence=wd_fence,
                                  acfence_dn=acfence_dn):
                                t.wait_ge(dv.h, act_fence)
                                t.wait_ge(ac.h, acfence_dn)
                                t.wait_ge(wdld[kk].h, wd_fence)
                                if j >= 2:
                                    t.wait_ge(dv.h, dn_dv[j - 2])
                                for icc in range(16):
                                    ins = t.matmul(
                                        pd[j % 2],
                                        act_sb[:, icc * TT + ts * 128: icc * TT + (ts + 1) * 128],
                                        wd_sb[kk][:, icc * 1024 + half * 512: icc * 1024 + (half + 1) * 512],
                                        start=(icc == 0), stop=(icc == 15))
                                pe.inc(ins)
                            dn_pe = pe.n

                            acguard = ac.n if e == 0 else 0

                            @blk.vector
                            def _(v, e=e, ts=ts, half=half, j=j, dn_pe=dn_pe,
                                  cw_fence=cw_fence, acguard=acguard):
                                v.wait_ge(pe.h, dn_pe)
                                v.wait_ge(dv.h, cw_fence)
                                if acguard:
                                    v.wait_ge(ac.h, acguard)
                                asl = acc_sb[:, ts * 1024 + half * 512: ts * 1024 + (half + 1) * 512]
                                if e == 0:
                                    dv.inc(v.tensor_copy(asl, pd[j % 2]))
                                else:
                                    dv.inc(v.scalar_tensor_tensor(
                                        out=asl, in0=pd[j % 2],
                                        scalar=cw_sb[:, ts * 8 + (e - 1): ts * 8 + e],
                                        in1=asl, op0=OP.mult, op1=OP.add))
                            dn_dv[j] = dv.n
                    wd_free[key] = dv.n
                    prev_dn_pe = pe.n
                    prev_dn_dv = dv.n
                ffn_fence = dv.n

                for ts in range(8):
                    tglob = t0 + ts * 128
                    bsel = ts % 2

                    @blk.sync
                    def _(sy, ti=ti, ts=ts, tglob=tglob, bsel=bsel):
                        if ti * 8 + ts >= 2:
                            sy.wait_ge(dv.h, epi_done[ti * 8 + ts - 2])
                        for j in range(8):
                            sy.dma_start(out=sin_sb[bsel][:, j * 128:(j + 1) * 128],
                                         in_=s_f[j * 128:(j + 1) * 128, tglob:tglob + 128])\
                              .then_inc(slds[bsel].h, 16)
                        sy.dma_start(out=xin_sb[bsel], in_=x_ap[tglob:tglob + 128, :])\
                          .then_inc(xilds[bsel].h, 16)
                    slds[bsel].n += 16 * 8
                    xilds[bsel].n += 16
                    sfence = slds[bsel].n
                    xifence = xilds[bsel].n

                    stm_dv = None
                    for rnd in range(2):
                        @blk.tensor
                        def _(t, ts=ts, bsel=bsel, sfence=sfence, rnd=rnd):
                            t.wait_ge(slds[bsel].h, sfence)
                            t.wait_ge(gmk.h, gmk_all)
                            t.wait_ge(dv.h, dv.n)
                            for j in range(4):
                                jj = rnd * 4 + j
                                ins = t.transpose(ptr_s[:, j * 128:(j + 1) * 128],
                                                  sin_sb[bsel][:, jj * 128:(jj + 1) * 128],
                                                  identf)
                            pe.inc(ins)
                        str_pe = pe.n

                        @blk.vector
                        def _(v, str_pe=str_pe, rnd=rnd):
                            v.wait_ge(pe.h, str_pe)
                            v.wait_ge(dv.h, dv.n)
                            dv.inc(v.tensor_copy(stm_sb[:, rnd * 512:(rnd + 1) * 512],
                                                 ptr_s))
                    stm_dv = dv.n

                    @blk.scalar
                    def _(sc, ts=ts, ffn_fence=ffn_fence, acfen=None):
                        sc.wait_ge(dv.h, ffn_fence)
                        sc.wait_ge(dv.h, dv.n)
                        sc.wait_ge(ac.h, ac.n)
                        ac.inc(sc.activation(out=sqs, in_=acc_sb[:, ts * 1024:(ts + 1) * 1024],
                                             func=AF.Silu))
                    sil_ac2 = ac.n

                    @blk.vector
                    def _(v, ts=ts, bsel=bsel, stm_dv=stm_dv, sil_ac2=sil_ac2,
                          xifence=xifence):
                        v.wait_ge(dv.h, stm_dv)
                        v.wait_ge(ac.h, sil_ac2)
                        dv.inc(v.tensor_mul(stm_sb, stm_sb, sqs))
                        v.wait_ge(xilds[bsel].h, xifence)
                        v.wait_ge(sox[bsel].h, sox[bsel].n)
                        v.wait_ge(dv.h, dv.n)
                        dv.inc(v.tensor_add(x1t_sb[bsel], xin_sb[bsel], stm_sb))
                    x1_dv = dv.n

                    @blk.scalar
                    def _(sc, bsel=bsel, x1_dv=x1_dv):
                        sc.wait_ge(dv.h, x1_dv)
                        ac.inc(sc.activation(out=sqs, in_=x1t_sb[bsel], func=AF.Square,
                                             accum_out=lstt[:, 1:2]))
                    s2_ac = ac.n

                    @blk.vector
                    def _(v, bsel=bsel, x1_dv=x1_dv, s2_ac=s2_ac):
                        v.wait_ge(dv.h, x1_dv)
                        dv.inc(v.reduce_sum(lstt[:, 0:1], x1t_sb[bsel], axis=AX))
                        v.wait_ge(dv.h, dv.n)
                        dv.inc(v.tensor_scalar(out=lstt[:, 2:3], in0=lstt[:, 0:1],
                                               scalar1=1.0 / D, scalar2=None, op0=OP.mult))
                        v.wait_ge(dv.h, dv.n)
                        dv.inc(v.scalar_tensor_tensor(out=lstt[:, 5:6], in0=lstt[:, 2:3],
                                                      scalar=1.0, in1=lstt[:, 2:3],
                                                      op0=OP.mult, op1=OP.mult))
                        v.wait_ge(ac.h, s2_ac)
                        v.wait_ge(dv.h, dv.n)
                        dv.inc(v.scalar_tensor_tensor(out=lstt[:, 3:4], in0=lstt[:, 1:2],
                                                      scalar=1.0 / D, in1=lstt[:, 5:6],
                                                      op0=OP.mult, op1=OP.subtract))
                    var_dv = dv.n

                    @blk.scalar
                    def _(sc, var_dv=var_dv):
                        sc.wait_ge(dv.h, var_dv)
                        ac.inc(sc.activation(out=lstt[:, 4:5], in_=lstt[:, 3:4],
                                             func=AF.Sqrt, bias=epst[:, 0:1]))
                    std_ac = ac.n

                    @blk.vector
                    def _(v, bsel=bsel, std_ac=std_ac):
                        v.wait_ge(ac.h, std_ac)
                        dv.inc(v.reciprocal(lstt[:, 5:6], lstt[:, 4:5]))
                        v.wait_ge(dv.h, dv.n)
                        dv.inc(v.tensor_scalar(out=xn1_sb, in0=x1t_sb[bsel],
                                               scalar1=lstt[:, 2:3], scalar2=lstt[:, 5:6],
                                               op0=OP.subtract, op1=OP.mult))
                    xn1_dv = dv.n

                    for rnd in range(2):
                        @blk.tensor
                        def _(t, xn1_dv=xn1_dv, rnd=rnd):
                            t.wait_ge(dv.h, xn1_dv)
                            t.wait_ge(dv.h, dv.n)
                            for j in range(4):
                                jj = rnd * 4 + j
                                ins = t.transpose(ptr_n[:, j * 128:(j + 1) * 128],
                                                  xn1_sb[:, jj * 128:(jj + 1) * 128],
                                                  identf)
                            pe.inc(ins)
                        ntr_pe = pe.n

                        @blk.vector
                        def _(v, ti=ti, ts=ts, bsel=bsel, ntr_pe=ntr_pe, rnd=rnd):
                            v.wait_ge(pe.h, ntr_pe)
                            if ti * 8 + ts >= 2:
                                v.wait_ge(son[bsel].h, son[bsel].n)
                            dv.inc(v.tensor_copy(nst_sb[bsel][:, rnd * 512:(rnd + 1) * 512],
                                                 ptr_n))
                    nst_dv = dv.n

                    @blk.sync
                    def _(sy, ti=ti, ts=ts, tglob=tglob, bsel=bsel, x1_dv=x1_dv,
                          nst_dv=nst_dv):
                        sy.wait_ge(dv.h, x1_dv)
                        sy.dma_start(out=x1_ap[tglob:tglob + 128, :],
                                     in_=x1t_sb[bsel]).then_inc(sox[bsel].h, 16)
                        sy.wait_ge(dv.h, nst_dv)
                        for j in range(8):
                            sy.dma_start(out=xn1f_ap[j * 128:(j + 1) * 128,
                                                     tglob:tglob + 128],
                                         in_=nst_sb[bsel][:, j * 128:(j + 1) * 128])\
                              .then_inc(son[bsel].h, 16)
                        if xn1send_ap is not None:
                            w = tglob // 512
                            c0 = tglob - w * 512
                            for j in range(8):
                                sy.dma_start(
                                    out=xn1send_ap[w * 1024 + j * 128:
                                                   w * 1024 + (j + 1) * 128,
                                                   c0:c0 + 128],
                                    in_=nst_sb[bsel][:, j * 128:(j + 1) * 128])\
                                  .then_inc(son[bsel].h, 16)
                    sox[bsel].n += 16
                    son[bsel].n += 16 * 8 * (2 if xn1send_ap is not None else 1)
                    epi_done[ti * 8 + ts] = dv.n
                pgfen = [0, 0]
                pufen = [dv.n, dv.n]
                prev_dn_pe = pe.n

            @blk.sync
            def _(sy):
                for cs in sox + son:
                    sy.wait_ge(cs.h, cs.n)
        return [(cs.h, cs.n) for cs in sox + son]


def emit_kvproj(nc, xn1f, wkt, wvt, k1_ap, v1_ap, tag):
    with ExitStack() as es:
        sb = lambda n, sh, dt: es.enter_context(nc.sbuf_tensor(tag + n, sh, dt)).ap()
        ps = lambda n, sh, dt: es.enter_context(nc.psum_tensor(tag + n, sh, dt)).ap()
        xnt = sb("xnt", [128, 8 * 1024], BF16)
        wk_sb = sb("wk", [128, 8 * 1024], BF16)
        wv_sb = sb("wv", [128, 8 * 1024], BF16)
        kst = [sb(f"kst{i}", [128, 1024], BF16) for i in range(2)]
        vst = [sb(f"vst{i}", [128, 1024], BF16) for i in range(2)]
        pk = ps("pk", [128, 1024], F32)
        pv = ps("pv", [128, 1024], F32)

        wld = Ctr(nc, tag + "wld")
        xld = Ctr(nc, tag + "xld")
        pe = Ctr(nc, tag + "pe")
        dv = Ctr(nc, tag + "dv")
        sos = [Ctr(nc, tag + f"so{i}") for i in range(2)]

        with nc.Block(tag) as blk:
            @blk.gpsimd
            def _(g):
                for hc in range(8):
                    g.dma_start(out=wk_sb[:, hc * 1024:(hc + 1) * 1024],
                                in_=wkt[hc * 128:(hc + 1) * 128, :]).then_inc(wld.h, 16)
                    g.dma_start(out=wv_sb[:, hc * 1024:(hc + 1) * 1024],
                                in_=wvt[hc * 128:(hc + 1) * 128, :]).then_inc(wld.h, 16)
            wld.n += 16 * 16
            wfence = wld.n

            out_done = {}
            for ti in range(4):
                t0 = ti * 1024

                @blk.sync
                def _(sy, ti=ti, t0=t0, pfen=None):
                    sy.wait_ge(pe.h, pe.n)
                    for hc in range(8):
                        sy.dma_start(out=xnt[:, hc * 1024:(hc + 1) * 1024],
                                     in_=xn1f[hc * 128:(hc + 1) * 128, t0:t0 + 1024])\
                          .then_inc(xld.h, 16)
                xld.n += 16 * 8
                xfence = xld.n

                for ts in range(8):
                    bsel = ts % 2
                    idx = ti * 8 + ts

                    @blk.tensor
                    def _(t, ts=ts, xfence=xfence, wfence=wfence):
                        t.wait_ge(xld.h, xfence)
                        t.wait_ge(wld.h, wfence)
                        t.wait_ge(dv.h, dv.n)
                        for hc in range(8):
                            for half in range(2):
                                t.matmul(pk[:, half * 512:(half + 1) * 512],
                                         xnt[:, hc * 1024 + ts * 128: hc * 1024 + (ts + 1) * 128],
                                         wk_sb[:, hc * 1024 + half * 512: hc * 1024 + (half + 1) * 512],
                                         start=(hc == 0), stop=(hc == 7))
                        for hc in range(8):
                            for half in range(2):
                                ins = t.matmul(pv[:, half * 512:(half + 1) * 512],
                                               xnt[:, hc * 1024 + ts * 128: hc * 1024 + (ts + 1) * 128],
                                               wv_sb[:, hc * 1024 + half * 512: hc * 1024 + (half + 1) * 512],
                                               start=(hc == 0), stop=(hc == 7))
                        pe.inc(ins)
                    kv_pe = pe.n

                    @blk.vector
                    def _(v, bsel=bsel, idx=idx, kv_pe=kv_pe):
                        v.wait_ge(pe.h, kv_pe)
                        if idx >= 2:
                            v.wait_ge(sos[bsel].h, out_done[idx - 2])
                        dv.inc(v.tensor_copy(kst[bsel], pk))
                        dv.inc(v.tensor_copy(vst[bsel], pv))
                    kv_dv = dv.n

                    @blk.sync
                    def _(sy, ti=ti, ts=ts, t0=t0, bsel=bsel, kv_dv=kv_dv):
                        sy.wait_ge(dv.h, kv_dv)
                        tg = t0 + ts * 128
                        sy.dma_start(out=k1_ap[tg:tg + 128, :], in_=kst[bsel])\
                          .then_inc(sos[bsel].h, 16)
                        sy.dma_start(out=v1_ap[tg:tg + 128, :], in_=vst[bsel])\
                          .then_inc(sos[bsel].h, 16)
                    sos[bsel].n += 32
                    out_done[idx] = sos[bsel].n

            @blk.sync
            def _(sy):
                sy.wait_ge(sos[0].h, sos[0].n)
                sy.wait_ge(sos[1].h, sos[1].n)
        return [(sos[0].h, sos[0].n), (sos[1].h, sos[1].n)]


def _one_block(nc, name):
    def deco(f):
        with nc.Block(name) as blk:
            f(blk)
    return deco



def emit_l1scores(nc, xn1q_sh, wqt, k1all, mask, a_send, tag, a_recv=None, cc=None):
    NB = 8
    with ExitStack() as es:
        dv = Ctr(nc, tag + "dv")
        ac = Ctr(nc, tag + "ac")
        pe = Ctr(nc, tag + "pe")

        with ExitStack() as es2:
            sb2 = lambda n, sh, dt: es2.enter_context(nc.sbuf_tensor(tag + n, sh, dt)).ap()
            ps2 = lambda n, sh, dt: es2.enter_context(nc.psum_tensor(tag + n, sh, dt)).ap()
            q1_sb = sb2("q1", [128, 8 * S], BF16)
            wq_sb = sb2("wq", [128, 8 * 1024], BF16)
            xq_sb = sb2("xq", [128, 8 * 512], BF16)
            kt_sb = [sb2(f"kt{i}", [128, 8 * 512], BF16) for i in range(3)]
            e_sb = sb2("e", [128, 32 * 512], F32)
            mk_sb = sb2("mk", [128, 4 * 512], I8)
            tiny = sb2("tiny", [128, 512], F32)
            d_sb = sb2("d", [128, 512], F32)
            r_sb = sb2("r", [128, 512], F32)
            a_st = [sb2(f"ast{i}", [128, 512], BF16) for i in range(2)]
            pq = [ps2(f"pq{i}", [128, 512], F32) for i in range(2)]

            wld = Ctr(nc, tag + "wld")
            xqld = Ctr(nc, tag + "xqld")
            ktld = [Ctr(nc, tag + f"ktld{i}") for i in range(3)]
            mkld = Ctr(nc, tag + "mkld")
            aso = [Ctr(nc, tag + f"aso{i}") for i in range(2)]
            gmk = Ctr(nc, tag + "gmk")

            with nc.Block(tag + "qs") as blk:
                @blk.gpsimd
                def _(g):
                    g.memset(tiny, 1e-9).then_inc(gmk.h, 1)
                    for hc in range(8):
                        g.dma_start(out=wq_sb[:, hc * 1024:(hc + 1) * 1024],
                                    in_=wqt[hc * 128:(hc + 1) * 128, :])\
                          .then_inc(wld.h, 16)
                gmk.n += 1
                wld.n += 16 * 8
                wfen = wld.n

                for b in range(NB):
                    @blk.sync
                    def _(sy, b=b):
                        sy.wait_ge(pe.h, pe.n)
                        for hc in range(8):
                            sy.dma_start(out=xq_sb[:, hc * 512:(hc + 1) * 512],
                                         in_=xn1q_sh[b * 1024 + hc * 128:
                                                     b * 1024 + (hc + 1) * 128, 0:512])\
                              .then_inc(xqld.h, 16)
                    xqld.n += 16 * 8
                    xfen = xqld.n

                    for dc in range(8):
                        @blk.tensor
                        def _(t, b=b, dc=dc, xfen=xfen, wfen=wfen):
                            t.wait_ge(xqld.h, xfen)
                            t.wait_ge(wld.h, wfen)
                            t.wait_ge(dv.h, dv.n)
                            for hc in range(8):
                                ins = t.matmul(pq[dc % 2],
                                               wq_sb[:, hc * 1024 + dc * 128: hc * 1024 + (dc + 1) * 128],
                                               xq_sb[:, hc * 512:(hc + 1) * 512],
                                               start=(hc == 0), stop=(hc == 7))
                            pe.inc(ins)
                        qpe = pe.n

                        @blk.vector
                        def _(v, b=b, dc=dc, qpe=qpe):
                            v.wait_ge(pe.h, qpe)
                            dv.inc(v.tensor_copy(
                                q1_sb[:, dc * S + b * 512: dc * S + (b + 1) * 512],
                                pq[dc % 2]))
                q_fence = dv.n

                prev_stripe_dv = dv.n
                ktuse = {}
                for st in range(8):
                    j0 = st * 512

                    @blk.sync
                    def _(sy, st=st, j0=j0):
                        sy.wait_ge(dv.h, dv.n)
                        msrc = mask[j0:j0 + 512, :]\
                            .rearrange("(jc p) i -> p jc i", p=128)
                        mdst = mk_sb.rearrange("p (jc i) -> p jc i", jc=4)
                        sy.dma_start(out=mdst, in_=msrc).then_inc(mkld.h, 16)
                    mkld.n += 16
                    mfen = mkld.n

                    for b in range(NB):
                        kti = st * NB + b
                        ktc = ktld[kti % 3]

                        @blk.gpsimd
                        def _(g, st=st, b=b, j0=j0, kti=kti, ktc=ktc):
                            if kti >= 3:
                                g.wait_ge(pe.h, ktuse[kti - 3])
                            ktv = k1all[b * S:(b + 1) * S, :]\
                                .rearrange("s d -> (s d)")\
                                .rearrange("(dc p j) -> p dc j", dc=8, p=128)
                            dst = kt_sb[kti % 3].rearrange("p (dc j) -> p dc j", dc=8)
                            g.dma_start(out=dst, in_=ktv[:, :, j0:j0 + 512])\
                              .then_inc(ktc.h, 16)
                        ktc.n += 16
                        kfen = ktc.n

                        for jc in range(4):
                            @blk.tensor
                            def _(t, st=st, b=b, jc=jc, kti=kti, kfen=kfen, ktc=ktc,
                                  q_fence=q_fence):
                                t.wait_ge(ktc.h, kfen)
                                t.wait_ge(dv.h, q_fence)
                                t.wait_ge(ac.h, ac.n)
                                for dc in range(8):
                                    ins = t.matmul(
                                        pq[jc % 2],
                                        kt_sb[kti % 3][:, dc * 512 + jc * 128: dc * 512 + (jc + 1) * 128],
                                        q1_sb[:, dc * S + b * 512: dc * S + (b + 1) * 512],
                                        start=(dc == 0), stop=(dc == 7))
                                pe.inc(ins)
                            spe = pe.n

                            @blk.scalar
                            def _(sc, b=b, jc=jc, spe=spe, psd=prev_stripe_dv):
                                sc.wait_ge(pe.h, spe)
                                sc.wait_ge(dv.h, psd)
                                ac.inc(sc.activation(
                                    out=e_sb[:, (jc * 8 + b) * 512:(jc * 8 + b + 1) * 512],
                                    in_=pq[jc % 2], func=AF.Exp, scale=0.03125))
                        ktuse[kti] = pe.n
                    stripe_ac = ac.n

                    for jc in range(4):
                        @blk.vector
                        def _(v, jc=jc, stripe_ac=stripe_ac):
                            v.wait_ge(ac.h, stripe_ac)
                            esl = lambda b: e_sb[:, (jc * 8 + b) * 512:(jc * 8 + b + 1) * 512]
                            dv.inc(v.tensor_add(d_sb, esl(0), esl(1)))
                            for b in (2, 3, 4, 5, 6, 7):
                                v.wait_ge(dv.h, dv.n)
                                dv.inc(v.tensor_add(d_sb, d_sb, esl(b)))
                            v.wait_ge(dv.h, dv.n)
                            dv.inc(v.reciprocal(r_sb, d_sb))
                        r_fence = dv.n

                        for b in range(NB):
                            asb = a_st[b % 2]
                            asoc = aso[b % 2]

                            @blk.vector
                            def _(v, st=st, b=b, jc=jc, asb=asb, asoc=asoc,
                                  r_fence=r_fence, mfen=mfen):
                                v.wait_ge(dv.h, r_fence)
                                v.wait_ge(mkld.h, mfen)
                                v.wait_ge(gmk.h, 1)
                                v.wait_ge(asoc.h, asoc.n)
                                esl = e_sb[:, (jc * 8 + b) * 512:(jc * 8 + b + 1) * 512]
                                dv.inc(v.tensor_mul(esl, esl, r_sb))
                                v.wait_ge(dv.h, dv.n)
                                dv.inc(v.tensor_copy(asb, tiny))
                                v.wait_ge(dv.h, dv.n)
                                dv.inc(v.copy_predicated(asb,
                                                         mk_sb[:, jc * 512:(jc + 1) * 512],
                                                         esl))
                            a_dv = dv.n

                            @blk.sync
                            def _(sy, st=st, b=b, jc=jc, j0=j0, asb=asb,
                                  asoc=asoc, a_dv=a_dv):
                                sy.wait_ge(dv.h, a_dv)
                                sy.dma_start(
                                    out=a_send[st * S + b * 512 + jc * 128:
                                               st * S + b * 512 + (jc + 1) * 128, :],
                                    in_=asb).then_inc(asoc.h, 16)
                            asoc.n += 16
                    prev_stripe_dv = dv.n

                    if cc is not None:
                        @blk.gpsimd
                        def _(g, st=st, n0=aso[0].n, n1=aso[1].n):
                            g.wait_ge(aso[0].h, n0)
                            g.wait_ge(aso[1].h, n1)
                            with allow_ici_all_to_all():
                                g.collective_compute(
                                    "AllToAll", mybir.AluOpType.bypass,
                                    replica_groups=RG,
                                    ins=[a_send[st * S:(st + 1) * S, :].opt()],
                                    outs=[a_recv[st * S:(st + 1) * S, :].opt()])\
                                 .then_inc(cc.h, 1)
                        cc.n += 1

                @blk.sync
                def _(sy):
                    sy.wait_ge(aso[0].h, aso[0].n)
                    sy.wait_ge(aso[1].h, aso[1].n)
        return [(aso[0].h, aso[0].n), (aso[1].h, aso[1].n)]


def emit_l1ffn_g(nc, xn1f, ws, g1_ap, tag):
    sut, sdt, rut, rdt, rwt = ws
    NE = 18
    TT = 2048
    NTILE = 2
    NIC = 4
    with ExitStack() as es:
        sb = lambda n, sh, dt: es.enter_context(nc.sbuf_tensor(tag + n, sh, dt)).ap()
        ps = lambda n, sh, dt: es.enter_context(nc.psum_tensor(tag + n, sh, dt)).ap()
        xnt = sb("xnt", [128, 8 * TT], BF16)
        wu_sb = [sb(f"wu{i}", [128, 1024], BF16) for i in range(2)]
        wd_sb = [sb(f"wd{i}", [128, 4 * 1024], BF16) for i in range(2)]
        act_sb = sb("act", [128, 4 * TT], BF16)
        acc_sb = sb("acc", [128, 16 * 1024], F32)
        wr_sb = sb("wrs", [128, 8 * 16], BF16)
        gate_sb = sb("gate", [128, 16 * 16], F32)
        rt = sb("rt", [128, 6 * 16], F32)
        rm = sb("rm", [128, 4], F32)
        sqs = sb("sqs", [128, 1024], F32)
        gst = [sb(f"gst{i}", [128, 1024], BF16) for i in range(2)]

        pgx = [ps(f"pgx{i}", [128, 2048], F32) for i in range(2)]
        pd = [pgx[0][:, 0:512], pgx[1][:, 0:512]]
        prt = pgx[0][:, 512:1024]

        ac = Ctr(nc, tag + "ac")
        dv = Ctr(nc, tag + "dv")
        pe = Ctr(nc, tag + "pe")
        rld = Ctr(nc, tag + "rld")
        xld = Ctr(nc, tag + "xld")
        wlds = [Ctr(nc, tag + f"wld{i}") for i in range(2)]
        wdld = [Ctr(nc, tag + f"wdld{i}") for i in range(2)]
        dso = [Ctr(nc, tag + f"dso{i}") for i in range(2)]

        wd_free = {}
        epi_done = {}

        with nc.Block(tag) as blk:
            @blk.gpsimd
            def _(g):
                for hc in range(8):
                    g.dma_start(out=wr_sb[:, hc * 16:(hc + 1) * 16],
                                in_=rwt[hc * 128:(hc + 1) * 128, :]).then_inc(rld.h, 16)
            rld.n += 16 * 8

            prev_dn_pe = 0
            prev_dn_dv = 0
            pgfen = [0, 0]
            for ti in range(NTILE):
                t0 = ti * TT

                @blk.sync
                def _(sy, ti=ti, t0=t0):
                    sy.wait_ge(pe.h, pe.n)
                    for hc in range(8):
                        sy.dma_start(out=xnt[:, hc * TT:(hc + 1) * TT],
                                     in_=xn1f[hc * 128:(hc + 1) * 128, t0:t0 + TT])\
                          .then_inc(xld.h, 16)
                xld.n += 16 * 8
                xfence = xld.n

                for ts in range(16):
                    @blk.tensor
                    def _(t, ts=ts, xfence=xfence):
                        t.wait_ge(xld.h, xfence)
                        t.wait_ge(rld.h, rld.n)
                        t.wait_ge(dv.h, dv.n)
                        for hc in range(8):
                            ins = t.matmul(prt[:, 0:16],
                                           xnt[:, hc * TT + ts * 128: hc * TT + (ts + 1) * 128],
                                           wr_sb[:, hc * 16:(hc + 1) * 16],
                                           start=(hc == 0), stop=(hc == 7))
                        pe.inc(ins)
                    rpe = pe.n

                    @blk.vector
                    def _(v, ts=ts, rpe=rpe):
                        v.wait_ge(pe.h, rpe)
                        dv.inc(v.reduce_max(rm[:, 0:1], prt[:, 0:16], axis=AX))
                        v.wait_ge(dv.h, dv.n)
                        dv.inc(v.tensor_scalar(out=rt[:, 0:16], in0=prt[:, 0:16],
                                               scalar1=rm[:, 0:1], scalar2=None,
                                               op0=OP.subtract))
                    lg_dv = dv.n

                    @blk.scalar
                    def _(sc, lg_dv=lg_dv):
                        sc.wait_ge(dv.h, lg_dv)
                        sc.wait_ge(ac.h, ac.n)
                        ac.inc(sc.activation(out=rt[:, 16:32], in_=rt[:, 0:16],
                                             func=AF.Exp))
                    ex_ac = ac.n

                    @blk.vector
                    def _(v, ts=ts, ex_ac=ex_ac):
                        v.wait_ge(ac.h, ex_ac)
                        dv.inc(v.reduce_sum(rm[:, 1:2], rt[:, 16:32], axis=AX))
                        v.wait_ge(dv.h, dv.n)
                        dv.inc(v.reciprocal(rm[:, 2:3], rm[:, 1:2]))
                        v.wait_ge(dv.h, dv.n)
                        dv.inc(v.tensor_scalar(out=rt[:, 16:32], in0=rt[:, 16:32],
                                               scalar1=rm[:, 2:3], scalar2=None,
                                               op0=OP.mult))
                        v.wait_ge(dv.h, dv.n)
                        dv.inc(v.tensor_copy(rt[:, 32:48], rt[:, 16:32]))
                        for r in range(4):
                            v.wait_ge(dv.h, dv.n)
                            dv.inc(v.reduce_max(rm[:, 3:4], rt[:, 32:48], axis=AX))
                            if r < 3:
                                v.wait_ge(dv.h, dv.n)
                                dv.inc(v.tensor_scalar(out=rt[:, 48:64], in0=rt[:, 32:48],
                                                       scalar1=rm[:, 3:4], scalar2=None,
                                                       op0=OP.is_ge))
                                v.wait_ge(dv.h, dv.n)
                                dv.inc(v.tensor_sub(rt[:, 32:48], rt[:, 32:48],
                                                    rt[:, 48:64]))
                        v.wait_ge(dv.h, dv.n)
                        dv.inc(v.tensor_scalar(out=rt[:, 48:64], in0=rt[:, 16:32],
                                               scalar1=rm[:, 3:4], scalar2=None,
                                               op0=OP.is_ge))
                        v.wait_ge(dv.h, dv.n)
                        dv.inc(v.tensor_mul(gate_sb[:, ts * 16:(ts + 1) * 16],
                                            rt[:, 16:32], rt[:, 48:64]))
                gate_fence = dv.n

                for e in range(NE):
                    wut = sut[e] if e < 2 else rut[e - 2]
                    wdt = sdt[e] if e < 2 else rdt[e - 2]
                    key = ti * NE + e
                    kk = key % 2

                    @blk.gpsimd
                    def _(g, wdt=wdt, key=key, kk=kk):
                        if key >= 2:
                            g.wait_ge(dv.h, wd_free[key - 2])
                        for icc in range(4):
                            g.dma_start(out=wd_sb[kk][:, icc * 1024:(icc + 1) * 1024],
                                        in_=wdt[icc * 128:(icc + 1) * 128, :])\
                              .then_inc(wdld[kk].h, 16)
                    wdld[kk].n += 16 * 4
                    wd_fence = wdld[kk].n

                    for ic in range(NIC):
                        wguard = pgfen[ic % 2]

                        @blk.gpsimd
                        def _(g, ic=ic, wut=wut, wguard=wguard):
                            if wguard:
                                g.wait_ge(ac.h, wguard)
                            g.dma_start(out=wu_sb[ic % 2], in_=wut[ic])\
                             .then_inc(wlds[ic % 2].h, 16)
                        wlds[ic % 2].n += 16
                        wfen = wlds[ic % 2].n

                        @blk.tensor
                        def _(t, e=e, ic=ic, wfen=wfen, xfence=xfence,
                              pguard=pgfen[ic % 2], pdd=prev_dn_dv, gf=gate_fence):
                            t.wait_ge(xld.h, xfence)
                            t.wait_ge(wlds[ic % 2].h, wfen)
                            if e == 0 and ic == 0:
                                t.wait_ge(dv.h, gf)
                            if pdd and ic < 2:
                                t.wait_ge(dv.h, pdd)
                            if pguard:
                                t.wait_ge(ac.h, pguard)
                            for hc in range(8):
                                for q in range(4):
                                    ins = t.matmul(
                                        pgx[ic % 2][:, q * 512:(q + 1) * 512],
                                        wu_sb[ic % 2][:, hc * 128:(hc + 1) * 128],
                                        xnt[:, hc * TT + q * 512: hc * TT + (q + 1) * 512],
                                        start=(hc == 0), stop=(hc == 7))
                            pe.inc(ins)
                        up_pe = pe.n

                        @blk.scalar
                        def _(sc, e=e, ic=ic, up_pe=up_pe, dnp=prev_dn_pe):
                            sc.wait_ge(pe.h, up_pe)
                            if dnp:
                                sc.wait_ge(pe.h, dnp)
                            ac.inc(sc.activation(out=act_sb[:, ic * TT:(ic + 1) * TT],
                                                 in_=pgx[ic % 2], func=AF.Gelu))
                        pgfen[ic % 2] = ac.n
                    act_fence = ac.n

                    dn_dv = {}
                    for ts in range(16):
                        for half in range(2):
                            j = ts * 2 + half

                            @blk.tensor
                            def _(t, ts=ts, half=half, j=j, kk=kk,
                                  act_fence=act_fence, wd_fence=wd_fence):
                                t.wait_ge(ac.h, act_fence)
                                t.wait_ge(wdld[kk].h, wd_fence)
                                if j >= 2:
                                    t.wait_ge(dv.h, dn_dv[j - 2])
                                for icc in range(4):
                                    ins = t.matmul(
                                        pd[j % 2],
                                        act_sb[:, icc * TT + ts * 128: icc * TT + (ts + 1) * 128],
                                        wd_sb[kk][:, icc * 1024 + half * 512: icc * 1024 + (half + 1) * 512],
                                        start=(icc == 0), stop=(icc == 3))
                                pe.inc(ins)
                            dn_pe = pe.n
                            acguard = ac.n if e == 0 else 0

                            @blk.vector
                            def _(v, e=e, ts=ts, half=half, j=j, dn_pe=dn_pe,
                                  gate_fence=gate_fence, acguard=acguard):
                                v.wait_ge(pe.h, dn_pe)
                                v.wait_ge(dv.h, gate_fence)
                                if acguard:
                                    v.wait_ge(ac.h, acguard)
                                asl = acc_sb[:, ts * 1024 + half * 512: ts * 1024 + (half + 1) * 512]
                                if e == 0:
                                    dv.inc(v.tensor_copy(asl, pd[j % 2]))
                                elif e == 1:
                                    dv.inc(v.scalar_tensor_tensor(
                                        out=asl, in0=pd[j % 2], scalar=1.0,
                                        in1=asl, op0=OP.mult, op1=OP.add))
                                else:
                                    dv.inc(v.scalar_tensor_tensor(
                                        out=asl, in0=pd[j % 2],
                                        scalar=gate_sb[:, ts * 16 + (e - 2): ts * 16 + e - 1],
                                        in1=asl, op0=OP.mult, op1=OP.add))
                            dn_dv[j] = dv.n
                    wd_free[key] = dv.n
                    prev_dn_pe = pe.n
                    prev_dn_dv = dv.n
                ffn_fence = dv.n

                for ts in range(16):
                    tglob = t0 + ts * 128
                    bsel = ts % 2

                    @blk.scalar
                    def _(sc, ts=ts, ffn_fence=ffn_fence):
                        sc.wait_ge(dv.h, ffn_fence)
                        sc.wait_ge(dv.h, dv.n)
                        sc.wait_ge(ac.h, ac.n)
                        ac.inc(sc.activation(out=sqs,
                                             in_=acc_sb[:, ts * 1024:(ts + 1) * 1024],
                                             func=AF.Silu))
                    sil_ac = ac.n

                    @blk.vector
                    def _(v, ts=ts, bsel=bsel, sil_ac=sil_ac):
                        v.wait_ge(ac.h, sil_ac)
                        v.wait_ge(dso[bsel].h, dso[bsel].n)
                        dv.inc(v.tensor_copy(gst[bsel], sqs))
                    g_dv = dv.n

                    @blk.sync
                    def _(sy, tglob=tglob, bsel=bsel, g_dv=g_dv):
                        sy.wait_ge(dv.h, g_dv)
                        sy.dma_start(out=g1_ap[tglob:tglob + 128, :],
                                     in_=gst[bsel]).then_inc(dso[bsel].h, 16)
                    dso[bsel].n += 16
                    epi_done[ti * 16 + ts] = dv.n

            @blk.sync
            def _(sy):
                sy.wait_ge(dso[0].h, dso[0].n)
                sy.wait_ge(dso[1].h, dso[1].n)
        return [(dso[0].h, dso[0].n), (dso[1].h, dso[1].n)]


def emit_pv_final(nc, a_recv, v1loc, g1, x1_ap, y_ap, tag, cc=None, ccbase=2):
    NJ = 32
    with ExitStack() as es:
        sb = lambda n, sh, dt: es.enter_context(nc.sbuf_tensor(tag + n, sh, dt)).ap()
        ps = lambda n, sh, dt: es.enter_context(nc.psum_tensor(tag + n, sh, dt)).ap()
        v_all = sb("vall", [128, NJ * 1024], BF16)
        a_sb = [sb(f"a{i}", [128, 512], BF16) for i in range(4)]
        gt = [sb(f"g{i}", [128, 1024], BF16) for i in range(2)]
        xt = [sb(f"x{i}", [128, 1024], F32) for i in range(2)]
        yt = [sb(f"yy{i}", [128, 1024], F32) for i in range(2)]
        ppv = [ps(f"p{i}", [128, 512], F32) for i in range(8)]

        pe = Ctr(nc, tag + "pe")
        dv = Ctr(nc, tag + "dv")
        vld = Ctr(nc, tag + "vld")
        ald = [Ctr(nc, tag + f"ald{i}") for i in range(4)]
        gld = [Ctr(nc, tag + f"gld{i}") for i in range(2)]
        yso = [Ctr(nc, tag + f"yso{i}") for i in range(2)]

        with nc.Block(tag) as blk:
            @blk.gpsimd
            def _(g):
                for j in range(NJ):
                    g.dma_start(out=v_all[:, j * 1024:(j + 1) * 1024],
                                in_=v1loc[j * 128:(j + 1) * 128, :]).then_inc(vld.h, 16)
            vld.n += 16 * NJ

            g_free = {}
            use_pe = {}
            for cw in range(8):
                ev_prev = dv.n
                for j in range(NJ):
                    idx = cw * NJ + j
                    alc = ald[idx % 4]

                    @blk.sync
                    def _(sy, cw=cw, j=j, idx=idx, alc=alc):
                        if idx >= 4:
                            sy.wait_ge(pe.h, use_pe[idx - 4])
                        st, jl = j // 4, j % 4
                        if cc is not None:
                            sy.wait_ge(cc.h, ccbase + st + 1)
                        sy.dma_start(out=a_sb[idx % 4],
                                     in_=a_recv[st * S + cw * 512 + jl * 128:
                                                st * S + cw * 512 + (jl + 1) * 128, :])\
                          .then_inc(alc.h, 16)
                    alc.n += 16
                    afen = alc.n

                    @blk.tensor
                    def _(t, cw=cw, j=j, idx=idx, afen=afen, alc=alc, ev_prev=ev_prev):
                        t.wait_ge(alc.h, afen)
                        if cw == 0 and j == 0:
                            t.wait_ge(vld.h, vld.n)
                        if j == 0:
                            t.wait_ge(dv.h, ev_prev)
                        for isub in range(4):
                            for half in range(2):
                                ins = t.matmul(
                                    ppv[isub * 2 + half],
                                    a_sb[idx % 4][:, isub * 128:(isub + 1) * 128],
                                    v_all[:, j * 1024 + half * 512:
                                          j * 1024 + (half + 1) * 512],
                                    start=(j == 0), stop=(j == NJ - 1))
                        pe.inc(ins)
                    use_pe[idx] = pe.n
                pv_pe = pe.n

                for isub in range(4):
                    bsel = isub % 2
                    idx = cw * 4 + isub
                    row0 = cw * 512 + isub * 128

                    @blk.sync
                    def _(sy, idx=idx, bsel=bsel, row0=row0):
                        if idx >= 2:
                            sy.wait_ge(dv.h, g_free[idx - 2])
                        sy.dma_start(out=gt[bsel], in_=g1[row0:row0 + 128, :])\
                          .then_inc(gld[bsel].h, 16)
                        sy.dma_start(out=xt[bsel], in_=x1_ap[row0:row0 + 128, :])\
                          .then_inc(gld[bsel].h, 16)
                    gld[bsel].n += 32
                    gfen = gld[bsel].n

                    @blk.vector
                    def _(v, isub=isub, bsel=bsel, pv_pe=pv_pe, gfen=gfen):
                        v.wait_ge(pe.h, pv_pe)
                        v.wait_ge(gld[bsel].h, gfen)
                        v.wait_ge(yso[bsel].h, yso[bsel].n)
                        for half in range(2):
                            v.wait_ge(dv.h, dv.n)
                            dv.inc(v.tensor_mul(
                                yt[bsel][:, half * 512:(half + 1) * 512],
                                ppv[isub * 2 + half],
                                gt[bsel][:, half * 512:(half + 1) * 512]))
                        v.wait_ge(dv.h, dv.n)
                        dv.inc(v.tensor_add(yt[bsel], yt[bsel], xt[bsel]))
                    g_free[idx] = dv.n
                    ydv = dv.n

                    @blk.sync
                    def _(sy, bsel=bsel, row0=row0, ydv=ydv):
                        sy.wait_ge(dv.h, ydv)
                        sy.dma_start(out=y_ap[row0:row0 + 128, :], in_=yt[bsel])\
                          .then_inc(yso[bsel].h, 16)
                    yso[bsel].n += 16

            @blk.sync
            def _(sy):
                sy.wait_ge(yso[0].h, yso[0].n)
                sy.wait_ge(yso[1].h, yso[1].n)


RG = [list(range(NCORES))]


def fence_all(nc, deps, name):
    with nc.Block(name) as blk:
        for eng in ("sync", "gpsimd", "scalar", "vector", "tensor"):
            @getattr(blk, eng)
            def _(e, deps=deps):
                for (h, n) in deps:
                    e.wait_ge(h, n)


def emit_collectives(nc, ops, deps, cc, tag):
    with nc.Block(tag) as blk:
        @blk.gpsimd
        def _(g):
            for (h, n) in deps:
                g.wait_ge(h, n)
            with allow_ici_all_to_all():
                for kind, src, dst in ops:
                    g.collective_compute(kind, mybir.AluOpType.bypass,
                                         replica_groups=RG, ins=[src.opt()],
                                         outs=[dst.opt()]).then_inc(cc.h, 1)
    cc.n += len(ops)


def build_fused():
    nc = bass.Bass("TRN2", target_bir_lowering=False)
    DT = lambda n, sh, dt, kind: nc.dram_tensor(n, sh, dt, kind=kind).ap()
    x_ap = DT("x", [S, D], F32, "ExternalInput")
    wq = DT("wq", [D, 256], F32, "ExternalInput")
    wqc = DT("wqc", [D, 256], F32, "ExternalInput")
    wk = DT("wk", [D, 256], F32, "ExternalInput")
    wkc = DT("wkc", [D, 256], F32, "ExternalInput")
    wv = DT("wv", [D, D], F32, "ExternalInput")
    wvc = DT("wvc", [D, D], F32, "ExternalInput")
    bq = DT("bq", [256, 1], F32, "ExternalInput")
    bqc = DT("bqc", [256, 1], F32, "ExternalInput")
    bk = DT("bk", [256, 1], F32, "ExternalInput")
    bkc = DT("bkc", [256, 1], F32, "ExternalInput")
    bv = DT("bv", [1, D], F32, "ExternalInput")
    bvc = DT("bvc", [1, D], F32, "ExternalInput")
    wgs = DT("wgs", [16, 128, 1024], F32, "ExternalInput")
    wus = DT("wus", [16, 128, 1024], F32, "ExternalInput")
    wds = DT("wds", [2048, D], F32, "ExternalInput")
    wge = DT("wge", [8, 16, 128, 1024], F32, "ExternalInput")
    wue = DT("wue", [8, 16, 128, 1024], F32, "ExternalInput")
    wde = DT("wde", [8, 2048, D], F32, "ExternalInput")
    wr = DT("wr", [D, 8], F32, "ExternalInput")
    rb = DT("rb", [1, 8], F32, "ExternalInput")
    wkt = DT("wkt", [D, D], F32, "ExternalInput")
    wvt = DT("wvt", [D, D], F32, "ExternalInput")
    wqt = DT("wqt", [D, D], F32, "ExternalInput")
    mask = DT("mask", [S, QW], I8, "ExternalInput")
    sut = DT("sut", [2, 4, 128, 1024], F32, "ExternalInput")
    sdt = DT("sdt", [2, 512, D], F32, "ExternalInput")
    rut = DT("rut", [16, 4, 128, 1024], F32, "ExternalInput")
    rdt = DT("rdt", [16, 512, D], F32, "ExternalInput")
    rwt = DT("rwt", [D, 16], F32, "ExternalInput")
    y_ap = DT("y", [S, D], F32, "ExternalOutput")

    xnf = DT("xnf", [D, SP], BF16, "Internal")
    s_f = DT("s_f", [D, S], F32, "Internal")
    x1 = DT("x1", [S, D], F32, "Internal")
    xn1f = DT("xn1f", [D, S], BF16, "Internal")
    xn1send = DT("xn1send", [8 * D, QW], BF16, "Internal")
    xn1q = DT("xn1q", [8 * D, QW], BF16, "Internal")
    k1loc = DT("k1loc", [S, D], BF16, "Internal")
    k1all = DT("k1all", [B * S, D], BF16, "Internal")
    v1loc = DT("v1loc", [S, D], BF16, "Internal")
    a_send = DT("a_send", [B * S, QW], BF16, "Internal")
    a_recv = DT("a_recv", [B * S, QW], BF16, "Internal")
    g1 = DT("g1", [S, D], BF16, "Internal")

    cc = Ctr(nc, "cc")

    f1 = emit_ln(nc, x_ap, xnf, "ln0", SP)
    fence_all(nc, f1, "fn1")
    f2 = emit_l0attn(nc, xnf, (wq, wqc, wk, wkc, wv, wvc),
                     (bq, bqc, bk, bkc, bv, bvc), s_f, "a0")
    fence_all(nc, f2, "fn2")
    f3 = emit_l0ffn(nc, xnf, s_f, x_ap, (wgs, wus, wds, wge, wue, wde, wr, rb),
                    x1, xn1f, "f0", xn1send_ap=xn1send)
    f4 = emit_kvproj(nc, xn1f, wkt, wvt, k1loc, v1loc, "kv")
    fence_all(nc, f3 + f4, "fn3")
    emit_collectives(nc, [("AllToAll", xn1send, xn1q),
                          ("AllGather", k1loc, k1all)], [], cc, "cc1")
    f5 = emit_l1ffn_g(nc, xn1f, (sut, sdt, rut, rdt, rwt), g1, "e2")
    fence_all(nc, f5 + [(cc.h, cc.n)], "fn4")
    f6 = emit_l1scores(nc, xn1q, wqt, k1all, mask, a_send, "e1",
                       a_recv=a_recv, cc=cc)
    fence_all(nc, f6, "fn5")
    emit_pv_final(nc, a_recv, v1loc, g1, x1, y_ap, "pf", cc=cc, ccbase=2)
    return nc


_FUSED = None


def _fused_prog():
    global _FUSED
    if _FUSED is None:
        _FUSED = build_fused()
    return _FUSED


def kernel(**inputs):
    T = lambda a: np.ascontiguousarray(np.asarray(a).swapaxes(-1, -2))
    C = lambda a: np.ascontiguousarray(np.asarray(a))

    def WT(a):
        a = np.asarray(a)
        Dd, Ii = a.shape[-2], a.shape[-1]
        lead = a.shape[:-2]
        a = a.reshape(lead + (Dd // 128, 128, Ii // 128, 128))
        nl = len(lead)
        perm = tuple(range(nl)) + (nl + 2, nl + 1, nl, nl + 3)
        return np.ascontiguousarray(
            a.transpose(perm).reshape(lead + (Ii // 128, 128, (Dd // 128) * 128)))
    x = np.asarray(inputs["L0_attn_x"])

    base = {
        "wq": C(inputs["L0_attn_Wq"]), "wqc": C(inputs["L0_attn_Wqc"]),
        "wk": C(inputs["L0_attn_Wk"]), "wkc": C(inputs["L0_attn_Wkc"]),
        "wv": C(inputs["L0_attn_Wv"]), "wvc": C(inputs["L0_attn_Wvc"]),
        "bq": np.asarray(inputs["L0_attn_bq"]).reshape(256, 1).copy(),
        "bqc": np.asarray(inputs["L0_attn_bqc"]).reshape(256, 1).copy(),
        "bk": np.asarray(inputs["L0_attn_bk"]).reshape(256, 1).copy(),
        "bkc": np.asarray(inputs["L0_attn_bkc"]).reshape(256, 1).copy(),
        "bv": np.asarray(inputs["L0_attn_bv"]).reshape(1, D).copy(),
        "bvc": np.asarray(inputs["L0_attn_bvc"]).reshape(1, D).copy(),
        "wgs": WT(T(inputs["L0_ffn_Wg_s"])), "wus": WT(T(inputs["L0_ffn_Wu_s"])),
        "wds": T(inputs["L0_ffn_Wd_s"]),
        "wge": WT(T(inputs["L0_ffn_Wg"])), "wue": WT(T(inputs["L0_ffn_Wu"])),
        "wde": T(inputs["L0_ffn_Wd"]),
        "wr": T(inputs["L0_ffn_Wr"]),
        "rb": np.asarray(inputs["L0_ffn_rb"]).reshape(1, 8).copy(),
        "wkt": T(inputs["L1_attn_Wk"]), "wvt": T(inputs["L1_attn_Wv"]),
        "wqt": T(inputs["L1_attn_Wq"]),
        "sut": WT(T(inputs["L1_ffn_shared_up"])), "sdt": T(inputs["L1_ffn_shared_down"]),
        "rut": WT(T(inputs["L1_ffn_routed_up"])), "rdt": T(inputs["L1_ffn_routed_down"]),
        "rwt": T(inputs["L1_ffn_router_w"]),
    }
    jj = np.arange(S)[:, None]
    ii = np.arange(QW)[None, :]
    in_maps = []
    for c in range(NCORES):
        maskc = (jj <= c * QW + ii).astype(np.int8)
        in_maps.append(dict(base, x=C(x[c]), mask=maskc))
    res = run_bass_kernel_spmd(_fused_prog(), in_maps, core_ids=list(range(NCORES)))
    return np.stack([res.results[c]["y"] for c in range(NCORES)])
